# version 08
# baseline (speedup 1.0000x reference)
import numpy as np
import ml_dtypes
import contextlib
import concourse.bass as bass
import concourse.mybir as mybir
from concourse.bass_core_rust import IndirectOffsetOnAxis
from concourse.bass_utils import run_bass_kernel_spmd

F32 = mybir.dt.float32
BF16 = mybir.dt.bfloat16
I32 = mybir.dt.int32
AX = mybir.AxisListType.X
ALU = mybir.AluOpType
AF = mybir.ActivationFunctionType

NCORE = 8
SEQ, D, NH, DH, E, CAP, HID = 4096, 1024, 16, 64, 16, 1280, 4096
TLOC = 2048
NPAIR = 8
QB = 512
NQB = TLOC // QB
KCH = SEQ // 128
TT16 = TLOC // 128
DT8 = D // 128
SLOTS = 2 * CAP
TBLN = 20608
TRASH = 20480
SOFT_B = 16.0
NCHUNK = NQB * KCH


class Sem:
    def __init__(self, handle, name):
        self.h = handle
        self.name = name
        self.n = 0


def build_program(debug=False, stop_after=None, c_sub=None):
    nc = bass.Bass("TRN2", target_bir_lowering=False, debug=False)

    zT = nc.dram_tensor("zT", [D, SEQ], F32, kind="ExternalInput")
    wqkvT = nc.dram_tensor("wqkvT", [D, 3 * D], F32, kind="ExternalInput")
    gwT = nc.dram_tensor("gwT", [D, E], F32, kind="ExternalInput")
    w1o = nc.dram_tensor("w1o", [2 * D, HID], F32, kind="ExternalInput")
    w2o = nc.dram_tensor("w2o", [2 * HID, D], F32, kind="ExternalInput")
    identb_d = nc.dram_tensor("identbd", [128, 128], BF16, kind="ExternalInput")
    identf_d = nc.dram_tensor("identfd", [128, 128], F32, kind="ExternalInput")
    triu_d = nc.dram_tensor("triud", [128, 128], F32, kind="ExternalInput")
    trib_d = nc.dram_tensor("tribd", [16, 16 * 128], F32, kind="ExternalInput")
    rmb_d = nc.dram_tensor("rmbd", [8, 128], F32, kind="ExternalInput")
    tokidx_d = nc.dram_tensor("tokidxd", [128, TT16], I32, kind="ExternalInput")
    slotsel_d = nc.dram_tensor("slotseld", [128, 20], I32, kind="ExternalInput")
    part_d = nc.dram_tensor("partd", [128, TT16 * E], F32, kind="ExternalInput")
    ehalf_d = nc.dram_tensor("ehalfd", [128, TT16 * E], F32, kind="ExternalInput")

    y_out = nc.dram_tensor("y", [TLOC, D], F32, kind="ExternalOutput")

    sums_dram = nc.dram_tensor("sums_dram", [4 * NPAIR, TLOC], F32, kind="Internal")
    sums_r = nc.dram_tensor("sums_r", [2 * NPAIR, TLOC], F32, kind="Internal")
    x1fm_dram = nc.dram_tensor("x1fm_dram", [D, TLOC], BF16, kind="Internal")
    x1pub = nc.dram_tensor("x1pub", [TLOC, D], BF16, kind="Internal")
    xall = nc.dram_tensor("xall", [NCORE * TLOC, D], BF16, kind="Internal", addr_space="Shared")
    cnt_in = nc.dram_tensor("cnt_in", [1, E], F32, kind="Internal")
    cnt_out = nc.dram_tensor("cnt_out", [NCORE, E], F32, kind="Internal")
    tbl_in = nc.dram_tensor("tbl_in", [TBLN, 1], I32, kind="Internal")
    tbl_out = nc.dram_tensor("tbl_out", [TBLN, 1], I32, kind="Internal")
    eo_in = [nc.dram_tensor("eo_inE", [CAP, D], BF16, kind="Internal"),
             nc.dram_tensor("eo_inO", [CAP, D], BF16, kind="Internal")]
    eo_allE = nc.dram_tensor("eo_allE", [NCORE * CAP, D], BF16, kind="Internal", addr_space="Shared")
    eo_allO = nc.dram_tensor("eo_allO", [NCORE * CAP, D], BF16, kind="Internal", addr_space="Shared")
    gdram = nc.dram_tensor("gdram", [CAP, D], BF16, kind="Internal")
    odram = nc.dram_tensor("odram", [D, CAP], BF16, kind="Internal")

    dbg = {}
    if debug:
        dbg["attn"] = nc.dram_tensor("dbg_attn", [128, DT8 * TLOC], BF16, kind="ExternalOutput")
        dbg["sums"] = nc.dram_tensor("dbg_sums", [4 * NPAIR, TLOC], F32, kind="ExternalOutput")
        dbg["x1"] = nc.dram_tensor("dbg_x1", [TLOC, D], BF16, kind="ExternalOutput")
        dbg["logits"] = nc.dram_tensor("dbg_logits", [128, TT16 * E], F32, kind="ExternalOutput")
        dbg["route"] = nc.dram_tensor("dbg_route", [128, 5 * TT16], F32, kind="ExternalOutput")
        dbg["sg"] = nc.dram_tensor("dbg_sg", [128, TT16], mybir.dt.int32, kind="ExternalOutput")
        dbg["epar"] = nc.dram_tensor("dbg_epar", [128, TT16], F32, kind="ExternalOutput")
        dbg["moeE"] = nc.dram_tensor("dbg_moeE", [128, TT16 * 1024], BF16, kind="ExternalOutput")
        dbg["moeO"] = nc.dram_tensor("dbg_moeO", [128, TT16 * 1024], BF16, kind="ExternalOutput")
        dbg["eo"] = nc.dram_tensor("dbg_eo", [SLOTS, D], BF16, kind="ExternalOutput")

    ops = {k: [] for k in ("tensor", "vector", "scalar", "gpsimd", "sync")}
    waits_record = []

    P = contextlib.ExitStack()
    stk_b1 = contextlib.ExitStack()
    stk_a = contextlib.ExitStack()
    stk_b2 = contextlib.ExitStack()
    stk_c = contextlib.ExitStack()
    stk_d = contextlib.ExitStack()

    def sbuf(stk, name, shape, dt):
        return stk.enter_context(nc.sbuf_tensor(name, shape, dt))

    with P:
        def mksem(name):
            return Sem(P.enter_context(nc.semaphore(name)), name)

        sPE = mksem("sPE")
        sDV = mksem("sDV")
        sAC = mksem("sAC")
        sSc = mksem("sSc")
        sEx = mksem("sEx")
        sAV = mksem("sAV")
        sCC = mksem("sCC")
        gA0 = mksem("gA0")
        gWq = [mksem("gWq0"), mksem("gWq1")]
        gWv2 = mksem("gWv2")
        gST1 = mksem("gST1")
        gLdB = mksem("gLdB")
        gRow = mksem("gRow")
        gFM = mksem("gFM")
        gXb = mksem("gXb")
        gPub = mksem("gPub")
        gCP = mksem("gCP")
        gCNT = mksem("gCNT")
        gSCT = mksem("gSCT")
        gTBL = mksem("gTBL")
        gGA = mksem("gGA")
        gWs = [mksem("gWs0"), mksem("gWs1")]
        gEO = mksem("gEO")
        gD = mksem("gD")
        gY = [mksem("gY0"), mksem("gY1")]
        gDbg = mksem("gDbg")

        def rec(eng, fn, s=None, k=1):
            if s is None:
                ops[eng].append(fn)
                return None
            s.n += k
            v = s.n
            ops[eng].append(lambda e, fn=fn, h=s.h, k=k: fn(e).then_inc(h, k))
            return v

        def wt(eng, s, v):
            assert v is not None
            waits_record.append((s, v))
            ops[eng].append(lambda e, h=s.h, v=v: e.wait_ge(h, v))

        def dma(eng, s, out, in_, **kw):
            return rec(eng, lambda e, o=out, i=in_, kw=kw: e.dma_start(out=o, in_=i, **kw),
                       s, 16)

        onesb = sbuf(P, "onesb", [128, 1], BF16)
        onesf = sbuf(P, "onesf", [128, 1], F32)
        ones1b = sbuf(P, "ones1b", [1, 128], BF16)
        ones1f = sbuf(P, "ones1f", [1, 128], F32)
        identb = sbuf(P, "identb", [128, 128], BF16)
        identf = sbuf(P, "identf", [128, 128], F32)
        tokidx_s = sbuf(P, "tokidx_s", [128, TT16], I32)
        slotsel_s = sbuf(P, "slotsel_s", [128, 20], I32)
        tblown = sbuf(P, "tblown", [128, 20], I32)
        peff_s = sbuf(P, "peff_s", [128, TT16], F32)
        sg_i = sbuf(P, "sg_i", [128, TT16], I32)
        st_i = sbuf(P, "st_i", [128, TT16], I32)
        eidx_s = sbuf(P, "eidx_s", [128, TT16], F32)
        psel_s = sbuf(P, "psel_s", [128, TT16], F32)
        kept_s = sbuf(P, "kept_s", [128, TT16], F32)
        prob_s = sbuf(P, "prob_s", [128, TT16], F32)
        slotf_s = sbuf(P, "slotf_s", [128, TT16], F32)
        epar_s = sbuf(P, "epar_s", [128, TT16], F32)
        zeri = sbuf(P, "zeri", [128, 161], I32)
        negB = sbuf(P, "negB", [128, 1], F32)
        epsB = sbuf(P, "epsB", [128, 1], F32)
        iota16 = sbuf(P, "iota16", [128, E], F32)
        iotmb = sbuf(P, "iotmb", [128, E], F32)
        iota_t = sbuf(P, "iota_t", [128, TT16 * E], F32)
        iotmb_t = sbuf(P, "iotmb_t", [128, TT16 * E], F32)
        parT_s = sbuf(P, "parT_s", [128, TT16 * E], F32)
        ehalfT_s = sbuf(P, "ehalfT_s", [128, TT16 * E], F32)

        attn_f = sbuf(stk_b1, "attn_f", [128, DT8 * TLOC], BF16)
        sums_sb = sbuf(stk_b1, "sums_sb", [97, TLOC], F32)

        Zf = sbuf(stk_a, "Zf", [128, DT8 * SEQ], BF16)
        Vh = sbuf(stk_a, "Vh", [128, KCH * 512], BF16)
        QT = [sbuf(stk_a, f"QT{i}", [128, TLOC], BF16) for i in range(2)]
        KT = [sbuf(stk_a, f"KT{i}", [128, SEQ], BF16) for i in range(2)]
        Wq = [sbuf(stk_a, f"Wq{i}", [128, DT8 * 128], BF16) for i in range(2)]
        Wk = [sbuf(stk_a, f"Wk{i}", [128, DT8 * 128], BF16) for i in range(2)]
        Wv = sbuf(stk_a, "Wv", [128, DT8 * 512], BF16)
        PTb = [sbuf(stk_a, f"PTb{i}", [128, 1024], BF16) for i in range(3)]
        PS = stk_a.enter_context(nc.psum_tensor("PS", [128, 4096], F32))

        dma("gpsimd", gA0, identb[:, :], identb_d[:, :])
        dma("gpsimd", gA0, identf[:, :], identf_d[:, :])
        dma("gpsimd", gA0, tokidx_s[:, :], tokidx_d[:, :])
        dma("gpsimd", gA0, slotsel_s[:, :], slotsel_d[:, :])
        dma("gpsimd", gA0, parT_s[:, :], part_d[:, :])
        dma("gpsimd", gA0, ehalfT_s[:, :], ehalf_d[:, :])
        for j in range(DT8):
            dma("gpsimd", gA0, Zf[:, j * SEQ:(j + 1) * SEQ], zT[j * 128:(j + 1) * 128, :])

        rec("vector", lambda e: e.memset(onesb[:, :], 1.0))
        rec("vector", lambda e: e.memset(onesf[:, :], 1.0))
        rec("vector", lambda e: e.memset(ones1b[:, :], 1.0))
        rec("vector", lambda e: e.memset(ones1f[:, :], 1.0))
        rec("vector", lambda e: e.memset(negB[:, :], -SOFT_B))
        rec("vector", lambda e: e.memset(epsB[:, :], 1e-5))
        v_zeri = rec("vector", lambda e: e.memset(zeri[:, :], 0), sDV)
        v_iota = rec("gpsimd", lambda e: e.iota(iota16[:, :], [[1, E]], channel_multiplier=0,
                                               allow_small_or_imprecise_dtypes=True), gA0, 16)
        wt("gpsimd", sDV, v_zeri)
        dma("gpsimd", gA0, tbl_in.ap().rearrange("(p a) x -> p (a x)", p=128), zeri[:, :])

        wq_p = wqkvT.ap().rearrange("(j p) (h q x) -> p j h q x", p=128, h=NH, x=DH)

        def load_wv(half, s):
            v = None
            for j in range(DT8):
                v = dma("gpsimd", s,
                        Wv[:, j * 512:(j + 1) * 512].rearrange("p (h x) -> p h x", h=8),
                        wq_p[:, j, half * 8:half * 8 + 8, 2, :])
            return v

        v_wv1 = load_wv(0, gA0)
        vA0_full = gA0.n

        v_iot2 = rec("gpsimd", lambda e: e.iota(iota_t[:, :], [[0, TT16], [1, E]],
                                                channel_multiplier=0,
                                                allow_small_or_imprecise_dtypes=True), gA0, 16)
        vA0 = gA0.n
        if debug:
            wt("sync", gA0, vA0)
            dma("sync", gDbg, dbg["epar"].ap(), parT_s[:, 16:32])
        wt("vector", gA0, vA0)
        rec("vector", lambda e: e.tensor_scalar_add(iotmb[:, :], iota16[:, :], -100.0))
        rec("vector", lambda e: e.tensor_scalar_add(iotmb_t[:, :], iota_t[:, :], -100.0))

        def load_wqk(p):
            s = gWq[p % 2]
            for j in range(DT8):
                dma("gpsimd", s,
                    Wq[p % 2][:, j * 128:(j + 1) * 128].rearrange("p (h x) -> p h x", h=2),
                    wq_p[:, j, 2 * p:2 * p + 2, 0, :])
                v = dma("gpsimd", s,
                        Wk[p % 2][:, j * 128:(j + 1) * 128].rearrange("p (h x) -> p h x", h=2),
                        wq_p[:, j, 2 * p:2 * p + 2, 1, :])
            return v

        def emit_vpass(wait_list, vev):
            vmm = {}
            for tt in range(KCH):
                buf = tt % 2
                if tt == 0:
                    for s_, v_ in wait_list:
                        wt("tensor", s_, v_)
                if tt >= 2:
                    wt("tensor", sDV, vev[tt - 2])
                for j in range(DT8):
                    def mm(e, j=j, tt=tt, buf=buf):
                        return e.matmul(PS[:, 3072 + buf * 512:3072 + (buf + 1) * 512],
                                        Zf[:, j * SEQ + tt * 128: j * SEQ + tt * 128 + 128],
                                        Wv[:, j * 512:(j + 1) * 512],
                                        start=(j == 0), stop=(j == DT8 - 1),
                                        skip_group_check=True)
                    if j == DT8 - 1:
                        vmm[tt] = rec("tensor", mm, sPE)
                    else:
                        rec("tensor", mm)
                wt("vector", sPE, vmm[tt])
                def ev(e, tt=tt, buf=buf):
                    return e.tensor_copy(Vh[:, tt * 512:(tt + 1) * 512],
                                         PS[:, 3072 + buf * 512:3072 + (buf + 1) * 512])
                vev[tt] = rec("vector", ev, sDV)
            return vmm

        vev1 = {}
        emit_vpass([(gA0, vA0)], vev1)

        v_w = {0: load_wqk(0)}
        prod_mm, prod_ev, avev, sums_done = {}, {}, {}, {}
        score_v, exp_v, consumed_v = {}, {}, {}
        sums_pend = []
        sums_bank_wait = []
        vev2 = {}

        for p in range(NPAIR):
            par = p % 2
            half = p // 4
            vev_half = vev1 if half == 0 else vev2

            if p == 0:
                wt("tensor", sDV, vev1[KCH - 1])
            else:
                wt("tensor", sDV, avev[(p - 1, NQB - 1)])
            wt("tensor", gWq[par], v_w[p])
            if p >= 2:
                wt("tensor", sSc, (p - 1) * NCHUNK)
            for g in range(12):
                buf = g % 2
                isq = g < 4
                tc = g if isq else g - 4
                if p == 4 and g < 2:
                    wt("tensor", sDV, vev2[KCH - 2 + g])
                elif g >= 2:
                    wt("tensor", sDV, prod_ev[(p, g - 2)])
                elif p > 0:
                    wt("tensor", sDV, prod_ev[(p - 1, 10 + g)])
                for j in range(DT8):
                    def mm(e, par=par, isq=isq, tc=tc, j=j, buf=buf):
                        W = Wq[par] if isq else Wk[par]
                        return e.matmul(PS[:, 3072 + buf * 512:3072 + (buf + 1) * 512],
                                        W[:, j * 128:(j + 1) * 128],
                                        Zf[:, j * SEQ + tc * 512: j * SEQ + tc * 512 + 512],
                                        start=(j == 0), stop=(j == DT8 - 1),
                                        skip_group_check=True)
                    if j == DT8 - 1:
                        prod_mm[(p, g)] = rec("tensor", mm, sPE)
                    else:
                        rec("tensor", mm)
                wt("vector", sPE, prod_mm[(p, g)])
                def ev(e, par=par, isq=isq, tc=tc, buf=buf):
                    dst = QT[par] if isq else KT[par]
                    return e.tensor_copy(dst[:, tc * 512:(tc + 1) * 512],
                                         PS[:, 3072 + buf * 512:3072 + (buf + 1) * 512])
                prod_ev[(p, g)] = rec("vector", ev, sDV)
            if p + 1 < NPAIR:
                if p >= 1:
                    wt("gpsimd", sPE, prod_mm[(p - 1, 11)])
                v_w[p + 1] = load_wqk(p + 1)

            v_prod = prod_ev[(p, 11)]
            hhA = (2 * p) % 8
            hhB = hhA + 1

            def emit_scores(nl):
                qb, jj = nl // KCH, nl % KCH
                N = p * NCHUNK + nl
                spar = N % 3
                soff = (N % 3) * 1024
                if nl == 0:
                    wt("tensor", sDV, v_prod)
                if N >= 3:
                    wt("tensor", sEx, exp_v[N - 3])
                def mmA(e, par=par, soff=soff, jj=jj, qb=qb):
                    return e.matmul(PS[:, soff:soff + 512],
                                    KT[par][0:64, jj * 128:(jj + 1) * 128],
                                    QT[par][0:64, qb * QB:(qb + 1) * QB],
                                    start=True, stop=True, tile_position=(0, 0),
                                    skip_group_check=True)
                rec("tensor", mmA)
                def mmB(e, par=par, soff=soff, jj=jj, qb=qb):
                    return e.matmul(PS[:, soff + 512:soff + 1024],
                                    KT[par][64:128, jj * 128:(jj + 1) * 128],
                                    QT[par][64:128, qb * QB:(qb + 1) * QB],
                                    start=True, stop=True, tile_position=(64, 0),
                                    skip_group_check=True)
                score_v[N] = rec("tensor", mmB, sSc)
                wt("scalar", sSc, score_v[N])
                if N >= 3 and (N - 3) in consumed_v:
                    wt("scalar", sAV, consumed_v[N - 3])
                def ex(e, spar=spar, soff=soff):
                    return e.activation(PTb[spar][:, :], PS[:, soff:soff + 1024],
                                        AF.Exp, bias=negB[:, :], scale=0.125)
                exp_v[N] = rec("scalar", ex, sEx)

            def emit_av(nl):
                qb, jj = nl // KCH, nl % KCH
                N = p * NCHUNK + nl
                spar = N % 3
                wt("tensor", sEx, exp_v[N])
                if jj == 0:
                    if qb > 0:
                        wt("tensor", sDV, avev[(p, qb - 1)])
                    elif p > 0:
                        wt("tensor", sDV, avev[(p - 1, NQB - 1)])
                    wt("tensor", sDV, vev_half[KCH - 1])
                st_ = (jj == 0)
                sp_ = (jj == KCH - 1)
                def av1(e, jj=jj, spar=spar, st_=st_, sp_=sp_, hh=hhA):
                    return e.matmul(PS[0:64, 3072:3584],
                                    Vh[:, jj * 512 + hh * 64: jj * 512 + hh * 64 + 64],
                                    PTb[spar][:, 0:512], start=st_, stop=sp_,
                                    tile_position=(0, 0), skip_group_check=True)
                def av2(e, jj=jj, spar=spar, st_=st_, sp_=sp_, hh=hhB):
                    return e.matmul(PS[64:128, 3072:3584],
                                    Vh[:, jj * 512 + hh * 64: jj * 512 + hh * 64 + 64],
                                    PTb[spar][:, 512:1024], start=st_, stop=sp_,
                                    tile_position=(0, 64), skip_group_check=True)
                rec("tensor", av1)
                rec("tensor", av2)
                if jj % 2 == 1:
                    emit_sums_pair(nl - 1, nl)
                if jj == KCH - 1:
                    emit_qb_evac(qb)

            def emit_sums_pair(nl0, nl1):
                jj0, jj1 = nl0 % KCH, nl1 % KCH
                N0, N1 = p * NCHUNK + nl0, p * NCHUNK + nl1
                if jj0 == 0:
                    for w_ in sums_bank_wait:
                        wt("tensor", sDV, w_)
                v_sm = None
                for k, (N, jj) in enumerate(((N0, jj0), (N1, jj1))):
                    spar = N % 3
                    st_ = (jj == k)
                    sp_ = (jj == KCH - 2 + k)
                    for h2 in range(2):
                        row = k * 64 + h2 * 32
                        def sm(e, spar=spar, h2=h2, row=row, st_=st_, sp_=sp_):
                            return e.matmul(PS[row:row + 1, 3584:4096], onesb[:, :],
                                            PTb[spar][:, h2 * 512:(h2 + 1) * 512],
                                            start=st_, stop=sp_,
                                            tile_position=(0, row),
                                            skip_group_check=True)
                        v_sm = rec("tensor", sm, sAV if (k, h2) == (1, 1) else None)
                consumed_v[N0] = v_sm
                consumed_v[N1] = v_sm

            def emit_qb_evac(qb):
                v_last = consumed_v[p * NCHUNK + qb * KCH + KCH - 1]
                wt("vector", sAV, v_last)
                if p > 0 and qb == 0:
                    wt("vector", gST1, sums_done[p - 1])
                cols = p * TLOC + qb * QB
                rec("vector", lambda e, cols=cols: e.tensor_copy(
                    attn_f[:, cols:cols + QB], PS[:, 3072:3584]))
                v_se = None
                for row in (0, 32, 64, 96):
                    def se(e, row=row, qb=qb):
                        return e.tensor_copy(sums_sb[row:row + 1, qb * QB:(qb + 1) * QB],
                                             PS[row:row + 1, 3584:4096])
                    v_se = rec("vector", se, sDV)
                avev[(p, qb)] = v_se
                sums_bank_wait[:] = [v_se]

            for nl in range(NCHUNK):
                emit_scores(nl)
                if nl >= 1:
                    emit_av(nl - 1)
            emit_av(NCHUNK - 1)
            wt("sync", sDV, avev[(p, NQB - 1)])
            v_sd = None
            for k in range(2):
                for h2 in range(2):
                    row = k * 64 + h2 * 32
                    v_sd = dma("sync", gST1,
                               sums_dram[k * 2 * NPAIR + 2 * p + h2:
                                         k * 2 * NPAIR + 2 * p + h2 + 1, :],
                               sums_sb[row:row + 1, 0:TLOC])
            sums_done[p] = v_sd
            if p == 3:
                v_wv2 = load_wv(1, gWv2)
                emit_vpass([(gWv2, v_wv2), (sDV, avev[(3, NQB - 1)]),
                            (sDV, prod_ev[(3, 11)])], vev2)

        v_attn_done = avev[(NPAIR - 1, NQB - 1)]
        v_dbg_attn = None
        if debug:
            wt("sync", sDV, v_attn_done)
            v_dbg_attn = dma("sync", gDbg, dbg["attn"].ap(), attn_f[:, :])
            wt("sync", gST1, sums_done[NPAIR - 1])
            dma("sync", gDbg, dbg["sums"].ap(), sums_dram.ap())

        if stop_after != "A":
            stk_a.close()
            zz = sbuf(stk_b2, "zz", [128, DT8 * TLOC], F32)
            scrA = sbuf(stk_b2, "scrA", [128, 2048], F32)
            scrB = sbuf(stk_b2, "scrB", [128, 1024], F32)
            rowA = sbuf(stk_b2, "rowA", [1, 2048], F32)
            rowB = sbuf(stk_b2, "rowB", [1, 2048], F32)
            rowC = sbuf(stk_b2, "rowC", [1, 2048], F32)
            rowD = sbuf(stk_b2, "rowD", [1, 2048], F32)
            srec = sbuf(stk_b2, "srec", [128, 2 * NPAIR * 16], F32)
            srec2 = sbuf(stk_b2, "srec2", [128, 2 * NPAIR * 16], F32)
            browb = sbuf(stk_b2, "browb", [1, 4096], BF16)
            mx16 = sbuf(stk_b2, "mx16", [128, TT16], F32)
            sm16 = sbuf(stk_b2, "sm16", [128, TT16], F32)
            df16 = sbuf(stk_b2, "df16", [128, TT16 * E], F32)
            ex16 = sbuf(stk_b2, "ex16", [128, TT16 * E], F32)
            eq16 = sbuf(stk_b2, "eq16", [128, TT16 * E], F32)

            gwT_s = sbuf(stk_b2, "gwT_s", [128, DT8 * E], F32)
            lgs = sbuf(stk_b2, "lgs", [16, TLOC], F32)
            logit_t = sbuf(stk_b2, "logit_t", [128, TT16 * E], F32)
            triu_s = sbuf(stk_b2, "triu_s", [128, 128], F32)
            trib_s = sbuf(stk_b2, "trib_s", [16, 16 * 128], F32)
            rmb_s = sbuf(stk_b2, "rmb_s", [8, 128], F32)
            cntag = sbuf(stk_b2, "cntag", [8, E], F32)
            cnt2 = sbuf(stk_b2, "cnt2", [16, E], F32)
            cntrow = sbuf(stk_b2, "cntrow", [1, TT16 * E], F32)
            oh_all = sbuf(stk_b2, "oh_all", [128, TT16 * E], F32)
            cum_s = sbuf(stk_b2, "cum_s", [128, TT16 * E], F32)
            pos_s = sbuf(stk_b2, "pos_s", [128, TT16 * E], F32)
            ctot_s = sbuf(stk_b2, "ctot_s", [1, E], F32)
            sexp = sbuf(stk_b2, "sexp", [128, E], F32)
            x1t_s = [sbuf(stk_b2, f"x1t_s{i}", [128, 1024], BF16) for i in range(2)]
            PS2 = stk_b2.enter_context(nc.psum_tensor("PS2", [128, 4096], F32))

            dma("sync", gLdB, gwT_s[:, :].rearrange("p (j e) -> p j e", j=DT8),
                gwT.ap().rearrange("(j p) e -> p j e", p=128))
            dma("sync", gLdB, triu_s[:, :], triu_d[:, :])
            dma("sync", gLdB, trib_s[:, :], trib_d[:, :])
            dma("sync", gLdB, rmb_s[:, :], rmb_d[:, :])
            for j in range(DT8):
                dma("sync", gLdB, zz[:, j * TLOC:(j + 1) * TLOC],
                    zT[j * 128:(j + 1) * 128, 0:TLOC])
            vLdB = gLdB.n

            wt("sync", gST1, sums_done[NPAIR - 1])
            dma("sync", gRow, srec[:, :].rearrange("p (h c) -> p h c", h=16),
                sums_dram[0:2 * NPAIR, :].rearrange("h (p c) -> p h c", c=16))
            v_sl = dma("sync", gRow, srec2[:, :].rearrange("p (h c) -> p h c", h=16),
                       sums_dram[2 * NPAIR:4 * NPAIR, :].rearrange("h (p c) -> p h c", c=16))
            wt("vector", gRow, v_sl)
            v_ra = rec("vector", lambda e: e.tensor_add(srec[:, :], srec[:, :],
                                                        srec2[:, :]), sDV)
            wt("vector", sDV, v_ra)
            v_r = rec("vector", lambda e: e.reciprocal(srec[:, :], srec[:, :]), sDV)
            wt("sync", sDV, v_r)
            v_sr = dma("sync", gRow, sums_r.ap().rearrange("h (p c) -> p h c", c=16),
                       srec[:, :].rearrange("p (h c) -> p h c", h=16))
            v_scaled = None
            v_bc_last = {}
            for p2 in range(NPAIR):
                if p2 > 0:
                    wt("gpsimd", sPE, v_bc_last[p2 - 1])
                else:
                    wt("gpsimd", gRow, v_sr)
                v_ld = dma("gpsimd", gCNT, browb[0:1, :].rearrange("o (h t) -> o h t", h=2),
                           sums_r[2 * p2:2 * p2 + 2, :])
                wt("tensor", gCNT, v_ld)
                for hh2 in range(2):
                    par2 = hh2
                    if p2 > 0:
                        wt("tensor", sDV, v_sc_prev[hh2])
                    vb = None
                    for c4 in range(4):
                        def bc(e, hh2=hh2, c4=c4, par2=par2):
                            return e.matmul(
                                PS2[:, par2 * 2048 + c4 * 512:par2 * 2048 + (c4 + 1) * 512],
                                ones1b[0:1, :],
                                browb[0:1, hh2 * 2048 + c4 * 512:hh2 * 2048 + (c4 + 1) * 512],
                                start=True, stop=True, skip_group_check=True)
                        vb = rec("tensor", bc, sPE if c4 == 3 else None)
                    v_bc_last[p2] = vb
                    wt("vector", sPE, vb)
                    if p2 == 0 and hh2 == 0:
                        wt("vector", sDV, v_attn_done)
                        if debug:
                            wt("vector", gDbg, v_dbg_attn)
                    rows = slice(64 * hh2, 64 * hh2 + 64)
                    def sc(e, rows=rows, p2=p2, par2=par2):
                        return e.tensor_tensor(attn_f[rows, p2 * TLOC:(p2 + 1) * TLOC],
                                               attn_f[rows, p2 * TLOC:(p2 + 1) * TLOC],
                                               PS2[rows, par2 * 2048:(par2 + 1) * 2048],
                                               ALU.mult)
                    v_scaled = rec("vector", sc, sDV)
                    v_sc_prev = locals().get("v_sc_prev", {})
                    v_sc_prev[hh2] = v_scaled

            wt("vector", gLdB, vLdB)
            wt("vector", sDV, v_scaled)
            v_x = None
            for j in range(DT8):
                def add(e, j=j):
                    return e.tensor_add(zz[:, j * TLOC:(j + 1) * TLOC],
                                        zz[:, j * TLOC:(j + 1) * TLOC],
                                        attn_f[:, j * TLOC:(j + 1) * TLOC])
                v_x = rec("vector", add, sDV if j == DT8 - 1 else None)

            sx_off = [0, 512, 3072, 3584]
            v_sq = {}
            v_stat = None
            v_mmrow = None
            for j in range(DT8):
                for hf in range(2):
                    if j == 0 and hf == 0:
                        wt("scalar", sDV, v_x)
                    elif v_mmrow is not None:
                        wt("scalar", sPE, v_mmrow)
                    def sq(e, j=j, hf=hf):
                        return e.activation(scrA[:, hf * 1024:(hf + 1) * 1024],
                                            zz[:, j * TLOC + hf * 1024: j * TLOC + hf * 1024 + 1024],
                                            AF.Square)
                    v_sq[(j, hf)] = rec("scalar", sq, sAC)
                    if j == 0 and hf == 0:
                        wt("tensor", sDV, v_x)
                    wt("tensor", sAC, v_sq[(j, hf)])
                    for c2 in range(2):
                        c4 = hf * 2 + c2
                        def m1(e, j=j, c4=c4):
                            return e.matmul(PS2[0:1, sx_off[c4]:sx_off[c4] + 512], onesf[:, :],
                                            zz[:, j * TLOC + c4 * 512: j * TLOC + (c4 + 1) * 512],
                                            start=(j == 0), stop=(j == DT8 - 1),
                                            skip_group_check=True)
                        rec("tensor", m1)
                    for c2 in range(2):
                        c4 = hf * 2 + c2
                        def m2(e, j=j, hf=hf, c2=c2, c4=c4):
                            return e.matmul(PS2[0:1, 1024 + c4 * 512:1024 + (c4 + 1) * 512], onesf[:, :],
                                            scrA[:, hf * 1024 + c2 * 512: hf * 1024 + (c2 + 1) * 512],
                                            start=(j == 0), stop=(j == DT8 - 1),
                                            skip_group_check=True)
                        v_mmrow = rec("tensor", m2, sPE if c2 == 1 else None)
            v_stat = v_mmrow
            wt("vector", sPE, v_stat)
            rec("vector", lambda e: e.tensor_copy(rowA[0:1, 0:1024], PS2[0:1, 0:1024]))
            v1 = rec("vector", lambda e: e.tensor_copy(rowA[0:1, 1024:2048], PS2[0:1, 3072:4096]), sDV)
            v2 = rec("vector", lambda e: e.tensor_copy(rowB[0:1, :], PS2[0:1, 1024:3072]), sDV)
            wt("vector", sDV, v2)
            v3 = rec("vector", lambda e: e.tensor_scalar_mul(rowA[0:1, :], rowA[0:1, :], 1.0 / D), sDV)
            wt("vector", sDV, v3)
            v4 = rec("vector", lambda e: e.tensor_scalar_mul(rowB[0:1, :], rowB[0:1, :], 1.0 / D), sDV)
            wt("vector", sDV, v4)
            v5 = rec("vector", lambda e: e.tensor_tensor(rowC[0:1, :], rowA[0:1, :], rowA[0:1, :], ALU.mult), sDV)
            wt("vector", sDV, v5)
            v6 = rec("vector", lambda e: e.tensor_tensor(rowB[0:1, :], rowB[0:1, :], rowC[0:1, :], ALU.subtract), sDV)
            wt("scalar", sDV, v6)
            v7 = rec("scalar", lambda e: e.activation(rowC[0:1, :], rowB[0:1, :], AF.Ln, bias=epsB[0:1, :], scale=1.0), sAC)
            wt("scalar", sAC, v7)
            v8 = rec("scalar", lambda e: e.activation(rowB[0:1, :], rowC[0:1, :], AF.Exp, bias=0.0, scale=-0.5), sAC)
            wt("vector", sAC, v8)
            v9 = rec("vector", lambda e: e.scalar_tensor_tensor(rowD[0:1, :], rowA[0:1, :], -1.0,
                                                                rowB[0:1, :], ALU.mult, ALU.mult), sDV)
            v_ln = None
            for hhf in range(2):
                wt("tensor", sDV, v9)
                if v_ln is not None:
                    wt("tensor", sDV, v_ln)
                vb = None
                for c2 in range(2):
                    def bcr(e, hhf=hhf, c2=c2):
                        return e.matmul(PS2[:, 1024 + c2 * 512:1024 + (c2 + 1) * 512], ones1f[0:1, :],
                                        rowB[0:1, hhf * 1024 + c2 * 512:hhf * 1024 + (c2 + 1) * 512],
                                        start=True, stop=True, skip_group_check=True)
                    rec("tensor", bcr)
                    def bcn(e, hhf=hhf, c2=c2):
                        return e.matmul(PS2[:, 2048 + c2 * 512:2048 + (c2 + 1) * 512], ones1f[0:1, :],
                                        rowD[0:1, hhf * 1024 + c2 * 512:hhf * 1024 + (c2 + 1) * 512],
                                        start=True, stop=True, skip_group_check=True)
                    vb = rec("tensor", bcn, sPE if c2 == 1 else None)
                wt("vector", sPE, vb)
                for j in range(DT8):
                    def t1(e, j=j, hhf=hhf):
                        return e.tensor_tensor(scrB[:, 0:1024],
                                               zz[:, j * TLOC + hhf * 1024: j * TLOC + hhf * 1024 + 1024],
                                               PS2[:, 2048:3072], ALU.add)
                    va = rec("vector", t1, sDV)
                    wt("vector", sDV, va)
                    def t2(e, j=j, hhf=hhf):
                        return e.tensor_tensor(zz[:, j * TLOC + hhf * 1024: j * TLOC + hhf * 1024 + 1024],
                                               scrB[:, 0:1024], PS2[:, 1024:2048], ALU.mult)
                    v_ln = rec("vector", t2, sDV)
                    wt("vector", sDV, v_ln)

            wt("gpsimd", sDV, v_ln)
            for j in range(DT8):
                dma("gpsimd", gFM, x1fm_dram[j * 128:(j + 1) * 128, :], zz[:, j * TLOC:(j + 1) * TLOC])
            vFM = gFM.n
            v_xb = {}
            for tt in range(TT16):
                buf = tt % 2
                if tt == 0:
                    wt("sync", gFM, vFM)
                if tt >= 2:
                    wt("sync", gPub, tt * 16)
                v_xb[tt] = rec("sync", lambda e, tt=tt, buf=buf: e.dma_start(
                    out=x1t_s[buf][:, :], in_=x1fm_dram[:, tt * 128:(tt + 1) * 128],
                    transpose=True), gXb, 16)
                wt("sync", gXb, v_xb[tt])
                dma("sync", gPub, x1pub[tt * 128:(tt + 1) * 128, :], x1t_s[buf][:, :])
            vPub = gPub.n
            wt("gpsimd", gPub, vPub)
            v_ag_x = rec("gpsimd", lambda e: e.collective_compute(
                "AllGather", ALU.bypass, replica_groups=[list(range(NCORE))],
                ins=[x1pub.ap()], outs=[xall.ap()]), sCC)

            wt("tensor", gLdB, vLdB)
            wt("tensor", sDV, v_ln)
            v_lg = None
            for j in range(DT8):
                for c4 in range(4):
                    def mg(e, j=j, c4=c4):
                        return e.matmul(PS2[0:16, 1024 + c4 * 512:1024 + (c4 + 1) * 512],
                                        gwT_s[:, j * E:(j + 1) * E],
                                        zz[:, j * TLOC + c4 * 512: j * TLOC + (c4 + 1) * 512],
                                        start=(j == 0), stop=(j == DT8 - 1),
                                        skip_group_check=True)
                    v_lg = rec("tensor", mg, sPE if (j == DT8 - 1 and c4 == 3) else None)
            wt("vector", sPE, v_lg)
            v_lge = rec("vector", lambda e: e.tensor_copy(lgs[:, :], PS2[0:16, 1024:3072]), sDV)
            wt("tensor", sDV, v_lge)
            v_lt = None
            for tt in range(TT16):
                def tr(e, tt=tt):
                    return e.transpose(PS2[:, tt * 16:(tt + 1) * 16],
                                       lgs[0:16, tt * 128:(tt + 1) * 128], identf[0:16, 0:16])
                v_lt = rec("tensor", tr, sPE if tt == TT16 - 1 else None)
            wt("vector", sPE, v_lt)
            v_lte = rec("vector", lambda e: e.tensor_copy(logit_t[:, :], PS2[:, 0:256]), sDV)
            if debug:
                wt("sync", sDV, v_lte)
                dma("sync", gDbg, dbg["logits"].ap(), logit_t[:, :])

            def seg(ap):
                return ap.rearrange("p (t e) -> p t e", e=E)

            def bc16(tile_ap):
                return bass.AP(tile_ap.tensor, tile_ap.offset,
                               [[TT16, 128], [1, TT16], [0, E]])

            wt("vector", sDV, v_lte)
            r1 = rec("vector", lambda e: e.tensor_reduce(mx16[:, :], seg(logit_t[:, :]),
                                                         axis=AX, op=ALU.max), sDV)
            wt("vector", sDV, r1)
            r2 = rec("vector", lambda e: e.tensor_tensor(seg(df16[:, :]), seg(logit_t[:, :]),
                                                         bc16(mx16[:, :]), ALU.subtract), sDV)
            wt("scalar", sDV, r2)
            r3 = rec("scalar", lambda e: e.activation(ex16[:, :], df16[:, :], AF.Exp), sAC)
            wt("vector", sAC, r3)
            r4 = rec("vector", lambda e: e.tensor_reduce(sm16[:, :], seg(ex16[:, :]),
                                                         axis=AX, op=ALU.add), sDV)
            wt("vector", sDV, r4)
            rec("vector", lambda e: e.reciprocal(prob_s[:, :], sm16[:, :]))
            r6 = rec("vector", lambda e: e.tensor_tensor(seg(eq16[:, :]), seg(logit_t[:, :]),
                                                         bc16(mx16[:, :]), ALU.is_ge), sDV)
            wt("vector", sDV, r6)
            r7 = rec("vector", lambda e: e.tensor_tensor(eq16[:, :], eq16[:, :],
                                                         iotmb_t[:, :], ALU.mult), sDV)
            wt("vector", sDV, r7)
            r8 = rec("vector", lambda e: e.tensor_scalar_add(eq16[:, :], eq16[:, :], 100.0), sDV)
            wt("vector", sDV, r8)
            r9 = rec("vector", lambda e: e.tensor_reduce(eidx_s[:, :], seg(eq16[:, :]),
                                                         axis=AX, op=ALU.min), sDV)
            wt("vector", sDV, r9)
            v_ohv = rec("vector", lambda e: e.tensor_tensor(seg(oh_all[:, :]), iota_t[:, :].rearrange("p (t e) -> p t e", e=E),
                                                            bc16(eidx_s[:, :]), ALU.is_equal), sDV)

            wt("tensor", gLdB, vLdB)
            wt("tensor", sDV, v_ohv)
            for tt in range(TT16):
                def mc(e, tt=tt):
                    return e.matmul(PS2[:, 256 + tt * 16:256 + (tt + 1) * 16], triu_s[:, :],
                                    oh_all[:, tt * E:(tt + 1) * E], start=True, stop=True,
                                    skip_group_check=True)
                rec("tensor", mc)
            for tt in range(TT16):
                def mn(e, tt=tt):
                    return e.matmul(PS2[0:1, 512 + tt * 16:512 + (tt + 1) * 16], onesf[:, :],
                                    oh_all[:, tt * E:(tt + 1) * E], start=True, stop=True,
                                    skip_group_check=True)
                rec("tensor", mn)
            v_ct = None
            for tt in range(TT16):
                def mt(e, tt=tt):
                    return e.matmul(PS2[0:1, 768:784], onesf[:, :],
                                    oh_all[:, tt * E:(tt + 1) * E],
                                    start=(tt == 0), stop=(tt == TT16 - 1),
                                    skip_group_check=True)
                v_ct = rec("tensor", mt, sPE if tt == TT16 - 1 else None)
            wt("vector", sPE, v_ct)
            v_cume = rec("vector", lambda e: e.tensor_copy(cum_s[:, :], PS2[:, 256:512]), sDV)
            v_cnte = rec("vector", lambda e: e.tensor_copy(cntrow[0:1, :], PS2[0:1, 512:768]), sDV)
            v_cte = rec("vector", lambda e: e.tensor_copy(ctot_s[0:1, :], PS2[0:1, 768:784]), sDV)
            wt("sync", sDV, v_cte)
            vCP = dma("sync", gCP, cnt_in[0:1, :], ctot_s[0:1, :])
            wt("gpsimd", gCP, vCP)
            v_ag_c = rec("gpsimd", lambda e: e.collective_compute(
                "AllGather", ALU.bypass, replica_groups=[list(range(NCORE))],
                ins=[cnt_in.ap()], outs=[cnt_out.ap()]), sCC)
            wt("gpsimd", sDV, v_cnte)
            for tt in range(TT16):
                dma("gpsimd", gCNT, cnt2[tt:tt + 1, :], cntrow[0:1, tt * E:(tt + 1) * E])
            wt("gpsimd", sCC, v_ag_c)
            dma("gpsimd", gCNT, cntag[:, :], cnt_out.ap())
            vCNT = gCNT.n
            wt("tensor", gCNT, vCNT)
            v_pb = None
            for tt in range(TT16):
                def mb1(e, tt=tt):
                    return e.matmul(PS2[:, 784 + tt * 16:784 + (tt + 1) * 16],
                                    trib_s[:, tt * 128:(tt + 1) * 128], cnt2[:, :],
                                    start=True, stop=False, skip_group_check=True)
                rec("tensor", mb1)
                def mb2(e, tt=tt):
                    return e.matmul(PS2[:, 784 + tt * 16:784 + (tt + 1) * 16],
                                    rmb_s[:, 0:128], cntag[:, :],
                                    start=False, stop=True, skip_group_check=True)
                v_pb = rec("tensor", mb2, sPE if tt == TT16 - 1 else None)
            wt("vector", sPE, v_pb)
            wt("vector", sDV, v_cume)
            v_pos = rec("vector", lambda e: e.tensor_tensor(pos_s[:, :], cum_s[:, :],
                                                            PS2[:, 784:1040], ALU.add), sDV)
            wt("vector", sDV, v_pos)
            s1_ = rec("vector", lambda e: e.tensor_tensor(eq16[:, :], oh_all[:, :],
                                                          pos_s[:, :], ALU.mult), sDV)
            wt("vector", sDV, s1_)
            s2_ = rec("vector", lambda e: e.tensor_reduce(psel_s[:, :], seg(eq16[:, :]),
                                                          axis=AX, op=ALU.add), sDV)
            wt("vector", sDV, s2_)
            s3_ = rec("vector", lambda e: e.tensor_single_scalar(kept_s[:, :], psel_s[:, :],
                                                                 float(CAP), ALU.is_lt), sDV)
            wt("vector", sDV, s3_)
            rec("vector", lambda e: e.tensor_tensor(peff_s[:, :], prob_s[:, :],
                                                    kept_s[:, :], ALU.mult))
            s5_ = rec("vector", lambda e: e.scalar_tensor_tensor(slotf_s[:, :], eidx_s[:, :],
                                                                 float(CAP), psel_s[:, :],
                                                                 ALU.mult, ALU.add), sDV)
            sp0 = rec("vector", lambda e: e.tensor_tensor(ex16[:, :], oh_all[:, :],
                                                          parT_s[:, :], ALU.mult), sDV)
            wt("vector", sDV, sp0)
            sp1 = rec("vector", lambda e: e.tensor_reduce(epar_s[:, :], seg(ex16[:, :]),
                                                          axis=AX, op=ALU.add), sDV)
            wt("vector", sDV, sp1)
            sp0b = rec("vector", lambda e: e.tensor_tensor(ex16[:, :], oh_all[:, :],
                                                           ehalfT_s[:, :], ALU.mult), sDV)
            wt("vector", sDV, sp0b)
            sp2 = rec("vector", lambda e: e.tensor_reduce(mx16[:, :], seg(ex16[:, :]),
                                                          axis=AX, op=ALU.add), sDV)
            wt("vector", sDV, sp2)
            sp3 = rec("vector", lambda e: e.scalar_tensor_tensor(mx16[:, :], mx16[:, :],
                                                                 float(CAP), psel_s[:, :],
                                                                 ALU.mult, ALU.add), sDV)
            wt("vector", sDV, sp3)
            sp4 = rec("vector", lambda e: e.tensor_tensor(mx16[:, :], mx16[:, :],
                                                          kept_s[:, :], ALU.mult), sDV)
            wt("vector", sDV, sp4)
            rec("vector", lambda e: e.tensor_copy(sg_i[:, :], mx16[:, :]))
            wt("vector", sDV, s5_)
            s6_ = rec("vector", lambda e: e.tensor_tensor(df16[:, 0:TT16], slotf_s[:, :],
                                                          kept_s[:, :], ALU.mult), sDV)
            wt("vector", sDV, s6_)
            s8_ = rec("vector", lambda e: e.scalar_tensor_tensor(df16[:, TT16:2 * TT16], kept_s[:, :],
                                                                 -float(TRASH), df16[:, 0:TT16],
                                                                 ALU.mult, ALU.add), sDV)
            wt("vector", sDV, s8_)
            s9_ = rec("vector", lambda e: e.tensor_scalar_add(df16[:, TT16:2 * TT16],
                                                              df16[:, TT16:2 * TT16],
                                                              float(TRASH)), sDV)
            wt("vector", sDV, s9_)
            v_slots = rec("vector", lambda e: e.tensor_copy(st_i[:, :], df16[:, TT16:2 * TT16]), sDV)
            if debug:
                wt("sync", sDV, v_slots)
                dma("sync", gDbg, dbg["route"][:, 0 * TT16:1 * TT16], eidx_s[:, :])
                dma("sync", gDbg, dbg["route"][:, 1 * TT16:2 * TT16], psel_s[:, :])
                dma("sync", gDbg, dbg["route"][:, 2 * TT16:3 * TT16], kept_s[:, :])
                dma("sync", gDbg, dbg["route"][:, 3 * TT16:4 * TT16], prob_s[:, :])
                dma("sync", gDbg, dbg["route"][:, 4 * TT16:5 * TT16], slotf_s[:, :])
                wt("sync", gPub, vPub)
                dma("sync", gDbg, dbg["x1"].ap(), x1pub.ap())

            wt("gpsimd", sDV, v_slots)
            wt("gpsimd", gA0, vA0)
            for tt in range(TT16):
                def sct(e, tt=tt):
                    return e.indirect_dma_start(
                        tbl_in[:, :], IndirectOffsetOnAxis(st_i[:, tt:tt + 1], 0),
                        tokidx_s[:, tt:tt + 1], None)
                rec("gpsimd", sct, gSCT, 16)
            wt("gpsimd", gSCT, gSCT.n)
            v_ar_t = rec("gpsimd", lambda e: e.collective_compute(
                "AllReduce", ALU.add, replica_groups=[list(range(NCORE))],
                ins=[tbl_in.ap()], outs=[tbl_out.ap()]), sCC)

        if stop_after not in ("A", "B"):
            _stages = ["tbl", "g", "gf", "h", "o", "eo", None]
            def c_active(stage):
                return _stages.index(stage) <= _stages.index(c_sub) if c_sub else True
            stk_b2.close()
            stk_b1.close()
            G = sbuf(stk_c, "G", [128, 10 * 1024], BF16)
            Gf = sbuf(stk_c, "Gf", [128, DT8 * CAP], BF16)
            hT = sbuf(stk_c, "hT", [128, 32 * CAP], BF16)
            oT = sbuf(stk_c, "oT", [128, DT8 * CAP], BF16)
            EOs = sbuf(stk_c, "EOs", [128, 10 * 1024], BF16)
            w1b = [sbuf(stk_c, f"w1b{i}", [128, DT8 * 128], BF16) for i in range(2)]
            w2b = [sbuf(stk_c, f"w2b{i}", [128, 32 * 128], BF16) for i in range(2)]
            PSC = stk_c.enter_context(nc.psum_tensor("PSC", [128, 3072], F32))
            PSB = stk_c.enter_context(nc.psum_tensor("PSB", [128, 2048], BF16))

            w1r = w1o.ap().rearrange("(l j p) hh -> l p j hh", l=2, p=128)
            w2r = w2o.ap().rearrange("(l i p) dd -> l p i dd", l=2, p=128)

            wt("gpsimd", sCC, v_ar_t)
            wt("gpsimd", gA0, vA0)
            for k in range(20):
                def tg(e, k=k):
                    return e.indirect_dma_start(
                        tblown[:, k:k + 1], None, tbl_out[:, :],
                        IndirectOffsetOnAxis(slotsel_s[:, k:k + 1], 0))
                rec("gpsimd", tg, gTBL, 16)
            vTBL = gTBL.n

            gf_tr_last = {}
            v_ag_eo = {}
            gf_ev = {}
            ht_mm_last = {}
            ht_ac = {}
            ot_mm_last = {}
            ot_ev = {}
            eo_ev = {}
            v_eo_w = {}
            v_w1 = {}
            v_w2 = {}
            v_dbg_eo = {}
            for le in range(2):
                if not c_active("g"):
                    break
                wt("gpsimd", sCC, v_ag_x)
                wt("gpsimd", gTBL, vTBL)
                if le == 1:
                    wt("gpsimd", gGA, v_gd_prev)
                    wt("gpsimd", gXb, gf_tr_last[0])
                if le == 1 and c_sub == "g" and debug:
                    wt("gpsimd", gDbg, v_dbg_eo[0])
                for k in range(10):
                    def gg(e, le=le, k=k):
                        return e.indirect_dma_start(
                            G[:, k * 1024:(k + 1) * 1024], None, xall[:, :],
                            IndirectOffsetOnAxis(tblown[:, le * 10 + k:le * 10 + k + 1], 0))
                    rec("gpsimd", gg, gGA, 16)
                vGA = gGA.n
                if debug and c_sub == "g":
                    wt("sync", gGA, vGA)
                    v_dbg_eo[le] = dma("sync", gDbg,
                                       dbg["eo"].ap().rearrange("(l k p) dd -> l p k dd", l=2, p=128)[le],
                                       G[:, :].rearrange("p (k dd) -> p k dd", k=10))
                if not c_active("gf"):
                    continue
                v_gd = dma("gpsimd", gGA, gdram.ap().rearrange("(k p) dd -> p k dd", p=128),
                           G[:, :].rearrange("p (k dd) -> p k dd", k=10))
                v_gd_prev = v_gd
                wt("sync", gGA, v_gd)
                if le == 1:
                    wt("sync", sPE, ht_mm_last[(0, 31)])
                v_gfx = 0
                for j in range(DT8):
                    v_gfx = rec("sync", lambda e, j=j: e.dma_start(
                        out=Gf[:, j * CAP:(j + 1) * CAP],
                        in_=gdram[:, j * 128:(j + 1) * 128], transpose=True), gXb, 16)
                gf_tr_last[le] = v_gfx
                gf_last = v_gfx
                if not c_active("h"):
                    continue
                for i in range(2):
                    if le == 1:
                        wt("gpsimd", sPE, ht_mm_last[(0, 30 + i)])
                    v_w1[(le, i)] = dma("gpsimd", gWs[i % 2],
                                        w1b[i % 2][:, :].rearrange("p (j hh) -> p j hh", j=DT8),
                                        w1r[le, :, :, i * 128:(i + 1) * 128])
                for i in range(32):
                    buf = i % 2
                    wt("tensor", gWs[buf], v_w1[(le, i)])
                    if i == 0:
                        wt("tensor", gXb, gf_last)
                        if le == 1:
                            wt("tensor", sDV, ot_ev[(0, DT8 - 1)])
                    if i >= 2:
                        wt("tensor", sAC, ht_ac[(le, i - 2)])
                    for j in range(DT8):
                        for c3 in range(3):
                            w_ = 512 if c3 < 2 else 256
                            def mh(e, i=i, j=j, c3=c3, w_=w_, buf=buf):
                                return e.matmul(PSC[:, buf * 1536 + c3 * 512: buf * 1536 + c3 * 512 + w_],
                                                w1b[buf][:, j * 128:(j + 1) * 128],
                                                Gf[:, j * CAP + c3 * 512: j * CAP + c3 * 512 + w_],
                                                start=(j == 0), stop=(j == DT8 - 1),
                                                skip_group_check=True)
                            if j == DT8 - 1 and c3 == 2:
                                ht_mm_last[(le, i)] = rec("tensor", mh, sPE)
                            else:
                                rec("tensor", mh)
                    wt("scalar", sPE, ht_mm_last[(le, i)])
                    def rl(e, i=i, buf=buf):
                        return e.activation(hT[:, i * CAP:(i + 1) * CAP],
                                            PSC[:, buf * 1536:buf * 1536 + CAP], AF.Relu)
                    ht_ac[(le, i)] = rec("scalar", rl, sAC)
                    if i + 2 < 32:
                        wt("gpsimd", sPE, ht_mm_last[(le, i)])
                        v_w1[(le, i + 2)] = dma("gpsimd", gWs[buf],
                                                w1b[buf][:, :].rearrange("p (j hh) -> p j hh", j=DT8),
                                                w1r[le, :, :, (i + 2) * 128:(i + 3) * 128])
                if not c_active("o"):
                    continue
                for j in range(2):
                    if le == 1:
                        wt("gpsimd", sPE, ot_mm_last[(0, 6 + j)])
                    v_w2[(le, j)] = dma("gpsimd", gWs[j % 2],
                                        w2b[j % 2][:, :].rearrange("p (i hh) -> p i hh", i=32),
                                        w2r[le, :, :, j * 128:(j + 1) * 128])
                for j in range(DT8):
                    buf = j % 2
                    wt("tensor", gWs[buf], v_w2[(le, j)])
                    if j == 0:
                        wt("tensor", sAC, ht_ac[(le, 31)])
                    if j >= 2:
                        wt("tensor", sDV, ot_ev[(le, j - 2)])
                    for i in range(32):
                        for c3 in range(3):
                            w_ = 512 if c3 < 2 else 256
                            def mo(e, i=i, j=j, c3=c3, w_=w_, buf=buf):
                                return e.matmul(PSC[:, buf * 1536 + c3 * 512: buf * 1536 + c3 * 512 + w_],
                                                w2b[buf][:, i * 128:(i + 1) * 128],
                                                hT[:, i * CAP + c3 * 512: i * CAP + c3 * 512 + w_],
                                                start=(i == 0), stop=(i == 31),
                                                skip_group_check=True)
                            if i == 31 and c3 == 2:
                                ot_mm_last[(le, j)] = rec("tensor", mo, sPE)
                            else:
                                rec("tensor", mo)
                    wt("vector", sPE, ot_mm_last[(le, j)])
                    def oev(e, j=j, buf=buf):
                        return e.tensor_copy(oT[:, j * CAP:(j + 1) * CAP],
                                             PSC[:, buf * 1536:buf * 1536 + CAP])
                    ot_ev[(le, j)] = rec("vector", oev, sDV)
                    if j + 2 < DT8:
                        wt("gpsimd", sPE, ot_mm_last[(le, j)])
                        v_w2[(le, j + 2)] = dma("gpsimd", gWs[buf],
                                                w2b[buf][:, :].rearrange("p (i hh) -> p i hh", i=32),
                                                w2r[le, :, :, (j + 2) * 128:(j + 3) * 128])
                if not c_active("eo"):
                    continue
                wt("gpsimd", sDV, ot_ev[(le, DT8 - 1)])
                v_od = dma("gpsimd", gEO, odram.ap().rearrange("(j p) s -> p j s", p=128),
                           oT[:, :].rearrange("p (j s) -> p j s", j=DT8))
                wt("sync", gEO, gEO.n)
                v_eox = 0
                for k in range(10):
                    v_eox = rec("sync", lambda e, k=k: e.dma_start(
                        out=EOs[:, k * 1024:(k + 1) * 1024],
                        in_=odram[:, k * 128:(k + 1) * 128], transpose=True), gXb, 16)
                wt("gpsimd", gXb, v_eox)
                v_eo_w[le] = dma("gpsimd", gEO,
                                 eo_in[le].ap().rearrange("(k p) dd -> p k dd", p=128),
                                 EOs[:, :].rearrange("p (k dd) -> p k dd", k=10))
                wt("gpsimd", gEO, gEO.n)
                v_ag_eo[le] = rec("gpsimd", lambda e, le=le: e.collective_compute(
                    "AllGather", ALU.bypass, replica_groups=[list(range(NCORE))],
                    ins=[eo_in[le].ap()],
                    outs=[(eo_allE if le == 0 else eo_allO).ap()]), sCC)
                if debug:
                    wt("sync", gEO, v_eo_w[le])
                    v_dbg_eo[le] = dma("sync", gDbg,
                                       dbg["eo"].ap().rearrange("(l k p) dd -> l p k dd", l=2, p=128)[le],
                                       EOs[:, :].rearrange("p (k dd) -> p k dd", k=10))
            if c_sub is None and stop_after == "C":
                wt("gpsimd", sCC, v_ag_eo[1])

        if stop_after is None:
            stk_c.close()
            x1tm = sbuf(stk_d, "x1tm", [128, TT16 * 1024], BF16)
            moe = sbuf(stk_d, "moe", [128, TT16 * 1024], BF16)
            moeO = sbuf(stk_d, "moeO", [128, TT16 * 1024], BF16)
            x2t = [sbuf(stk_d, f"x2t{i}", [128, 1024], F32) for i in range(2)]
            yt = [sbuf(stk_d, f"yt{i}", [128, 1024], F32) for i in range(2)]
            dsc = sbuf(stk_d, "dsc", [128, 1024], F32)
            dsc2 = sbuf(stk_d, "dsc2", [128, 1024], BF16)
            cst = sbuf(stk_d, "cst", [128, 16], F32)

            wt("gpsimd", gPub, vPub)
            for tt in range(TT16):
                dma("gpsimd", gD, x1tm[:, tt * 1024:(tt + 1) * 1024], x1pub[tt * 128:(tt + 1) * 128, :])
            wt("gpsimd", sCC, v_ag_eo[1])
            for tt in range(TT16):
                def mgthE(e, tt=tt):
                    return e.indirect_dma_start(
                        moe[:, tt * 1024:(tt + 1) * 1024], None, eo_allE[:, :],
                        IndirectOffsetOnAxis(sg_i[:, tt:tt + 1], 0))
                rec("gpsimd", mgthE, gD, 16)
                def mgthO(e, tt=tt):
                    return e.indirect_dma_start(
                        moeO[:, tt * 1024:(tt + 1) * 1024], None, eo_allO[:, :],
                        IndirectOffsetOnAxis(sg_i[:, tt:tt + 1], 0))
                rec("gpsimd", mgthO, gD, 16)
            vD = gD.n

            if debug:
                wt("sync", gD, vD)
                dma("sync", gDbg, dbg["moeE"].ap(), moe[:, :])
                dma("sync", gDbg, dbg["moeO"].ap(), moeO[:, :])
                wt("sync", sDV, v_slots)
                dma("sync", gDbg, dbg["sg"].ap(), sg_i[:, :])
            vst = {}
            for tt in range(TT16):
                buf = tt % 2
                if tt == 0:
                    wt("vector", gD, vD)
                if tt >= 2:
                    wt("vector", gY[buf], vst[tt - 2])
                c0a = rec("vector", lambda e, tt=tt: e.tensor_tensor(
                    dsc[:, :], moeO[:, tt * 1024:(tt + 1) * 1024],
                    moe[:, tt * 1024:(tt + 1) * 1024], ALU.subtract), sDV)
                wt("vector", sDV, c0a)
                c0b = rec("vector", lambda e, tt=tt: e.scalar_tensor_tensor(
                    dsc[:, :], dsc[:, :], epar_s[:, tt:tt + 1],
                    moe[:, tt * 1024:(tt + 1) * 1024], ALU.mult, ALU.add), sDV)
                wt("vector", sDV, c0b)
                c1 = rec("vector", lambda e, tt=tt, buf=buf: e.tensor_scalar(
                    x2t[buf][:, :], dsc[:, :],
                    peff_s[:, tt:tt + 1], None, ALU.mult), sDV)
                wt("vector", sDV, c1)
                c2 = rec("vector", lambda e, tt=tt, buf=buf: e.tensor_tensor(
                    x2t[buf][:, :], x2t[buf][:, :], x1tm[:, tt * 1024:(tt + 1) * 1024],
                    ALU.add), sDV)
                wt("vector", sDV, c2)
                c3 = rec("vector", lambda e, buf=buf: e.reduce_sum(cst[:, 0:1], x2t[buf][:, :],
                                                                   axis=AX), sDV)
                wt("scalar", sDV, c2)
                c4 = rec("scalar", lambda e, buf=buf: e.activation(dsc2[:, :], x2t[buf][:, :],
                                                                   AF.Square, accum_out=cst[:, 1:2]), sAC)
                wt("vector", sAC, c4)
                wt("vector", sDV, c3)
                c5 = rec("vector", lambda e: e.tensor_scalar_mul(cst[:, 2:3], cst[:, 0:1], 1.0 / D), sDV)
                wt("vector", sDV, c5)
                c6 = rec("vector", lambda e: e.tensor_scalar_mul(cst[:, 3:4], cst[:, 1:2], 1.0 / D), sDV)
                wt("vector", sDV, c6)
                c7 = rec("vector", lambda e: e.tensor_tensor(cst[:, 4:5], cst[:, 2:3], cst[:, 2:3],
                                                             ALU.mult), sDV)
                wt("vector", sDV, c7)
                c8 = rec("vector", lambda e: e.tensor_tensor(cst[:, 5:6], cst[:, 3:4], cst[:, 4:5],
                                                             ALU.subtract), sDV)
                wt("scalar", sDV, c8)
                c9 = rec("scalar", lambda e: e.activation(cst[:, 6:7], cst[:, 5:6], AF.Sqrt,
                                                          bias=epsB[:, :], scale=1.0), sAC)
                wt("vector", sAC, c9)
                c10 = rec("vector", lambda e: e.reciprocal(cst[:, 7:8], cst[:, 6:7]), sDV)
                wt("vector", sDV, c10)
                c11 = rec("vector", lambda e: e.scalar_tensor_tensor(cst[:, 8:9], cst[:, 2:3], -1.0,
                                                                     cst[:, 7:8], ALU.mult, ALU.mult), sDV)
                wt("vector", sDV, c11)
                c12 = rec("vector", lambda e, buf=buf: e.tensor_scalar(
                    yt[buf][:, :], x2t[buf][:, :], cst[:, 7:8], cst[:, 8:9],
                    ALU.mult, ALU.add), sDV)
                wt("sync", sDV, c12)
                vst[tt] = dma("sync", gY[buf], y_out[tt * 128:(tt + 1) * 128, :], yt[buf][:, :])
            wt("sync", gY[0], gY[0].n)
            wt("sync", gY[1], gY[1].n)
            if debug:
                wt("sync", gDbg, gDbg.n)
            stk_d.close()

        if stop_after is not None:
            if debug:
                wt("sync", gDbg, gDbg.n)
            if stop_after == "A":
                stk_a.close()
                stk_b1.close()
            elif stop_after == "B":
                stk_b2.close()
                stk_b1.close()
            elif stop_after == "C":
                stk_c.close()
        for s, v in waits_record:
            assert v <= s.n, f"wait on {s.name} for {v} but final count {s.n}"

        with nc.Block() as block:
            @block.tensor
            def _(eng):
                for f in ops["tensor"]:
                    f(eng)

            @block.vector
            def _(eng):
                for f in ops["vector"]:
                    f(eng)

            @block.scalar
            def _(eng):
                for f in ops["scalar"]:
                    f(eng)

            @block.gpsimd
            def _(eng):
                for f in ops["gpsimd"]:
                    f(eng)

            @block.sync
            def _(eng):
                for f in ops["sync"]:
                    f(eng)

    return nc


_CACHE = {}


def _consts():
    identb = np.eye(128, dtype=ml_dtypes.bfloat16)
    identf = np.eye(128, dtype=np.float32)
    triu = (np.arange(128)[:, None] < np.arange(128)[None, :]).astype(np.float32)
    trib = np.zeros((16, 16 * 128), dtype=np.float32)
    for tt in range(16):
        trib[:, tt * 128:(tt + 1) * 128] = (np.arange(16)[:, None] < tt).astype(np.float32)
    return identb, identf, triu, trib


def make_in_maps(z, wqkv, gw, w1, w2):
    identb, identf, triu, trib = _consts()
    wqkvT = np.ascontiguousarray(wqkv.T)
    gwT = np.ascontiguousarray(gw.T)
    zT = np.ascontiguousarray(np.transpose(z, (0, 2, 1)))
    in_maps = []
    for c in range(NCORE):
        b, qoff = c // 2, (c % 2) * TLOC
        zTc = np.roll(zT[b], -qoff, axis=1) if qoff else zT[b]
        rmb = np.zeros((8, 128), dtype=np.float32)
        rmb[0:c, :] = 1.0
        tok = (c * TLOC + np.arange(TT16)[None, :] * 128 +
               np.arange(128)[:, None]).astype(np.int32)
        ssel = (2 * c * CAP + np.arange(20)[None, :] * 128 +
                np.arange(128)[:, None]).astype(np.int32)
        parT = np.tile((np.arange(16) % 2).astype(np.float32), (128, 16))
        ehalfT = np.tile((np.arange(16) // 2).astype(np.float32), (128, 16))
        in_maps.append({
            "zT": np.ascontiguousarray(zTc),
            "wqkvT": wqkvT,
            "gwT": gwT,
            "w1o": np.ascontiguousarray(w1[2 * c:2 * c + 2].reshape(2 * D, HID)),
            "w2o": np.ascontiguousarray(w2[2 * c:2 * c + 2].reshape(2 * HID, D)),
            "identbd": identb,
            "identfd": identf,
            "triud": triu,
            "tribd": trib,
            "rmbd": rmb,
            "tokidxd": tok,
            "slotseld": ssel,
            "partd": parT,
            "ehalfd": ehalfT,
        })
    return in_maps


def kernel(L0_attn_z, L0_attn_W_qkv, L0_ffn_gate_w, L0_ffn_w1, L0_ffn_w2):
    z = np.asarray(L0_attn_z, dtype=np.float32)
    wqkv = np.asarray(L0_attn_W_qkv, dtype=np.float32)
    gw = np.asarray(L0_ffn_gate_w, dtype=np.float32)
    w1 = np.asarray(L0_ffn_w1, dtype=np.float32)
    w2 = np.asarray(L0_ffn_w2, dtype=np.float32)

    if "nc" not in _CACHE:
        _CACHE["nc"] = build_program(debug=False)
    nc = _CACHE["nc"]
    in_maps = make_in_maps(z, wqkv, gw, w1, w2)
    res = run_bass_kernel_spmd(nc, in_maps, core_ids=list(range(NCORE)))
    y = np.concatenate([np.asarray(res.results[c]["y"]) for c in range(NCORE)], axis=0)
    return y.reshape(4, SEQ, D)


# version 11
# speedup vs baseline: 1.0771x; 1.0771x over previous
import numpy as np
import ml_dtypes
import contextlib
import concourse.bass as bass
import concourse.mybir as mybir
from concourse.bass_core_rust import IndirectOffsetOnAxis
from concourse.bass_utils import run_bass_kernel_spmd

F32 = mybir.dt.float32
BF16 = mybir.dt.bfloat16
I32 = mybir.dt.int32
AX = mybir.AxisListType.X
ALU = mybir.AluOpType
AF = mybir.ActivationFunctionType

NCORE = 8
SEQ, D, NH, DH, E, CAP, HID = 4096, 1024, 16, 64, 16, 1280, 4096
TLOC = 2048
NPAIR = 8
QB = 512
NQB = TLOC // QB
KCH = SEQ // 128
TT16 = TLOC // 128
DT8 = D // 128
SLOTS = 2 * CAP
TBLN = 20608
TRASH = 20480
SOFT_B = 16.0
NCHUNK = NQB * KCH


class Sem:
    def __init__(self, handle, name):
        self.h = handle
        self.name = name
        self.n = 0


def build_program(debug=False, stop_after=None, c_sub=None):
    nc = bass.Bass("TRN2", target_bir_lowering=False, debug=False)

    zT = nc.dram_tensor("zT", [D, SEQ], F32, kind="ExternalInput")
    wqkvT = nc.dram_tensor("wqkvT", [D, 3 * D], F32, kind="ExternalInput")
    gwT = nc.dram_tensor("gwT", [D, E], F32, kind="ExternalInput")
    w1o = nc.dram_tensor("w1o", [2 * D, HID], F32, kind="ExternalInput")
    w2o = nc.dram_tensor("w2o", [2 * HID, D], F32, kind="ExternalInput")
    identb_d = nc.dram_tensor("identbd", [128, 128], BF16, kind="ExternalInput")
    identf_d = nc.dram_tensor("identfd", [128, 128], F32, kind="ExternalInput")
    triu_d = nc.dram_tensor("triud", [128, 128], F32, kind="ExternalInput")
    trib_d = nc.dram_tensor("tribd", [16, 16 * 128], F32, kind="ExternalInput")
    rmb_d = nc.dram_tensor("rmbd", [8, 128], F32, kind="ExternalInput")
    tokidx_d = nc.dram_tensor("tokidxd", [128, TT16], I32, kind="ExternalInput")
    slotsel_d = nc.dram_tensor("slotseld", [128, 20], I32, kind="ExternalInput")
    part_d = nc.dram_tensor("partd", [128, TT16 * E], F32, kind="ExternalInput")
    ehalf_d = nc.dram_tensor("ehalfd", [128, TT16 * E], F32, kind="ExternalInput")

    y_out = nc.dram_tensor("y", [TLOC, D], F32, kind="ExternalOutput")

    sums_dram = nc.dram_tensor("sums_dram", [2 * NPAIR, TLOC], F32, kind="Internal")
    sums_r = nc.dram_tensor("sums_r", [2 * NPAIR, TLOC], F32, kind="Internal")
    x1fm_dram = nc.dram_tensor("x1fm_dram", [D, TLOC], BF16, kind="Internal")
    x1pub = nc.dram_tensor("x1pub", [TLOC, D], BF16, kind="Internal")
    xall = nc.dram_tensor("xall", [NCORE * TLOC, D], BF16, kind="Internal", addr_space="Shared")
    cnt_in = nc.dram_tensor("cnt_in", [1, E], F32, kind="Internal")
    cnt_out = nc.dram_tensor("cnt_out", [NCORE, E], F32, kind="Internal")
    tbl_in = nc.dram_tensor("tbl_in", [TBLN, 1], I32, kind="Internal")
    tbl_out = nc.dram_tensor("tbl_out", [TBLN, 1], I32, kind="Internal")
    eo_in = [nc.dram_tensor("eo_inE", [CAP, D], BF16, kind="Internal"),
             nc.dram_tensor("eo_inO", [CAP, D], BF16, kind="Internal")]
    eo_allE = nc.dram_tensor("eo_allE", [NCORE * CAP, D], BF16, kind="Internal", addr_space="Shared")
    eo_allO = nc.dram_tensor("eo_allO", [NCORE * CAP, D], BF16, kind="Internal", addr_space="Shared")
    gdram = nc.dram_tensor("gdram", [CAP, D], BF16, kind="Internal")
    odram = nc.dram_tensor("odram", [D, CAP], BF16, kind="Internal")

    dbg = {}
    if debug:
        dbg["attn"] = nc.dram_tensor("dbg_attn", [128, DT8 * TLOC], BF16, kind="ExternalOutput")
        dbg["sums"] = nc.dram_tensor("dbg_sums", [2 * NPAIR, TLOC], F32, kind="ExternalOutput")
        dbg["x1"] = nc.dram_tensor("dbg_x1", [TLOC, D], BF16, kind="ExternalOutput")
        dbg["logits"] = nc.dram_tensor("dbg_logits", [128, TT16 * E], F32, kind="ExternalOutput")
        dbg["route"] = nc.dram_tensor("dbg_route", [128, 5 * TT16], F32, kind="ExternalOutput")
        dbg["sg"] = nc.dram_tensor("dbg_sg", [128, TT16], mybir.dt.int32, kind="ExternalOutput")
        dbg["epar"] = nc.dram_tensor("dbg_epar", [128, TT16], F32, kind="ExternalOutput")
        dbg["moeE"] = nc.dram_tensor("dbg_moeE", [128, TT16 * 1024], BF16, kind="ExternalOutput")
        dbg["moeO"] = nc.dram_tensor("dbg_moeO", [128, TT16 * 1024], BF16, kind="ExternalOutput")
        dbg["eo"] = nc.dram_tensor("dbg_eo", [SLOTS, D], BF16, kind="ExternalOutput")

    ops = {k: [] for k in ("tensor", "vector", "scalar", "gpsimd", "sync")}
    waits_record = []

    P = contextlib.ExitStack()
    stk_b1 = contextlib.ExitStack()
    stk_a = contextlib.ExitStack()
    stk_b2 = contextlib.ExitStack()
    stk_c = contextlib.ExitStack()
    stk_d = contextlib.ExitStack()

    def sbuf(stk, name, shape, dt):
        return stk.enter_context(nc.sbuf_tensor(name, shape, dt))

    with P:
        def mksem(name):
            return Sem(P.enter_context(nc.semaphore(name)), name)

        sPE = mksem("sPE")
        sDV = mksem("sDV")
        sAC = mksem("sAC")
        sSc = mksem("sSc")
        sEx = mksem("sEx")
        sAV = mksem("sAV")
        sCC = mksem("sCC")
        gA0 = mksem("gA0")
        gWq = [mksem("gWq0"), mksem("gWq1")]
        gWv2 = mksem("gWv2")
        gST1 = mksem("gST1")
        gLdB = mksem("gLdB")
        gRow = mksem("gRow")
        gFM = mksem("gFM")
        gXb = mksem("gXb")
        gXE = mksem("gXE")
        gPub = mksem("gPub")
        gCP = mksem("gCP")
        gCNT = mksem("gCNT")
        gSCT = mksem("gSCT")
        gTBL = mksem("gTBL")
        gGA = mksem("gGA")
        gWs = [mksem("gWs0"), mksem("gWs1")]
        gEO = mksem("gEO")
        gD = mksem("gD")
        gY = [mksem("gY0"), mksem("gY1")]
        gDbg = mksem("gDbg")

        def rec(eng, fn, s=None, k=1):
            if s is None:
                ops[eng].append(fn)
                return None
            s.n += k
            v = s.n
            ops[eng].append(lambda e, fn=fn, h=s.h, k=k: fn(e).then_inc(h, k))
            return v

        def wt(eng, s, v):
            assert v is not None
            waits_record.append((s, v))
            ops[eng].append(lambda e, h=s.h, v=v: e.wait_ge(h, v))

        def dma(eng, s, out, in_, **kw):
            return rec(eng, lambda e, o=out, i=in_, kw=kw: e.dma_start(out=o, in_=i, **kw),
                       s, 16)

        onesb = sbuf(P, "onesb", [128, 1], BF16)
        onesf = sbuf(P, "onesf", [128, 1], F32)
        ones1b = sbuf(P, "ones1b", [1, 128], BF16)
        ones1f = sbuf(P, "ones1f", [1, 128], F32)
        identb = sbuf(P, "identb", [128, 128], BF16)
        identf = sbuf(P, "identf", [128, 128], F32)
        tokidx_s = sbuf(P, "tokidx_s", [128, TT16], I32)
        slotsel_s = sbuf(P, "slotsel_s", [128, 20], I32)
        tblown = sbuf(P, "tblown", [128, 20], I32)
        peff_s = sbuf(P, "peff_s", [128, TT16], F32)
        sg_i = sbuf(P, "sg_i", [128, TT16], I32)
        st_i = sbuf(P, "st_i", [128, TT16], I32)
        eidx_s = sbuf(P, "eidx_s", [128, TT16], F32)
        psel_s = sbuf(P, "psel_s", [128, TT16], F32)
        kept_s = sbuf(P, "kept_s", [128, TT16], F32)
        prob_s = sbuf(P, "prob_s", [128, TT16], F32)
        slotf_s = sbuf(P, "slotf_s", [128, TT16], F32)
        epar_s = sbuf(P, "epar_s", [128, TT16], F32)
        zeri = sbuf(P, "zeri", [128, 161], I32)
        negB = sbuf(P, "negB", [128, 1], F32)
        epsB = sbuf(P, "epsB", [128, 1], F32)
        iota16 = sbuf(P, "iota16", [128, E], F32)
        iotmb = sbuf(P, "iotmb", [128, E], F32)
        iota_t = sbuf(P, "iota_t", [128, TT16 * E], F32)
        iotmb_t = sbuf(P, "iotmb_t", [128, TT16 * E], F32)
        parT_s = sbuf(P, "parT_s", [128, TT16 * E], F32)
        ehalfT_s = sbuf(P, "ehalfT_s", [128, TT16 * E], F32)

        attn_f = sbuf(stk_b1, "attn_f", [128, DT8 * TLOC], BF16)
        sums_sb = sbuf(stk_b1, "sums_sb", [33, TLOC], F32)

        Zf = sbuf(stk_a, "Zf", [128, DT8 * SEQ], BF16)
        Vh = sbuf(stk_a, "Vh", [128, KCH * 512], BF16)
        QT = [sbuf(stk_a, f"QT{i}", [128, TLOC], BF16) for i in range(2)]
        KT = [sbuf(stk_a, f"KT{i}", [128, SEQ], BF16) for i in range(2)]
        Wq = [sbuf(stk_a, f"Wq{i}", [128, DT8 * 128], BF16) for i in range(2)]
        Wk = [sbuf(stk_a, f"Wk{i}", [128, DT8 * 128], BF16) for i in range(2)]
        Wv = sbuf(stk_a, "Wv", [128, DT8 * 512], BF16)
        PTb = [sbuf(stk_a, f"PTb{i}", [128, 1024], BF16) for i in range(2)]
        PS = stk_a.enter_context(nc.psum_tensor("PS", [128, 4096], F32))

        dma("gpsimd", gA0, identb[:, :], identb_d[:, :])
        dma("gpsimd", gA0, identf[:, :], identf_d[:, :])
        dma("gpsimd", gA0, tokidx_s[:, :], tokidx_d[:, :])
        dma("gpsimd", gA0, slotsel_s[:, :], slotsel_d[:, :])
        dma("gpsimd", gA0, parT_s[:, :], part_d[:, :])
        dma("gpsimd", gA0, ehalfT_s[:, :], ehalf_d[:, :])
        for j in range(DT8):
            dma("gpsimd", gA0, Zf[:, j * SEQ:(j + 1) * SEQ], zT[j * 128:(j + 1) * 128, :])

        rec("vector", lambda e: e.memset(onesb[:, :], 1.0))
        rec("vector", lambda e: e.memset(onesf[:, :], 1.0))
        rec("vector", lambda e: e.memset(ones1b[:, :], 1.0))
        rec("vector", lambda e: e.memset(ones1f[:, :], 1.0))
        rec("vector", lambda e: e.memset(negB[:, :], -SOFT_B))
        rec("vector", lambda e: e.memset(epsB[:, :], 1e-5))
        v_zeri = rec("vector", lambda e: e.memset(zeri[:, :], 0), sDV)
        v_iota = rec("gpsimd", lambda e: e.iota(iota16[:, :], [[1, E]], channel_multiplier=0,
                                               allow_small_or_imprecise_dtypes=True), gA0, 16)
        wt("gpsimd", sDV, v_zeri)
        dma("gpsimd", gA0, tbl_in.ap().rearrange("(p a) x -> p (a x)", p=128), zeri[:, :])

        wq_p = wqkvT.ap().rearrange("(j p) (h q x) -> p j h q x", p=128, h=NH, x=DH)

        def load_wv(half, s):
            v = None
            for j in range(DT8):
                v = dma("gpsimd", s,
                        Wv[:, j * 512:(j + 1) * 512].rearrange("p (h x) -> p h x", h=8),
                        wq_p[:, j, half * 8:half * 8 + 8, 2, :])
            return v

        v_wv1 = load_wv(0, gA0)
        vA0_full = gA0.n

        v_iot2 = rec("gpsimd", lambda e: e.iota(iota_t[:, :], [[0, TT16], [1, E]],
                                                channel_multiplier=0,
                                                allow_small_or_imprecise_dtypes=True), gA0, 16)
        vA0 = gA0.n
        if debug:
            wt("sync", gA0, vA0)
            dma("sync", gDbg, dbg["epar"].ap(), parT_s[:, 16:32])
        wt("vector", gA0, vA0)
        rec("vector", lambda e: e.tensor_scalar_add(iotmb[:, :], iota16[:, :], -100.0))
        rec("vector", lambda e: e.tensor_scalar_add(iotmb_t[:, :], iota_t[:, :], -100.0))

        def load_wqk(p):
            s = gWq[p % 2]
            for j in range(DT8):
                dma("gpsimd", s,
                    Wq[p % 2][:, j * 128:(j + 1) * 128].rearrange("p (h x) -> p h x", h=2),
                    wq_p[:, j, 2 * p:2 * p + 2, 0, :])
                v = dma("gpsimd", s,
                        Wk[p % 2][:, j * 128:(j + 1) * 128].rearrange("p (h x) -> p h x", h=2),
                        wq_p[:, j, 2 * p:2 * p + 2, 1, :])
            return v

        def emit_vpass(wait_list, vev):
            vmm = {}
            for tt in range(KCH):
                buf = tt % 2
                if tt == 0:
                    for s_, v_ in wait_list:
                        wt("tensor", s_, v_)
                if tt >= 2:
                    wt("tensor", sDV, vev[tt - 2])
                for j in range(DT8):
                    def mm(e, j=j, tt=tt, buf=buf):
                        return e.matmul(PS[:, 3072 + buf * 512:3072 + (buf + 1) * 512],
                                        Zf[:, j * SEQ + tt * 128: j * SEQ + tt * 128 + 128],
                                        Wv[:, j * 512:(j + 1) * 512],
                                        start=(j == 0), stop=(j == DT8 - 1),
                                        skip_group_check=True)
                    if j == DT8 - 1:
                        vmm[tt] = rec("tensor", mm, sPE)
                    else:
                        rec("tensor", mm)
                wt("vector", sPE, vmm[tt])
                def ev(e, tt=tt, buf=buf):
                    return e.tensor_copy(Vh[:, tt * 512:(tt + 1) * 512],
                                         PS[:, 3072 + buf * 512:3072 + (buf + 1) * 512])
                vev[tt] = rec("vector", ev, sDV)
            return vmm

        vev1 = {}
        emit_vpass([(gA0, vA0)], vev1)

        v_w = {0: load_wqk(0)}
        prod_mm, prod_ev, avev, sums_done = {}, {}, {}, {}
        score_v, exp_v, av_v = {}, {}, {}
        vev2 = {}

        for p in range(NPAIR):
            par = p % 2
            half = p // 4
            vev_half = vev1 if half == 0 else vev2

            if p == 0:
                wt("tensor", sDV, vev1[KCH - 1])
            else:
                wt("tensor", sDV, avev[(p - 1, NQB - 1)])
            wt("tensor", gWq[par], v_w[p])
            if p >= 2:
                wt("tensor", sSc, (p - 1) * NCHUNK)
            for g in range(12):
                buf = g % 2
                isq = g < 4
                tc = g if isq else g - 4
                if p == 4 and g < 2:
                    wt("tensor", sDV, vev2[KCH - 2 + g])
                elif g >= 2:
                    wt("tensor", sDV, prod_ev[(p, g - 2)])
                elif p > 0:
                    wt("tensor", sDV, prod_ev[(p - 1, 10 + g)])
                for j in range(DT8):
                    def mm(e, par=par, isq=isq, tc=tc, j=j, buf=buf):
                        W = Wq[par] if isq else Wk[par]
                        return e.matmul(PS[:, 3072 + buf * 512:3072 + (buf + 1) * 512],
                                        W[:, j * 128:(j + 1) * 128],
                                        Zf[:, j * SEQ + tc * 512: j * SEQ + tc * 512 + 512],
                                        start=(j == 0), stop=(j == DT8 - 1),
                                        skip_group_check=True)
                    if j == DT8 - 1:
                        prod_mm[(p, g)] = rec("tensor", mm, sPE)
                    else:
                        rec("tensor", mm)
                wt("vector", sPE, prod_mm[(p, g)])
                def ev(e, par=par, isq=isq, tc=tc, buf=buf):
                    dst = QT[par] if isq else KT[par]
                    return e.tensor_copy(dst[:, tc * 512:(tc + 1) * 512],
                                         PS[:, 3072 + buf * 512:3072 + (buf + 1) * 512])
                prod_ev[(p, g)] = rec("vector", ev, sDV)
            if p + 1 < NPAIR:
                if p >= 1:
                    wt("gpsimd", sPE, prod_mm[(p - 1, 11)])
                v_w[p + 1] = load_wqk(p + 1)

            v_prod = prod_ev[(p, 11)]
            hhA = (2 * p) % 8
            hhB = hhA + 1

            def emit_scores(nl):
                qb, jj = nl // KCH, nl % KCH
                N = p * NCHUNK + nl
                spar = N % 2
                soff = (N % 3) * 1024
                if nl == 0:
                    wt("tensor", sDV, v_prod)
                if N >= 3:
                    wt("tensor", sEx, exp_v[N - 3])
                def mmA(e, par=par, soff=soff, jj=jj, qb=qb):
                    return e.matmul(PS[:, soff:soff + 512],
                                    KT[par][0:64, jj * 128:(jj + 1) * 128],
                                    QT[par][0:64, qb * QB:(qb + 1) * QB],
                                    start=True, stop=True, tile_position=(0, 0),
                                    skip_group_check=True)
                rec("tensor", mmA)
                def mmB(e, par=par, soff=soff, jj=jj, qb=qb):
                    return e.matmul(PS[:, soff + 512:soff + 1024],
                                    KT[par][64:128, jj * 128:(jj + 1) * 128],
                                    QT[par][64:128, qb * QB:(qb + 1) * QB],
                                    start=True, stop=True, tile_position=(64, 0),
                                    skip_group_check=True)
                score_v[N] = rec("tensor", mmB, sSc)
                wt("scalar", sSc, score_v[N])
                if N >= 2:
                    wt("scalar", sAV, av_v[N - 2])
                def ex(e, spar=spar, soff=soff):
                    return e.activation(PTb[spar][:, :], PS[:, soff:soff + 1024],
                                        AF.Exp, bias=negB[:, :], scale=0.125)
                exp_v[N] = rec("scalar", ex, sEx)

            def emit_av(nl):
                qb, jj = nl // KCH, nl % KCH
                N = p * NCHUNK + nl
                spar = N % 2
                wt("tensor", sEx, exp_v[N])
                if jj == 0:
                    if qb > 0:
                        wt("tensor", sDV, avev[(p, qb - 1)])
                    elif p > 0:
                        wt("tensor", sDV, avev[(p - 1, NQB - 1)])
                    wt("tensor", sDV, vev_half[KCH - 1])
                st_ = (jj == 0)
                sp_ = (jj == KCH - 1)
                def av1(e, jj=jj, spar=spar, st_=st_, sp_=sp_, hh=hhA):
                    return e.matmul(PS[0:64, 3072:3584],
                                    Vh[:, jj * 512 + hh * 64: jj * 512 + hh * 64 + 64],
                                    PTb[spar][:, 0:512], start=st_, stop=sp_,
                                    tile_position=(0, 0), skip_group_check=True)
                def av2(e, jj=jj, spar=spar, st_=st_, sp_=sp_, hh=hhB):
                    return e.matmul(PS[64:128, 3072:3584],
                                    Vh[:, jj * 512 + hh * 64: jj * 512 + hh * 64 + 64],
                                    PTb[spar][:, 512:1024], start=st_, stop=sp_,
                                    tile_position=(0, 64), skip_group_check=True)
                def sm1(e, spar=spar, st_=st_, sp_=sp_):
                    return e.matmul(PS[0:1, 3584:4096], onesb[:, :],
                                    PTb[spar][:, 0:512], start=st_, stop=sp_,
                                    tile_position=(0, 0), skip_group_check=True)
                def sm2(e, spar=spar, st_=st_, sp_=sp_):
                    return e.matmul(PS[32:33, 3584:4096], onesb[:, :],
                                    PTb[spar][:, 512:1024], start=st_, stop=sp_,
                                    tile_position=(0, 32), skip_group_check=True)
                rec("tensor", av1)
                rec("tensor", av2)
                rec("tensor", sm1)
                av_v[N] = rec("tensor", sm2, sAV)
                if jj == KCH - 1:
                    emit_qb_evac(qb)

            def emit_qb_evac(qb):
                wt("vector", sAV, av_v[p * NCHUNK + qb * KCH + KCH - 1])
                if p > 0 and qb == 0:
                    wt("vector", gST1, sums_done[p - 1])
                cols = p * TLOC + qb * QB
                rec("vector", lambda e, cols=cols: e.tensor_copy(
                    attn_f[0:64, cols:cols + QB], PS[0:64, 3072:3584]))
                rec("vector", lambda e, cols=cols: e.tensor_copy(
                    attn_f[64:128, cols:cols + QB], PS[64:128, 3072:3584]))
                rec("vector", lambda e, qb=qb: e.tensor_copy(
                    sums_sb[0:1, qb * QB:(qb + 1) * QB], PS[0:1, 3584:4096]))
                avev[(p, qb)] = rec("vector", lambda e, qb=qb: e.tensor_copy(
                    sums_sb[32:33, qb * QB:(qb + 1) * QB], PS[32:33, 3584:4096]), sDV)

            for nl in range(NCHUNK):
                emit_scores(nl)
                if nl >= 1:
                    emit_av(nl - 1)
            emit_av(NCHUNK - 1)
            wt("sync", sDV, avev[(p, NQB - 1)])
            dma("sync", gST1, sums_dram[2 * p:2 * p + 1, :], sums_sb[0:1, 0:TLOC])
            sums_done[p] = dma("sync", gST1, sums_dram[2 * p + 1:2 * p + 2, :],
                               sums_sb[32:33, 0:TLOC])
            if p == 3:
                v_wv2 = load_wv(1, gWv2)
                emit_vpass([(gWv2, v_wv2), (sDV, avev[(3, NQB - 1)]),
                            (sDV, prod_ev[(3, 11)])], vev2)

        v_attn_done = avev[(NPAIR - 1, NQB - 1)]
        v_dbg_attn = None
        if debug:
            wt("sync", sDV, v_attn_done)
            v_dbg_attn = dma("sync", gDbg, dbg["attn"].ap(), attn_f[:, :])
            wt("sync", gST1, sums_done[NPAIR - 1])
            dma("sync", gDbg, dbg["sums"].ap(), sums_dram.ap())

        if stop_after != "A":
            stk_a.close()
            zz = sbuf(stk_b2, "zz", [128, DT8 * TLOC], F32)
            scrA = sbuf(stk_b2, "scrA", [128, 2048], F32)
            scrB = sbuf(stk_b2, "scrB", [128, 1024], F32)
            rowA = sbuf(stk_b2, "rowA", [1, 2048], F32)
            rowB = sbuf(stk_b2, "rowB", [1, 2048], F32)
            rowC = sbuf(stk_b2, "rowC", [1, 2048], F32)
            rowD = sbuf(stk_b2, "rowD", [1, 2048], F32)
            srec = sbuf(stk_b2, "srec", [128, 2 * NPAIR * 16], F32)
            browb = sbuf(stk_b2, "browb", [1, 4096], BF16)
            mx16 = sbuf(stk_b2, "mx16", [128, TT16], F32)
            sm16 = sbuf(stk_b2, "sm16", [128, TT16], F32)
            df16 = sbuf(stk_b2, "df16", [128, TT16 * E], F32)
            ex16 = sbuf(stk_b2, "ex16", [128, TT16 * E], F32)
            eq16 = sbuf(stk_b2, "eq16", [128, TT16 * E], F32)

            gwT_s = sbuf(stk_b2, "gwT_s", [128, DT8 * E], F32)
            lgs = sbuf(stk_b2, "lgs", [16, TLOC], F32)
            logit_t = sbuf(stk_b2, "logit_t", [128, TT16 * E], F32)
            triu_s = sbuf(stk_b2, "triu_s", [128, 128], F32)
            trib_s = sbuf(stk_b2, "trib_s", [16, 16 * 128], F32)
            rmb_s = sbuf(stk_b2, "rmb_s", [8, 128], F32)
            cntag = sbuf(stk_b2, "cntag", [8, E], F32)
            cnt2 = sbuf(stk_b2, "cnt2", [16, E], F32)
            cntrow = sbuf(stk_b2, "cntrow", [1, TT16 * E], F32)
            oh_all = sbuf(stk_b2, "oh_all", [128, TT16 * E], F32)
            cum_s = sbuf(stk_b2, "cum_s", [128, TT16 * E], F32)
            pos_s = sbuf(stk_b2, "pos_s", [128, TT16 * E], F32)
            ctot_s = sbuf(stk_b2, "ctot_s", [1, E], F32)
            sexp = sbuf(stk_b2, "sexp", [128, E], F32)
            x1t_s = [sbuf(stk_b2, f"x1t_s{i}", [128, 1024], BF16) for i in range(2)]
            PS2 = stk_b2.enter_context(nc.psum_tensor("PS2", [128, 4096], F32))

            dma("sync", gLdB, gwT_s[:, :].rearrange("p (j e) -> p j e", j=DT8),
                gwT.ap().rearrange("(j p) e -> p j e", p=128))
            dma("sync", gLdB, triu_s[:, :], triu_d[:, :])
            dma("sync", gLdB, trib_s[:, :], trib_d[:, :])
            dma("sync", gLdB, rmb_s[:, :], rmb_d[:, :])
            for j in range(DT8):
                dma("sync", gLdB, zz[:, j * TLOC:(j + 1) * TLOC],
                    zT[j * 128:(j + 1) * 128, 0:TLOC])
            vLdB = gLdB.n

            wt("sync", gST1, sums_done[NPAIR - 1])
            v_sl = dma("sync", gRow, srec[:, :].rearrange("p (h c) -> p h c", h=16),
                       sums_dram.ap().rearrange("h (p c) -> p h c", c=16))
            wt("vector", gRow, v_sl)
            v_r = rec("vector", lambda e: e.reciprocal(srec[:, :], srec[:, :]), sDV)
            wt("sync", sDV, v_r)
            v_sr = dma("sync", gRow, sums_r.ap().rearrange("h (p c) -> p h c", c=16),
                       srec[:, :].rearrange("p (h c) -> p h c", h=16))
            v_scaled = None
            v_bc_last = {}
            for p2 in range(NPAIR):
                if p2 > 0:
                    wt("gpsimd", sPE, v_bc_last[p2 - 1])
                else:
                    wt("gpsimd", gRow, v_sr)
                v_ld = dma("gpsimd", gCNT, browb[0:1, :].rearrange("o (h t) -> o h t", h=2),
                           sums_r[2 * p2:2 * p2 + 2, :])
                wt("tensor", gCNT, v_ld)
                for hh2 in range(2):
                    par2 = hh2
                    if p2 > 0:
                        wt("tensor", sDV, v_sc_prev[hh2])
                    vb = None
                    for c4 in range(4):
                        def bc(e, hh2=hh2, c4=c4, par2=par2):
                            return e.matmul(
                                PS2[:, par2 * 2048 + c4 * 512:par2 * 2048 + (c4 + 1) * 512],
                                ones1b[0:1, :],
                                browb[0:1, hh2 * 2048 + c4 * 512:hh2 * 2048 + (c4 + 1) * 512],
                                start=True, stop=True, skip_group_check=True)
                        vb = rec("tensor", bc, sPE if c4 == 3 else None)
                    v_bc_last[p2] = vb
                    wt("vector", sPE, vb)
                    if p2 == 0 and hh2 == 0:
                        wt("vector", sDV, v_attn_done)
                        if debug:
                            wt("vector", gDbg, v_dbg_attn)
                    rows = slice(64 * hh2, 64 * hh2 + 64)
                    def sc(e, rows=rows, p2=p2, par2=par2):
                        return e.tensor_tensor(attn_f[rows, p2 * TLOC:(p2 + 1) * TLOC],
                                               attn_f[rows, p2 * TLOC:(p2 + 1) * TLOC],
                                               PS2[rows, par2 * 2048:(par2 + 1) * 2048],
                                               ALU.mult)
                    v_scaled = rec("vector", sc, sDV)
                    v_sc_prev = locals().get("v_sc_prev", {})
                    v_sc_prev[hh2] = v_scaled

            wt("vector", gLdB, vLdB)
            wt("vector", sDV, v_scaled)
            v_x = None
            for j in range(DT8):
                def add(e, j=j):
                    return e.tensor_add(zz[:, j * TLOC:(j + 1) * TLOC],
                                        zz[:, j * TLOC:(j + 1) * TLOC],
                                        attn_f[:, j * TLOC:(j + 1) * TLOC])
                v_x = rec("vector", add, sDV if j == DT8 - 1 else None)

            sx_off = [0, 512, 3072, 3584]
            v_sq = {}
            v_stat = None
            v_mmrow = None
            for j in range(DT8):
                for hf in range(2):
                    if j == 0 and hf == 0:
                        wt("scalar", sDV, v_x)
                    elif v_mmrow is not None:
                        wt("scalar", sPE, v_mmrow)
                    def sq(e, j=j, hf=hf):
                        return e.activation(scrA[:, hf * 1024:(hf + 1) * 1024],
                                            zz[:, j * TLOC + hf * 1024: j * TLOC + hf * 1024 + 1024],
                                            AF.Square)
                    v_sq[(j, hf)] = rec("scalar", sq, sAC)
                    if j == 0 and hf == 0:
                        wt("tensor", sDV, v_x)
                    wt("tensor", sAC, v_sq[(j, hf)])
                    for c2 in range(2):
                        c4 = hf * 2 + c2
                        def m1(e, j=j, c4=c4):
                            return e.matmul(PS2[0:1, sx_off[c4]:sx_off[c4] + 512], onesf[:, :],
                                            zz[:, j * TLOC + c4 * 512: j * TLOC + (c4 + 1) * 512],
                                            start=(j == 0), stop=(j == DT8 - 1),
                                            skip_group_check=True)
                        rec("tensor", m1)
                    for c2 in range(2):
                        c4 = hf * 2 + c2
                        def m2(e, j=j, hf=hf, c2=c2, c4=c4):
                            return e.matmul(PS2[0:1, 1024 + c4 * 512:1024 + (c4 + 1) * 512], onesf[:, :],
                                            scrA[:, hf * 1024 + c2 * 512: hf * 1024 + (c2 + 1) * 512],
                                            start=(j == 0), stop=(j == DT8 - 1),
                                            skip_group_check=True)
                        v_mmrow = rec("tensor", m2, sPE if c2 == 1 else None)
            v_stat = v_mmrow
            wt("vector", sPE, v_stat)
            rec("vector", lambda e: e.tensor_copy(rowA[0:1, 0:1024], PS2[0:1, 0:1024]))
            v1 = rec("vector", lambda e: e.tensor_copy(rowA[0:1, 1024:2048], PS2[0:1, 3072:4096]), sDV)
            v2 = rec("vector", lambda e: e.tensor_copy(rowB[0:1, :], PS2[0:1, 1024:3072]), sDV)
            wt("vector", sDV, v2)
            v3 = rec("vector", lambda e: e.tensor_scalar_mul(rowA[0:1, :], rowA[0:1, :], 1.0 / D), sDV)
            wt("vector", sDV, v3)
            v4 = rec("vector", lambda e: e.tensor_scalar_mul(rowB[0:1, :], rowB[0:1, :], 1.0 / D), sDV)
            wt("vector", sDV, v4)
            v5 = rec("vector", lambda e: e.tensor_tensor(rowC[0:1, :], rowA[0:1, :], rowA[0:1, :], ALU.mult), sDV)
            wt("vector", sDV, v5)
            v6 = rec("vector", lambda e: e.tensor_tensor(rowB[0:1, :], rowB[0:1, :], rowC[0:1, :], ALU.subtract), sDV)
            wt("scalar", sDV, v6)
            v7 = rec("scalar", lambda e: e.activation(rowC[0:1, :], rowB[0:1, :], AF.Ln, bias=epsB[0:1, :], scale=1.0), sAC)
            wt("scalar", sAC, v7)
            v8 = rec("scalar", lambda e: e.activation(rowB[0:1, :], rowC[0:1, :], AF.Exp, bias=0.0, scale=-0.5), sAC)
            wt("vector", sAC, v8)
            v9 = rec("vector", lambda e: e.scalar_tensor_tensor(rowD[0:1, :], rowA[0:1, :], -1.0,
                                                                rowB[0:1, :], ALU.mult, ALU.mult), sDV)
            v_ln = None
            for hhf in range(2):
                wt("tensor", sDV, v9)
                if v_ln is not None:
                    wt("tensor", sDV, v_ln)
                vb = None
                for c2 in range(2):
                    def bcr(e, hhf=hhf, c2=c2):
                        return e.matmul(PS2[:, 1024 + c2 * 512:1024 + (c2 + 1) * 512], ones1f[0:1, :],
                                        rowB[0:1, hhf * 1024 + c2 * 512:hhf * 1024 + (c2 + 1) * 512],
                                        start=True, stop=True, skip_group_check=True)
                    rec("tensor", bcr)
                    def bcn(e, hhf=hhf, c2=c2):
                        return e.matmul(PS2[:, 2048 + c2 * 512:2048 + (c2 + 1) * 512], ones1f[0:1, :],
                                        rowD[0:1, hhf * 1024 + c2 * 512:hhf * 1024 + (c2 + 1) * 512],
                                        start=True, stop=True, skip_group_check=True)
                    vb = rec("tensor", bcn, sPE if c2 == 1 else None)
                wt("vector", sPE, vb)
                for j in range(DT8):
                    def t1(e, j=j, hhf=hhf):
                        return e.tensor_tensor(scrB[:, 0:1024],
                                               zz[:, j * TLOC + hhf * 1024: j * TLOC + hhf * 1024 + 1024],
                                               PS2[:, 2048:3072], ALU.add)
                    va = rec("vector", t1, sDV)
                    wt("vector", sDV, va)
                    def t2(e, j=j, hhf=hhf):
                        return e.tensor_tensor(zz[:, j * TLOC + hhf * 1024: j * TLOC + hhf * 1024 + 1024],
                                               scrB[:, 0:1024], PS2[:, 1024:2048], ALU.mult)
                    v_ln = rec("vector", t2, sDV)
                    wt("vector", sDV, v_ln)

            wt("gpsimd", sDV, v_ln)
            for j in range(DT8):
                dma("gpsimd", gFM, x1fm_dram[j * 128:(j + 1) * 128, :], zz[:, j * TLOC:(j + 1) * TLOC])
            vFM = gFM.n
            v_xb = {}
            for tt in range(TT16):
                buf = tt % 2
                if tt == 0:
                    wt("sync", gFM, vFM)
                if tt >= 2:
                    wt("sync", gPub, tt * 16)
                v_xb[tt] = rec("sync", lambda e, tt=tt, buf=buf: e.dma_start(
                    out=x1t_s[buf][:, :], in_=x1fm_dram[:, tt * 128:(tt + 1) * 128],
                    transpose=True), gXb, 16)
                wt("sync", gXb, v_xb[tt])
                dma("sync", gPub, x1pub[tt * 128:(tt + 1) * 128, :], x1t_s[buf][:, :])
            vPub = gPub.n
            wt("gpsimd", gPub, vPub)
            v_ag_x = rec("gpsimd", lambda e: e.collective_compute(
                "AllGather", ALU.bypass, replica_groups=[list(range(NCORE))],
                ins=[x1pub.ap()], outs=[xall.ap()]), sCC)

            wt("tensor", gLdB, vLdB)
            wt("tensor", sDV, v_ln)
            v_lg = None
            for j in range(DT8):
                for c4 in range(4):
                    def mg(e, j=j, c4=c4):
                        return e.matmul(PS2[0:16, 1024 + c4 * 512:1024 + (c4 + 1) * 512],
                                        gwT_s[:, j * E:(j + 1) * E],
                                        zz[:, j * TLOC + c4 * 512: j * TLOC + (c4 + 1) * 512],
                                        start=(j == 0), stop=(j == DT8 - 1),
                                        skip_group_check=True)
                    v_lg = rec("tensor", mg, sPE if (j == DT8 - 1 and c4 == 3) else None)
            wt("vector", sPE, v_lg)
            v_lge = rec("vector", lambda e: e.tensor_copy(lgs[:, :], PS2[0:16, 1024:3072]), sDV)
            wt("tensor", sDV, v_lge)
            v_lt = None
            for tt in range(TT16):
                def tr(e, tt=tt):
                    return e.transpose(PS2[:, tt * 16:(tt + 1) * 16],
                                       lgs[0:16, tt * 128:(tt + 1) * 128], identf[0:16, 0:16])
                v_lt = rec("tensor", tr, sPE if tt == TT16 - 1 else None)
            wt("vector", sPE, v_lt)
            v_lte = rec("vector", lambda e: e.tensor_copy(logit_t[:, :], PS2[:, 0:256]), sDV)
            if debug:
                wt("sync", sDV, v_lte)
                dma("sync", gDbg, dbg["logits"].ap(), logit_t[:, :])

            def seg(ap):
                return ap.rearrange("p (t e) -> p t e", e=E)

            def bc16(tile_ap):
                return bass.AP(tile_ap.tensor, tile_ap.offset,
                               [[TT16, 128], [1, TT16], [0, E]])

            wt("vector", sDV, v_lte)
            r1 = rec("vector", lambda e: e.tensor_reduce(mx16[:, :], seg(logit_t[:, :]),
                                                         axis=AX, op=ALU.max), sDV)
            wt("vector", sDV, r1)
            r2 = rec("vector", lambda e: e.tensor_tensor(seg(df16[:, :]), seg(logit_t[:, :]),
                                                         bc16(mx16[:, :]), ALU.subtract), sDV)
            wt("scalar", sDV, r2)
            r3 = rec("scalar", lambda e: e.activation(ex16[:, :], df16[:, :], AF.Exp), sAC)
            wt("vector", sAC, r3)
            r4 = rec("vector", lambda e: e.tensor_reduce(sm16[:, :], seg(ex16[:, :]),
                                                         axis=AX, op=ALU.add), sDV)
            wt("vector", sDV, r4)
            rec("vector", lambda e: e.reciprocal(prob_s[:, :], sm16[:, :]))
            r6 = rec("vector", lambda e: e.tensor_tensor(seg(eq16[:, :]), seg(logit_t[:, :]),
                                                         bc16(mx16[:, :]), ALU.is_ge), sDV)
            wt("vector", sDV, r6)
            r7 = rec("vector", lambda e: e.tensor_tensor(eq16[:, :], eq16[:, :],
                                                         iotmb_t[:, :], ALU.mult), sDV)
            wt("vector", sDV, r7)
            r8 = rec("vector", lambda e: e.tensor_scalar_add(eq16[:, :], eq16[:, :], 100.0), sDV)
            wt("vector", sDV, r8)
            r9 = rec("vector", lambda e: e.tensor_reduce(eidx_s[:, :], seg(eq16[:, :]),
                                                         axis=AX, op=ALU.min), sDV)
            wt("vector", sDV, r9)
            v_ohv = rec("vector", lambda e: e.tensor_tensor(seg(oh_all[:, :]), iota_t[:, :].rearrange("p (t e) -> p t e", e=E),
                                                            bc16(eidx_s[:, :]), ALU.is_equal), sDV)

            wt("tensor", gLdB, vLdB)
            wt("tensor", sDV, v_ohv)
            for tt in range(TT16):
                def mc(e, tt=tt):
                    return e.matmul(PS2[:, 256 + tt * 16:256 + (tt + 1) * 16], triu_s[:, :],
                                    oh_all[:, tt * E:(tt + 1) * E], start=True, stop=True,
                                    skip_group_check=True)
                rec("tensor", mc)
            for tt in range(TT16):
                def mn(e, tt=tt):
                    return e.matmul(PS2[0:1, 512 + tt * 16:512 + (tt + 1) * 16], onesf[:, :],
                                    oh_all[:, tt * E:(tt + 1) * E], start=True, stop=True,
                                    skip_group_check=True)
                rec("tensor", mn)
            v_ct = None
            for tt in range(TT16):
                def mt(e, tt=tt):
                    return e.matmul(PS2[0:1, 768:784], onesf[:, :],
                                    oh_all[:, tt * E:(tt + 1) * E],
                                    start=(tt == 0), stop=(tt == TT16 - 1),
                                    skip_group_check=True)
                v_ct = rec("tensor", mt, sPE if tt == TT16 - 1 else None)
            wt("vector", sPE, v_ct)
            v_cume = rec("vector", lambda e: e.tensor_copy(cum_s[:, :], PS2[:, 256:512]), sDV)
            v_cnte = rec("vector", lambda e: e.tensor_copy(cntrow[0:1, :], PS2[0:1, 512:768]), sDV)
            v_cte = rec("vector", lambda e: e.tensor_copy(ctot_s[0:1, :], PS2[0:1, 768:784]), sDV)
            wt("sync", sDV, v_cte)
            vCP = dma("sync", gCP, cnt_in[0:1, :], ctot_s[0:1, :])
            wt("gpsimd", gCP, vCP)
            v_ag_c = rec("gpsimd", lambda e: e.collective_compute(
                "AllGather", ALU.bypass, replica_groups=[list(range(NCORE))],
                ins=[cnt_in.ap()], outs=[cnt_out.ap()]), sCC)
            wt("gpsimd", sDV, v_cnte)
            for tt in range(TT16):
                dma("gpsimd", gCNT, cnt2[tt:tt + 1, :], cntrow[0:1, tt * E:(tt + 1) * E])
            wt("gpsimd", sCC, v_ag_c)
            dma("gpsimd", gCNT, cntag[:, :], cnt_out.ap())
            vCNT = gCNT.n
            wt("tensor", gCNT, vCNT)
            v_pb = None
            for tt in range(TT16):
                def mb1(e, tt=tt):
                    return e.matmul(PS2[:, 784 + tt * 16:784 + (tt + 1) * 16],
                                    trib_s[:, tt * 128:(tt + 1) * 128], cnt2[:, :],
                                    start=True, stop=False, skip_group_check=True)
                rec("tensor", mb1)
                def mb2(e, tt=tt):
                    return e.matmul(PS2[:, 784 + tt * 16:784 + (tt + 1) * 16],
                                    rmb_s[:, 0:128], cntag[:, :],
                                    start=False, stop=True, skip_group_check=True)
                v_pb = rec("tensor", mb2, sPE if tt == TT16 - 1 else None)
            wt("vector", sPE, v_pb)
            wt("vector", sDV, v_cume)
            v_pos = rec("vector", lambda e: e.tensor_tensor(pos_s[:, :], cum_s[:, :],
                                                            PS2[:, 784:1040], ALU.add), sDV)
            wt("vector", sDV, v_pos)
            s1_ = rec("vector", lambda e: e.tensor_tensor(eq16[:, :], oh_all[:, :],
                                                          pos_s[:, :], ALU.mult), sDV)
            wt("vector", sDV, s1_)
            s2_ = rec("vector", lambda e: e.tensor_reduce(psel_s[:, :], seg(eq16[:, :]),
                                                          axis=AX, op=ALU.add), sDV)
            wt("vector", sDV, s2_)
            s3_ = rec("vector", lambda e: e.tensor_single_scalar(kept_s[:, :], psel_s[:, :],
                                                                 float(CAP), ALU.is_lt), sDV)
            wt("vector", sDV, s3_)
            rec("vector", lambda e: e.tensor_tensor(peff_s[:, :], prob_s[:, :],
                                                    kept_s[:, :], ALU.mult))
            s5_ = rec("vector", lambda e: e.scalar_tensor_tensor(slotf_s[:, :], eidx_s[:, :],
                                                                 float(CAP), psel_s[:, :],
                                                                 ALU.mult, ALU.add), sDV)
            sp0 = rec("vector", lambda e: e.tensor_tensor(ex16[:, :], oh_all[:, :],
                                                          parT_s[:, :], ALU.mult), sDV)
            wt("vector", sDV, sp0)
            sp1 = rec("vector", lambda e: e.tensor_reduce(epar_s[:, :], seg(ex16[:, :]),
                                                          axis=AX, op=ALU.add), sDV)
            wt("vector", sDV, sp1)
            sp0b = rec("vector", lambda e: e.tensor_tensor(ex16[:, :], oh_all[:, :],
                                                           ehalfT_s[:, :], ALU.mult), sDV)
            wt("vector", sDV, sp0b)
            sp2 = rec("vector", lambda e: e.tensor_reduce(mx16[:, :], seg(ex16[:, :]),
                                                          axis=AX, op=ALU.add), sDV)
            wt("vector", sDV, sp2)
            sp3 = rec("vector", lambda e: e.scalar_tensor_tensor(mx16[:, :], mx16[:, :],
                                                                 float(CAP), psel_s[:, :],
                                                                 ALU.mult, ALU.add), sDV)
            wt("vector", sDV, sp3)
            sp4 = rec("vector", lambda e: e.tensor_tensor(mx16[:, :], mx16[:, :],
                                                          kept_s[:, :], ALU.mult), sDV)
            wt("vector", sDV, sp4)
            rec("vector", lambda e: e.tensor_copy(sg_i[:, :], mx16[:, :]))
            wt("vector", sDV, s5_)
            s6_ = rec("vector", lambda e: e.tensor_tensor(df16[:, 0:TT16], slotf_s[:, :],
                                                          kept_s[:, :], ALU.mult), sDV)
            wt("vector", sDV, s6_)
            s8_ = rec("vector", lambda e: e.scalar_tensor_tensor(df16[:, TT16:2 * TT16], kept_s[:, :],
                                                                 -float(TRASH), df16[:, 0:TT16],
                                                                 ALU.mult, ALU.add), sDV)
            wt("vector", sDV, s8_)
            s9_ = rec("vector", lambda e: e.tensor_scalar_add(df16[:, TT16:2 * TT16],
                                                              df16[:, TT16:2 * TT16],
                                                              float(TRASH)), sDV)
            wt("vector", sDV, s9_)
            v_slots = rec("vector", lambda e: e.tensor_copy(st_i[:, :], df16[:, TT16:2 * TT16]), sDV)
            if debug:
                wt("sync", sDV, v_slots)
                dma("sync", gDbg, dbg["route"][:, 0 * TT16:1 * TT16], eidx_s[:, :])
                dma("sync", gDbg, dbg["route"][:, 1 * TT16:2 * TT16], psel_s[:, :])
                dma("sync", gDbg, dbg["route"][:, 2 * TT16:3 * TT16], kept_s[:, :])
                dma("sync", gDbg, dbg["route"][:, 3 * TT16:4 * TT16], prob_s[:, :])
                dma("sync", gDbg, dbg["route"][:, 4 * TT16:5 * TT16], slotf_s[:, :])
                wt("sync", gPub, vPub)
                dma("sync", gDbg, dbg["x1"].ap(), x1pub.ap())

            wt("gpsimd", sDV, v_slots)
            wt("gpsimd", gA0, vA0)
            for tt in range(TT16):
                def sct(e, tt=tt):
                    return e.indirect_dma_start(
                        tbl_in[:, :], IndirectOffsetOnAxis(st_i[:, tt:tt + 1], 0),
                        tokidx_s[:, tt:tt + 1], None)
                rec("gpsimd", sct, gSCT, 16)
            wt("gpsimd", gSCT, gSCT.n)
            v_ar_t = rec("gpsimd", lambda e: e.collective_compute(
                "AllReduce", ALU.add, replica_groups=[list(range(NCORE))],
                ins=[tbl_in.ap()], outs=[tbl_out.ap()]), sCC)

        if stop_after not in ("A", "B"):
            _stages = ["tbl", "g", "gf", "h", "o", "eo", None]
            def c_active(stage):
                return _stages.index(stage) <= _stages.index(c_sub) if c_sub else True
            stk_b2.close()
            stk_b1.close()
            G = sbuf(stk_c, "G", [128, 10 * 1024], BF16)
            Gf = sbuf(stk_c, "Gf", [128, DT8 * CAP], BF16)
            hT = sbuf(stk_c, "hT", [128, 32 * CAP], BF16)
            oT = sbuf(stk_c, "oT", [128, DT8 * CAP], BF16)
            EOs = sbuf(stk_c, "EOs", [128, 10 * 1024], BF16)
            w1b = [sbuf(stk_c, f"w1b{i}", [128, DT8 * 128], BF16) for i in range(2)]
            w2b = [sbuf(stk_c, f"w2b{i}", [128, 32 * 128], BF16) for i in range(2)]
            PSC = stk_c.enter_context(nc.psum_tensor("PSC", [128, 3072], F32))
            PSB = stk_c.enter_context(nc.psum_tensor("PSB", [128, 2048], BF16))

            w1r = w1o.ap().rearrange("(l j p) hh -> l p j hh", l=2, p=128)
            w2r = w2o.ap().rearrange("(l i p) dd -> l p i dd", l=2, p=128)

            wt("gpsimd", sCC, v_ar_t)
            wt("gpsimd", gA0, vA0)
            for k in range(20):
                def tg(e, k=k):
                    return e.indirect_dma_start(
                        tblown[:, k:k + 1], None, tbl_out[:, :],
                        IndirectOffsetOnAxis(slotsel_s[:, k:k + 1], 0))
                rec("gpsimd", tg, gTBL, 16)
            vTBL = gTBL.n

            gf_tr_last = {}
            v_ag_eo = {}
            gf_ev = {}
            ht_mm_last = {}
            ht_ac = {}
            ot_mm_last = {}
            ot_ev = {}
            eo_ev = {}
            v_eo_w = {}
            v_w1 = {}
            v_w2 = {}
            v_dbg_eo = {}
            for le in range(2):
                if not c_active("g"):
                    break
                wt("gpsimd", sCC, v_ag_x)
                wt("gpsimd", gTBL, vTBL)
                if le == 1:
                    wt("gpsimd", gGA, v_gd_prev)
                    wt("gpsimd", gXb, gf_tr_last[0])
                if le == 1 and c_sub == "g" and debug:
                    wt("gpsimd", gDbg, v_dbg_eo[0])
                for k in range(10):
                    def gg(e, le=le, k=k):
                        return e.indirect_dma_start(
                            G[:, k * 1024:(k + 1) * 1024], None, xall[:, :],
                            IndirectOffsetOnAxis(tblown[:, le * 10 + k:le * 10 + k + 1], 0))
                    rec("gpsimd", gg, gGA, 16)
                vGA = gGA.n
                if debug and c_sub == "g":
                    wt("sync", gGA, vGA)
                    v_dbg_eo[le] = dma("sync", gDbg,
                                       dbg["eo"].ap().rearrange("(l k p) dd -> l p k dd", l=2, p=128)[le],
                                       G[:, :].rearrange("p (k dd) -> p k dd", k=10))
                if not c_active("gf"):
                    continue
                v_gd = dma("gpsimd", gGA, gdram.ap().rearrange("(k p) dd -> p k dd", p=128),
                           G[:, :].rearrange("p (k dd) -> p k dd", k=10))
                v_gd_prev = v_gd
                wt("sync", gGA, v_gd)
                if le == 1:
                    wt("sync", sPE, ht_mm_last[(0, 31)])
                v_gfx = 0
                for j in range(DT8):
                    v_gfx = rec("sync", lambda e, j=j: e.dma_start(
                        out=Gf[:, j * CAP:(j + 1) * CAP],
                        in_=gdram[:, j * 128:(j + 1) * 128], transpose=True), gXb, 16)
                gf_tr_last[le] = v_gfx
                gf_last = v_gfx
                if not c_active("h"):
                    continue
                for i in range(2):
                    if le == 1:
                        wt("gpsimd", sPE, ht_mm_last[(0, 30 + i)])
                    v_w1[(le, i)] = dma("gpsimd", gWs[i % 2],
                                        w1b[i % 2][:, :].rearrange("p (j hh) -> p j hh", j=DT8),
                                        w1r[le, :, :, i * 128:(i + 1) * 128])
                for i in range(32):
                    buf = i % 2
                    wt("tensor", gWs[buf], v_w1[(le, i)])
                    if i == 0:
                        wt("tensor", gXb, gf_last)
                        if le == 1:
                            wt("tensor", sDV, ot_ev[(0, DT8 - 1)])
                    if i >= 2:
                        wt("tensor", sAC, ht_ac[(le, i - 2)])
                    for j in range(DT8):
                        for c3 in range(3):
                            w_ = 512 if c3 < 2 else 256
                            def mh(e, i=i, j=j, c3=c3, w_=w_, buf=buf):
                                return e.matmul(PSC[:, buf * 1536 + c3 * 512: buf * 1536 + c3 * 512 + w_],
                                                w1b[buf][:, j * 128:(j + 1) * 128],
                                                Gf[:, j * CAP + c3 * 512: j * CAP + c3 * 512 + w_],
                                                start=(j == 0), stop=(j == DT8 - 1),
                                                skip_group_check=True)
                            if j == DT8 - 1 and c3 == 2:
                                ht_mm_last[(le, i)] = rec("tensor", mh, sPE)
                            else:
                                rec("tensor", mh)
                    wt("scalar", sPE, ht_mm_last[(le, i)])
                    def rl(e, i=i, buf=buf):
                        return e.activation(hT[:, i * CAP:(i + 1) * CAP],
                                            PSC[:, buf * 1536:buf * 1536 + CAP], AF.Relu)
                    ht_ac[(le, i)] = rec("scalar", rl, sAC)
                    if i + 2 < 32:
                        wt("gpsimd", sPE, ht_mm_last[(le, i)])
                        v_w1[(le, i + 2)] = dma("gpsimd", gWs[buf],
                                                w1b[buf][:, :].rearrange("p (j hh) -> p j hh", j=DT8),
                                                w1r[le, :, :, (i + 2) * 128:(i + 3) * 128])
                if not c_active("o"):
                    continue
                for j in range(2):
                    if le == 1:
                        wt("gpsimd", sPE, ot_mm_last[(0, 6 + j)])
                    v_w2[(le, j)] = dma("gpsimd", gWs[j % 2],
                                        w2b[j % 2][:, :].rearrange("p (i hh) -> p i hh", i=32),
                                        w2r[le, :, :, j * 128:(j + 1) * 128])
                for j in range(DT8):
                    buf = j % 2
                    wt("tensor", gWs[buf], v_w2[(le, j)])
                    if j == 0:
                        wt("tensor", sAC, ht_ac[(le, 31)])
                    if j >= 2:
                        wt("tensor", sDV, ot_ev[(le, j - 2)])
                    for i in range(32):
                        for c3 in range(3):
                            w_ = 512 if c3 < 2 else 256
                            def mo(e, i=i, j=j, c3=c3, w_=w_, buf=buf):
                                return e.matmul(PSC[:, buf * 1536 + c3 * 512: buf * 1536 + c3 * 512 + w_],
                                                w2b[buf][:, i * 128:(i + 1) * 128],
                                                hT[:, i * CAP + c3 * 512: i * CAP + c3 * 512 + w_],
                                                start=(i == 0), stop=(i == 31),
                                                skip_group_check=True)
                            if i == 31 and c3 == 2:
                                ot_mm_last[(le, j)] = rec("tensor", mo, sPE)
                            else:
                                rec("tensor", mo)
                    wt("vector", sPE, ot_mm_last[(le, j)])
                    def oev(e, j=j, buf=buf):
                        return e.tensor_copy(oT[:, j * CAP:(j + 1) * CAP],
                                             PSC[:, buf * 1536:buf * 1536 + CAP])
                    ot_ev[(le, j)] = rec("vector", oev, sDV)
                    if j + 2 < DT8:
                        wt("gpsimd", sPE, ot_mm_last[(le, j)])
                        v_w2[(le, j + 2)] = dma("gpsimd", gWs[buf],
                                                w2b[buf][:, :].rearrange("p (i hh) -> p i hh", i=32),
                                                w2r[le, :, :, (j + 2) * 128:(j + 3) * 128])
                if not c_active("eo"):
                    continue
                wt("gpsimd", sDV, ot_ev[(le, DT8 - 1)])
                v_od = dma("gpsimd", gEO, odram.ap().rearrange("(j p) s -> p j s", p=128),
                           oT[:, :].rearrange("p (j s) -> p j s", j=DT8))
                wt("scalar", gEO, gEO.n)
                v_eox = 0
                for k in range(10):
                    v_eox = rec("scalar", lambda e, k=k: e.dma_start(
                        out=EOs[:, k * 1024:(k + 1) * 1024],
                        in_=odram[:, k * 128:(k + 1) * 128], transpose=True), gXE, 16)
                wt("gpsimd", gXE, v_eox)
                v_eo_w[le] = dma("gpsimd", gEO,
                                 eo_in[le].ap().rearrange("(k p) dd -> p k dd", p=128),
                                 EOs[:, :].rearrange("p (k dd) -> p k dd", k=10))
                wt("gpsimd", gEO, gEO.n)
                v_ag_eo[le] = rec("gpsimd", lambda e, le=le: e.collective_compute(
                    "AllGather", ALU.bypass, replica_groups=[list(range(NCORE))],
                    ins=[eo_in[le].ap()],
                    outs=[(eo_allE if le == 0 else eo_allO).ap()]), sCC)
                if debug:
                    wt("sync", gEO, v_eo_w[le])
                    v_dbg_eo[le] = dma("sync", gDbg,
                                       dbg["eo"].ap().rearrange("(l k p) dd -> l p k dd", l=2, p=128)[le],
                                       EOs[:, :].rearrange("p (k dd) -> p k dd", k=10))
            if c_sub is None and stop_after == "C":
                wt("gpsimd", sCC, v_ag_eo[1])

        if stop_after is None:
            stk_c.close()
            x1tm = sbuf(stk_d, "x1tm", [128, TT16 * 1024], BF16)
            moe = sbuf(stk_d, "moe", [128, TT16 * 1024], BF16)
            moeO = sbuf(stk_d, "moeO", [128, TT16 * 1024], BF16)
            x2t = [sbuf(stk_d, f"x2t{i}", [128, 1024], F32) for i in range(2)]
            yt = [sbuf(stk_d, f"yt{i}", [128, 1024], F32) for i in range(2)]
            dsc = sbuf(stk_d, "dsc", [128, 1024], F32)
            dsc2 = sbuf(stk_d, "dsc2", [128, 1024], BF16)
            cst = sbuf(stk_d, "cst", [128, 16], F32)

            wt("gpsimd", gPub, vPub)
            for tt in range(TT16):
                dma("gpsimd", gD, x1tm[:, tt * 1024:(tt + 1) * 1024], x1pub[tt * 128:(tt + 1) * 128, :])
            wt("gpsimd", sCC, v_ag_eo[1])
            for tt in range(TT16):
                def mgthE(e, tt=tt):
                    return e.indirect_dma_start(
                        moe[:, tt * 1024:(tt + 1) * 1024], None, eo_allE[:, :],
                        IndirectOffsetOnAxis(sg_i[:, tt:tt + 1], 0))
                rec("gpsimd", mgthE, gD, 16)
                def mgthO(e, tt=tt):
                    return e.indirect_dma_start(
                        moeO[:, tt * 1024:(tt + 1) * 1024], None, eo_allO[:, :],
                        IndirectOffsetOnAxis(sg_i[:, tt:tt + 1], 0))
                rec("gpsimd", mgthO, gD, 16)
            vD = gD.n

            if debug:
                wt("sync", gD, vD)
                dma("sync", gDbg, dbg["moeE"].ap(), moe[:, :])
                dma("sync", gDbg, dbg["moeO"].ap(), moeO[:, :])
                wt("sync", sDV, v_slots)
                dma("sync", gDbg, dbg["sg"].ap(), sg_i[:, :])
            vst = {}
            for tt in range(TT16):
                buf = tt % 2
                if tt == 0:
                    wt("vector", gD, vD)
                if tt >= 2:
                    wt("vector", gY[buf], vst[tt - 2])
                c0a = rec("vector", lambda e, tt=tt: e.tensor_tensor(
                    dsc[:, :], moeO[:, tt * 1024:(tt + 1) * 1024],
                    moe[:, tt * 1024:(tt + 1) * 1024], ALU.subtract), sDV)
                wt("vector", sDV, c0a)
                c0b = rec("vector", lambda e, tt=tt: e.scalar_tensor_tensor(
                    dsc[:, :], dsc[:, :], epar_s[:, tt:tt + 1],
                    moe[:, tt * 1024:(tt + 1) * 1024], ALU.mult, ALU.add), sDV)
                wt("vector", sDV, c0b)
                c1 = rec("vector", lambda e, tt=tt, buf=buf: e.tensor_scalar(
                    x2t[buf][:, :], dsc[:, :],
                    peff_s[:, tt:tt + 1], None, ALU.mult), sDV)
                wt("vector", sDV, c1)
                c2 = rec("vector", lambda e, tt=tt, buf=buf: e.tensor_tensor(
                    x2t[buf][:, :], x2t[buf][:, :], x1tm[:, tt * 1024:(tt + 1) * 1024],
                    ALU.add), sDV)
                wt("vector", sDV, c2)
                c3 = rec("vector", lambda e, buf=buf: e.reduce_sum(cst[:, 0:1], x2t[buf][:, :],
                                                                   axis=AX), sDV)
                wt("scalar", sDV, c2)
                c4 = rec("scalar", lambda e, buf=buf: e.activation(dsc2[:, :], x2t[buf][:, :],
                                                                   AF.Square, accum_out=cst[:, 1:2]), sAC)
                wt("vector", sAC, c4)
                wt("vector", sDV, c3)
                c5 = rec("vector", lambda e: e.tensor_scalar_mul(cst[:, 2:3], cst[:, 0:1], 1.0 / D), sDV)
                wt("vector", sDV, c5)
                c6 = rec("vector", lambda e: e.tensor_scalar_mul(cst[:, 3:4], cst[:, 1:2], 1.0 / D), sDV)
                wt("vector", sDV, c6)
                c7 = rec("vector", lambda e: e.tensor_tensor(cst[:, 4:5], cst[:, 2:3], cst[:, 2:3],
                                                             ALU.mult), sDV)
                wt("vector", sDV, c7)
                c8 = rec("vector", lambda e: e.tensor_tensor(cst[:, 5:6], cst[:, 3:4], cst[:, 4:5],
                                                             ALU.subtract), sDV)
                wt("scalar", sDV, c8)
                c9 = rec("scalar", lambda e: e.activation(cst[:, 6:7], cst[:, 5:6], AF.Sqrt,
                                                          bias=epsB[:, :], scale=1.0), sAC)
                wt("vector", sAC, c9)
                c10 = rec("vector", lambda e: e.reciprocal(cst[:, 7:8], cst[:, 6:7]), sDV)
                wt("vector", sDV, c10)
                c11 = rec("vector", lambda e: e.scalar_tensor_tensor(cst[:, 8:9], cst[:, 2:3], -1.0,
                                                                     cst[:, 7:8], ALU.mult, ALU.mult), sDV)
                wt("vector", sDV, c11)
                c12 = rec("vector", lambda e, buf=buf: e.tensor_scalar(
                    yt[buf][:, :], x2t[buf][:, :], cst[:, 7:8], cst[:, 8:9],
                    ALU.mult, ALU.add), sDV)
                wt("sync", sDV, c12)
                vst[tt] = dma("sync", gY[buf], y_out[tt * 128:(tt + 1) * 128, :], yt[buf][:, :])
            wt("sync", gY[0], gY[0].n)
            wt("sync", gY[1], gY[1].n)
            if debug:
                wt("sync", gDbg, gDbg.n)
            stk_d.close()

        if stop_after is not None:
            if debug:
                wt("sync", gDbg, gDbg.n)
            if stop_after == "A":
                stk_a.close()
                stk_b1.close()
            elif stop_after == "B":
                stk_b2.close()
                stk_b1.close()
            elif stop_after == "C":
                stk_c.close()
        for s, v in waits_record:
            assert v <= s.n, f"wait on {s.name} for {v} but final count {s.n}"

        with nc.Block() as block:
            @block.tensor
            def _(eng):
                for f in ops["tensor"]:
                    f(eng)

            @block.vector
            def _(eng):
                for f in ops["vector"]:
                    f(eng)

            @block.scalar
            def _(eng):
                for f in ops["scalar"]:
                    f(eng)

            @block.gpsimd
            def _(eng):
                for f in ops["gpsimd"]:
                    f(eng)

            @block.sync
            def _(eng):
                for f in ops["sync"]:
                    f(eng)

    return nc


_CACHE = {}


def _consts():
    identb = np.eye(128, dtype=ml_dtypes.bfloat16)
    identf = np.eye(128, dtype=np.float32)
    triu = (np.arange(128)[:, None] < np.arange(128)[None, :]).astype(np.float32)
    trib = np.zeros((16, 16 * 128), dtype=np.float32)
    for tt in range(16):
        trib[:, tt * 128:(tt + 1) * 128] = (np.arange(16)[:, None] < tt).astype(np.float32)
    return identb, identf, triu, trib


def make_in_maps(z, wqkv, gw, w1, w2):
    identb, identf, triu, trib = _consts()
    wqkvT = np.ascontiguousarray(wqkv.T)
    gwT = np.ascontiguousarray(gw.T)
    zT = np.ascontiguousarray(np.transpose(z, (0, 2, 1)))
    in_maps = []
    for c in range(NCORE):
        b, qoff = c // 2, (c % 2) * TLOC
        zTc = np.roll(zT[b], -qoff, axis=1) if qoff else zT[b]
        rmb = np.zeros((8, 128), dtype=np.float32)
        rmb[0:c, :] = 1.0
        tok = (c * TLOC + np.arange(TT16)[None, :] * 128 +
               np.arange(128)[:, None]).astype(np.int32)
        ssel = (2 * c * CAP + np.arange(20)[None, :] * 128 +
                np.arange(128)[:, None]).astype(np.int32)
        parT = np.tile((np.arange(16) % 2).astype(np.float32), (128, 16))
        ehalfT = np.tile((np.arange(16) // 2).astype(np.float32), (128, 16))
        in_maps.append({
            "zT": np.ascontiguousarray(zTc),
            "wqkvT": wqkvT,
            "gwT": gwT,
            "w1o": np.ascontiguousarray(w1[2 * c:2 * c + 2].reshape(2 * D, HID)),
            "w2o": np.ascontiguousarray(w2[2 * c:2 * c + 2].reshape(2 * HID, D)),
            "identbd": identb,
            "identfd": identf,
            "triud": triu,
            "tribd": trib,
            "rmbd": rmb,
            "tokidxd": tok,
            "slotseld": ssel,
            "partd": parT,
            "ehalfd": ehalfT,
        })
    return in_maps


def kernel(L0_attn_z, L0_attn_W_qkv, L0_ffn_gate_w, L0_ffn_w1, L0_ffn_w2):
    z = np.asarray(L0_attn_z, dtype=np.float32)
    wqkv = np.asarray(L0_attn_W_qkv, dtype=np.float32)
    gw = np.asarray(L0_ffn_gate_w, dtype=np.float32)
    w1 = np.asarray(L0_ffn_w1, dtype=np.float32)
    w2 = np.asarray(L0_ffn_w2, dtype=np.float32)

    if "nc" not in _CACHE:
        _CACHE["nc"] = build_program(debug=False)
    nc = _CACHE["nc"]
    in_maps = make_in_maps(z, wqkv, gw, w1, w2)
    res = run_bass_kernel_spmd(nc, in_maps, core_ids=list(range(NCORE)))
    y = np.concatenate([np.asarray(res.results[c]["y"]) for c in range(NCORE)], axis=0)
    return y.reshape(4, SEQ, D)


# version 12
# speedup vs baseline: 1.0830x; 1.0054x over previous
import numpy as np
import ml_dtypes
import contextlib
import concourse.bass as bass
import concourse.mybir as mybir
from concourse.bass_core_rust import IndirectOffsetOnAxis
from concourse.bass_utils import run_bass_kernel_spmd

F32 = mybir.dt.float32
BF16 = mybir.dt.bfloat16
I32 = mybir.dt.int32
AX = mybir.AxisListType.X
ALU = mybir.AluOpType
AF = mybir.ActivationFunctionType

NCORE = 8
SEQ, D, NH, DH, E, CAP, HID = 4096, 1024, 16, 64, 16, 1280, 4096
TLOC = 2048
NPAIR = 8
QB = 512
NQB = TLOC // QB
KCH = SEQ // 128
TT16 = TLOC // 128
DT8 = D // 128
SLOTS = 2 * CAP
TBLN = 20608
TRASH = 20480
SOFT_B = 16.0
NCHUNK = NQB * KCH


class Sem:
    def __init__(self, handle, name):
        self.h = handle
        self.name = name
        self.n = 0


def build_program(debug=False, stop_after=None, c_sub=None):
    nc = bass.Bass("TRN2", target_bir_lowering=False, debug=False)

    zT = nc.dram_tensor("zT", [D, SEQ], F32, kind="ExternalInput")
    wqkvT = nc.dram_tensor("wqkvT", [D, 3 * D], F32, kind="ExternalInput")
    gwT = nc.dram_tensor("gwT", [D, E], F32, kind="ExternalInput")
    w1o = nc.dram_tensor("w1o", [2 * D, HID], F32, kind="ExternalInput")
    w2o = nc.dram_tensor("w2o", [2 * HID, D], F32, kind="ExternalInput")
    identb_d = nc.dram_tensor("identbd", [128, 128], BF16, kind="ExternalInput")
    identf_d = nc.dram_tensor("identfd", [128, 128], F32, kind="ExternalInput")
    triu_d = nc.dram_tensor("triud", [128, 128], F32, kind="ExternalInput")
    trib_d = nc.dram_tensor("tribd", [16, 16 * 128], F32, kind="ExternalInput")
    rmb_d = nc.dram_tensor("rmbd", [8, 128], F32, kind="ExternalInput")
    tokidx_d = nc.dram_tensor("tokidxd", [128, TT16], I32, kind="ExternalInput")
    slotsel_d = nc.dram_tensor("slotseld", [128, 20], I32, kind="ExternalInput")
    part_d = nc.dram_tensor("partd", [128, TT16 * E], F32, kind="ExternalInput")
    ehalf_d = nc.dram_tensor("ehalfd", [128, TT16 * E], F32, kind="ExternalInput")

    y_out = nc.dram_tensor("y", [TLOC, D], F32, kind="ExternalOutput")

    sums_dram = nc.dram_tensor("sums_dram", [2 * NPAIR, TLOC], F32, kind="Internal")
    sums_r = nc.dram_tensor("sums_r", [2 * NPAIR, TLOC], F32, kind="Internal")
    x1fm_dram = nc.dram_tensor("x1fm_dram", [D, TLOC], BF16, kind="Internal")
    x1pub = nc.dram_tensor("x1pub", [TLOC, D], BF16, kind="Internal")
    xall = nc.dram_tensor("xall", [NCORE * TLOC, D], BF16, kind="Internal", addr_space="Shared")
    cnt_in = nc.dram_tensor("cnt_in", [1, E], F32, kind="Internal")
    cnt_out = nc.dram_tensor("cnt_out", [NCORE, E], F32, kind="Internal")
    tbl_in = nc.dram_tensor("tbl_in", [TBLN, 1], I32, kind="Internal")
    tbl_out = nc.dram_tensor("tbl_out", [TBLN, 1], I32, kind="Internal")
    eo_in = [nc.dram_tensor("eo_inE", [CAP, D], BF16, kind="Internal"),
             nc.dram_tensor("eo_inO", [CAP, D], BF16, kind="Internal")]
    eo_allE = nc.dram_tensor("eo_allE", [NCORE * CAP, D], BF16, kind="Internal", addr_space="Shared")
    eo_allO = nc.dram_tensor("eo_allO", [NCORE * CAP, D], BF16, kind="Internal", addr_space="Shared")
    gdram = nc.dram_tensor("gdram", [CAP, D], BF16, kind="Internal")
    odram = nc.dram_tensor("odram", [D, CAP], BF16, kind="Internal")

    dbg = {}
    if debug:
        dbg["attn"] = nc.dram_tensor("dbg_attn", [128, DT8 * TLOC], BF16, kind="ExternalOutput")
        dbg["sums"] = nc.dram_tensor("dbg_sums", [2 * NPAIR, TLOC], F32, kind="ExternalOutput")
        dbg["x1"] = nc.dram_tensor("dbg_x1", [TLOC, D], BF16, kind="ExternalOutput")
        dbg["logits"] = nc.dram_tensor("dbg_logits", [128, TT16 * E], F32, kind="ExternalOutput")
        dbg["route"] = nc.dram_tensor("dbg_route", [128, 5 * TT16], F32, kind="ExternalOutput")
        dbg["sg"] = nc.dram_tensor("dbg_sg", [128, TT16], mybir.dt.int32, kind="ExternalOutput")
        dbg["epar"] = nc.dram_tensor("dbg_epar", [128, TT16], F32, kind="ExternalOutput")
        dbg["moeE"] = nc.dram_tensor("dbg_moeE", [128, TT16 * 1024], BF16, kind="ExternalOutput")
        dbg["moeO"] = nc.dram_tensor("dbg_moeO", [128, TT16 * 1024], BF16, kind="ExternalOutput")
        dbg["eo"] = nc.dram_tensor("dbg_eo", [SLOTS, D], BF16, kind="ExternalOutput")

    ops = {k: [] for k in ("tensor", "vector", "scalar", "gpsimd", "sync")}
    waits_record = []

    P = contextlib.ExitStack()
    stk_b1 = contextlib.ExitStack()
    stk_a = contextlib.ExitStack()
    stk_b2 = contextlib.ExitStack()
    stk_c = contextlib.ExitStack()
    stk_d = contextlib.ExitStack()

    def sbuf(stk, name, shape, dt):
        return stk.enter_context(nc.sbuf_tensor(name, shape, dt))

    with P:
        def mksem(name):
            return Sem(P.enter_context(nc.semaphore(name)), name)

        sPE = mksem("sPE")
        sDV = mksem("sDV")
        sAC = mksem("sAC")
        sSc = mksem("sSc")
        sEx = mksem("sEx")
        sAV = mksem("sAV")
        sCC = mksem("sCC")
        gA0 = mksem("gA0")
        gWq = [mksem("gWq0"), mksem("gWq1")]
        gWv2 = mksem("gWv2")
        gST1 = mksem("gST1")
        gLdB = mksem("gLdB")
        gRow = mksem("gRow")
        gFM = mksem("gFM")
        gXb = mksem("gXb")
        gXE = mksem("gXE")
        gPub = mksem("gPub")
        gCP = mksem("gCP")
        gCNT = mksem("gCNT")
        gSCT = mksem("gSCT")
        gTBL = mksem("gTBL")
        gGA = mksem("gGA")
        gWs = [mksem("gWs0"), mksem("gWs1")]
        gEO = mksem("gEO")
        gD = mksem("gD")
        gY = [mksem("gY0"), mksem("gY1")]
        gDbg = mksem("gDbg")

        def rec(eng, fn, s=None, k=1):
            if s is None:
                ops[eng].append(fn)
                return None
            s.n += k
            v = s.n
            ops[eng].append(lambda e, fn=fn, h=s.h, k=k: fn(e).then_inc(h, k))
            return v

        def wt(eng, s, v):
            assert v is not None
            waits_record.append((s, v))
            ops[eng].append(lambda e, h=s.h, v=v: e.wait_ge(h, v))

        def dma(eng, s, out, in_, **kw):
            return rec(eng, lambda e, o=out, i=in_, kw=kw: e.dma_start(out=o, in_=i, **kw),
                       s, 16)

        onesb = sbuf(P, "onesb", [128, 1], BF16)
        onesf = sbuf(P, "onesf", [128, 1], F32)
        ones1b = sbuf(P, "ones1b", [1, 128], BF16)
        ones1f = sbuf(P, "ones1f", [1, 128], F32)
        identb = sbuf(P, "identb", [128, 128], BF16)
        identf = sbuf(P, "identf", [128, 128], F32)
        tokidx_s = sbuf(P, "tokidx_s", [128, TT16], I32)
        slotsel_s = sbuf(P, "slotsel_s", [128, 20], I32)
        tblown = sbuf(P, "tblown", [128, 20], I32)
        peff_s = sbuf(P, "peff_s", [128, TT16], F32)
        sg_i = sbuf(P, "sg_i", [128, TT16], I32)
        st_i = sbuf(P, "st_i", [128, TT16], I32)
        eidx_s = sbuf(P, "eidx_s", [128, TT16], F32)
        psel_s = sbuf(P, "psel_s", [128, TT16], F32)
        kept_s = sbuf(P, "kept_s", [128, TT16], F32)
        prob_s = sbuf(P, "prob_s", [128, TT16], F32)
        slotf_s = sbuf(P, "slotf_s", [128, TT16], F32)
        epar_s = sbuf(P, "epar_s", [128, TT16], F32)
        zeri = sbuf(P, "zeri", [128, 161], I32)
        negB = sbuf(P, "negB", [128, 1], F32)
        epsB = sbuf(P, "epsB", [128, 1], F32)
        iota16 = sbuf(P, "iota16", [128, E], F32)
        iotmb = sbuf(P, "iotmb", [128, E], F32)
        iota_t = sbuf(P, "iota_t", [128, TT16 * E], F32)
        iotmb_t = sbuf(P, "iotmb_t", [128, TT16 * E], F32)
        parT_s = sbuf(P, "parT_s", [128, TT16 * E], F32)
        ehalfT_s = sbuf(P, "ehalfT_s", [128, TT16 * E], F32)

        attn_f = sbuf(stk_b1, "attn_f", [128, DT8 * TLOC], BF16)
        sums_sb = sbuf(stk_b1, "sums_sb", [33, TLOC], F32)

        Zf = sbuf(stk_a, "Zf", [128, DT8 * SEQ], BF16)
        Vh = sbuf(stk_a, "Vh", [128, KCH * 512], BF16)
        QT = [sbuf(stk_a, f"QT{i}", [128, TLOC], BF16) for i in range(2)]
        KT = [sbuf(stk_a, f"KT{i}", [128, SEQ], BF16) for i in range(2)]
        Wq = [sbuf(stk_a, f"Wq{i}", [128, DT8 * 128], BF16) for i in range(2)]
        Wk = [sbuf(stk_a, f"Wk{i}", [128, DT8 * 128], BF16) for i in range(2)]
        Wv = sbuf(stk_a, "Wv", [128, DT8 * 512], BF16)
        PTb = [sbuf(stk_a, f"PTb{i}", [128, 1024], BF16) for i in range(2)]
        PS = stk_a.enter_context(nc.psum_tensor("PS", [128, 4096], F32))

        dma("gpsimd", gA0, identb[:, :], identb_d[:, :])
        dma("gpsimd", gA0, identf[:, :], identf_d[:, :])
        dma("gpsimd", gA0, tokidx_s[:, :], tokidx_d[:, :])
        dma("gpsimd", gA0, slotsel_s[:, :], slotsel_d[:, :])
        dma("gpsimd", gA0, parT_s[:, :], part_d[:, :])
        dma("gpsimd", gA0, ehalfT_s[:, :], ehalf_d[:, :])
        for j in range(DT8):
            dma("gpsimd", gA0, Zf[:, j * SEQ:(j + 1) * SEQ], zT[j * 128:(j + 1) * 128, :])

        rec("vector", lambda e: e.memset(onesb[:, :], 1.0))
        rec("vector", lambda e: e.memset(onesf[:, :], 1.0))
        rec("vector", lambda e: e.memset(ones1b[:, :], 1.0))
        rec("vector", lambda e: e.memset(ones1f[:, :], 1.0))
        rec("vector", lambda e: e.memset(negB[:, :], -SOFT_B))
        rec("vector", lambda e: e.memset(epsB[:, :], 1e-5))
        v_zeri = rec("vector", lambda e: e.memset(zeri[:, :], 0), sDV)
        v_iota = rec("gpsimd", lambda e: e.iota(iota16[:, :], [[1, E]], channel_multiplier=0,
                                               allow_small_or_imprecise_dtypes=True), gA0, 16)
        wt("gpsimd", sDV, v_zeri)
        dma("gpsimd", gA0, tbl_in.ap().rearrange("(p a) x -> p (a x)", p=128), zeri[:, :])

        wq_p = wqkvT.ap().rearrange("(j p) (h q x) -> p j h q x", p=128, h=NH, x=DH)

        def load_wv(half, s):
            v = None
            for j in range(DT8):
                v = dma("gpsimd", s,
                        Wv[:, j * 512:(j + 1) * 512].rearrange("p (h x) -> p h x", h=8),
                        wq_p[:, j, half * 8:half * 8 + 8, 2, :])
            return v

        v_wv1 = load_wv(0, gA0)
        vA0_full = gA0.n

        v_iot2 = rec("gpsimd", lambda e: e.iota(iota_t[:, :], [[0, TT16], [1, E]],
                                                channel_multiplier=0,
                                                allow_small_or_imprecise_dtypes=True), gA0, 16)
        vA0 = gA0.n
        if debug:
            wt("sync", gA0, vA0)
            dma("sync", gDbg, dbg["epar"].ap(), parT_s[:, 16:32])
        wt("vector", gA0, vA0)
        rec("vector", lambda e: e.tensor_scalar_add(iotmb[:, :], iota16[:, :], -100.0))
        rec("vector", lambda e: e.tensor_scalar_add(iotmb_t[:, :], iota_t[:, :], -100.0))

        def load_wqk(p):
            s = gWq[p % 2]
            for j in range(DT8):
                dma("gpsimd", s,
                    Wq[p % 2][:, j * 128:(j + 1) * 128].rearrange("p (h x) -> p h x", h=2),
                    wq_p[:, j, 2 * p:2 * p + 2, 0, :])
                v = dma("gpsimd", s,
                        Wk[p % 2][:, j * 128:(j + 1) * 128].rearrange("p (h x) -> p h x", h=2),
                        wq_p[:, j, 2 * p:2 * p + 2, 1, :])
            return v

        def emit_vpass(wait_list, vev):
            vmm = {}
            for tt in range(KCH):
                buf = tt % 2
                if tt == 0:
                    for s_, v_ in wait_list:
                        wt("tensor", s_, v_)
                if tt >= 2:
                    wt("tensor", sDV, vev[tt - 2])
                for j in range(DT8):
                    def mm(e, j=j, tt=tt, buf=buf):
                        return e.matmul(PS[:, 3072 + buf * 512:3072 + (buf + 1) * 512],
                                        Zf[:, j * SEQ + tt * 128: j * SEQ + tt * 128 + 128],
                                        Wv[:, j * 512:(j + 1) * 512],
                                        start=(j == 0), stop=(j == DT8 - 1),
                                        skip_group_check=True)
                    if j == DT8 - 1:
                        vmm[tt] = rec("tensor", mm, sPE)
                    else:
                        rec("tensor", mm)
                wt("vector", sPE, vmm[tt])
                def ev(e, tt=tt, buf=buf):
                    return e.tensor_copy(Vh[:, tt * 512:(tt + 1) * 512],
                                         PS[:, 3072 + buf * 512:3072 + (buf + 1) * 512])
                vev[tt] = rec("vector", ev, sDV)
            return vmm

        vev1 = {}
        emit_vpass([(gA0, vA0)], vev1)

        v_w = {0: load_wqk(0)}
        prod_mm, prod_ev, avev, sums_done = {}, {}, {}, {}
        score_v, exp_v, av_v = {}, {}, {}
        vev2 = {}

        for p in range(NPAIR):
            par = p % 2
            half = p // 4
            vev_half = vev1 if half == 0 else vev2

            if p == 0:
                wt("tensor", sDV, vev1[KCH - 1])
            else:
                wt("tensor", sDV, avev[(p - 1, NQB - 1)])
            wt("tensor", gWq[par], v_w[p])
            if p >= 2:
                wt("tensor", sSc, (p - 1) * NCHUNK)
            for g in range(12):
                buf = g % 2
                isq = g < 4
                tc = g if isq else g - 4
                if p == 4 and g < 2:
                    wt("tensor", sDV, vev2[KCH - 2 + g])
                elif g >= 2:
                    wt("tensor", sDV, prod_ev[(p, g - 2)])
                elif p > 0:
                    wt("tensor", sDV, prod_ev[(p - 1, 10 + g)])
                for j in range(DT8):
                    def mm(e, par=par, isq=isq, tc=tc, j=j, buf=buf):
                        W = Wq[par] if isq else Wk[par]
                        return e.matmul(PS[:, 3072 + buf * 512:3072 + (buf + 1) * 512],
                                        W[:, j * 128:(j + 1) * 128],
                                        Zf[:, j * SEQ + tc * 512: j * SEQ + tc * 512 + 512],
                                        start=(j == 0), stop=(j == DT8 - 1),
                                        skip_group_check=True)
                    if j == DT8 - 1:
                        prod_mm[(p, g)] = rec("tensor", mm, sPE)
                    else:
                        rec("tensor", mm)
                wt("vector", sPE, prod_mm[(p, g)])
                def ev(e, par=par, isq=isq, tc=tc, buf=buf):
                    dst = QT[par] if isq else KT[par]
                    return e.tensor_copy(dst[:, tc * 512:(tc + 1) * 512],
                                         PS[:, 3072 + buf * 512:3072 + (buf + 1) * 512])
                prod_ev[(p, g)] = rec("vector", ev, sDV)
            if p + 1 < NPAIR:
                if p >= 1:
                    wt("gpsimd", sPE, prod_mm[(p - 1, 11)])
                v_w[p + 1] = load_wqk(p + 1)

            v_prod = prod_ev[(p, 11)]
            hhA = (2 * p) % 8
            hhB = hhA + 1

            def emit_scores(nl):
                qb, jj = nl // KCH, nl % KCH
                N = p * NCHUNK + nl
                spar = N % 2
                soff = (N % 3) * 1024
                if nl == 0:
                    wt("tensor", sDV, v_prod)
                if N >= 3:
                    wt("tensor", sEx, exp_v[N - 3])
                def mmA(e, par=par, soff=soff, jj=jj, qb=qb):
                    return e.matmul(PS[:, soff:soff + 512],
                                    KT[par][0:64, jj * 128:(jj + 1) * 128],
                                    QT[par][0:64, qb * QB:(qb + 1) * QB],
                                    start=True, stop=True, tile_position=(0, 0),
                                    skip_group_check=True)
                rec("tensor", mmA)
                def mmB(e, par=par, soff=soff, jj=jj, qb=qb):
                    return e.matmul(PS[:, soff + 512:soff + 1024],
                                    KT[par][64:128, jj * 128:(jj + 1) * 128],
                                    QT[par][64:128, qb * QB:(qb + 1) * QB],
                                    start=True, stop=True, tile_position=(64, 0),
                                    skip_group_check=True)
                score_v[N] = rec("tensor", mmB, sSc)
                wt("scalar", sSc, score_v[N])
                if N >= 2:
                    wt("scalar", sAV, av_v[N - 2])
                def ex(e, spar=spar, soff=soff):
                    return e.activation(PTb[spar][:, :], PS[:, soff:soff + 1024],
                                        AF.Exp, bias=negB[:, :], scale=0.125)
                exp_v[N] = rec("scalar", ex, sEx)

            def emit_av(nl):
                qb, jj = nl // KCH, nl % KCH
                N = p * NCHUNK + nl
                spar = N % 2
                wt("tensor", sEx, exp_v[N])
                if jj == 0:
                    if qb > 0:
                        wt("tensor", sDV, avev[(p, qb - 1)])
                    elif p > 0:
                        wt("tensor", sDV, avev[(p - 1, NQB - 1)])
                    wt("tensor", sDV, vev_half[KCH - 1])
                st_ = (jj == 0)
                sp_ = (jj == KCH - 1)
                def av1(e, jj=jj, spar=spar, st_=st_, sp_=sp_, hh=hhA):
                    return e.matmul(PS[0:64, 3072:3584],
                                    Vh[:, jj * 512 + hh * 64: jj * 512 + hh * 64 + 64],
                                    PTb[spar][:, 0:512], start=st_, stop=sp_,
                                    tile_position=(0, 0), skip_group_check=True)
                def av2(e, jj=jj, spar=spar, st_=st_, sp_=sp_, hh=hhB):
                    return e.matmul(PS[64:128, 3072:3584],
                                    Vh[:, jj * 512 + hh * 64: jj * 512 + hh * 64 + 64],
                                    PTb[spar][:, 512:1024], start=st_, stop=sp_,
                                    tile_position=(0, 64), skip_group_check=True)
                def sm1(e, spar=spar, st_=st_, sp_=sp_):
                    return e.matmul(PS[0:1, 3584:4096], onesb[:, :],
                                    PTb[spar][:, 0:512], start=st_, stop=sp_,
                                    tile_position=(0, 0), skip_group_check=True)
                def sm2(e, spar=spar, st_=st_, sp_=sp_):
                    return e.matmul(PS[32:33, 3584:4096], onesb[:, :],
                                    PTb[spar][:, 512:1024], start=st_, stop=sp_,
                                    tile_position=(0, 32), skip_group_check=True)
                rec("tensor", av1)
                rec("tensor", av2)
                rec("tensor", sm1)
                av_v[N] = rec("tensor", sm2, sAV)
                if jj == KCH - 1:
                    emit_qb_evac(qb)

            def emit_qb_evac(qb):
                wt("vector", sAV, av_v[p * NCHUNK + qb * KCH + KCH - 1])
                if p > 0 and qb == 0:
                    wt("vector", gST1, sums_done[p - 1])
                cols = p * TLOC + qb * QB
                rec("vector", lambda e, cols=cols: e.tensor_copy(
                    attn_f[0:64, cols:cols + QB], PS[0:64, 3072:3584]))
                rec("vector", lambda e, cols=cols: e.tensor_copy(
                    attn_f[64:128, cols:cols + QB], PS[64:128, 3072:3584]))
                rec("vector", lambda e, qb=qb: e.tensor_copy(
                    sums_sb[0:1, qb * QB:(qb + 1) * QB], PS[0:1, 3584:4096]))
                avev[(p, qb)] = rec("vector", lambda e, qb=qb: e.tensor_copy(
                    sums_sb[32:33, qb * QB:(qb + 1) * QB], PS[32:33, 3584:4096]), sDV)

            for nl in range(NCHUNK):
                emit_scores(nl)
                if nl >= 1:
                    emit_av(nl - 1)
            emit_av(NCHUNK - 1)
            wt("sync", sDV, avev[(p, NQB - 1)])
            dma("sync", gST1, sums_dram[2 * p:2 * p + 1, :], sums_sb[0:1, 0:TLOC])
            sums_done[p] = dma("sync", gST1, sums_dram[2 * p + 1:2 * p + 2, :],
                               sums_sb[32:33, 0:TLOC])
            if p == 3:
                v_wv2 = load_wv(1, gWv2)
                emit_vpass([(gWv2, v_wv2), (sDV, avev[(3, NQB - 1)]),
                            (sDV, prod_ev[(3, 11)])], vev2)

        v_attn_done = avev[(NPAIR - 1, NQB - 1)]
        v_dbg_attn = None
        if debug:
            wt("sync", sDV, v_attn_done)
            v_dbg_attn = dma("sync", gDbg, dbg["attn"].ap(), attn_f[:, :])
            wt("sync", gST1, sums_done[NPAIR - 1])
            dma("sync", gDbg, dbg["sums"].ap(), sums_dram.ap())

        if stop_after != "A":
            stk_a.close()
            zz = sbuf(stk_b2, "zz", [128, DT8 * TLOC], F32)
            scrA = sbuf(stk_b2, "scrA", [128, 2048], F32)
            scrB = sbuf(stk_b2, "scrB", [128, 1024], F32)
            rowA = sbuf(stk_b2, "rowA", [1, 2048], F32)
            rowB = sbuf(stk_b2, "rowB", [1, 2048], F32)
            rowC = sbuf(stk_b2, "rowC", [1, 2048], F32)
            rowD = sbuf(stk_b2, "rowD", [1, 2048], F32)
            srec = sbuf(stk_b2, "srec", [128, 2 * NPAIR * 16], F32)
            browb = sbuf(stk_b2, "browb", [1, 4096], BF16)
            mx16 = sbuf(stk_b2, "mx16", [128, TT16], F32)
            sm16 = sbuf(stk_b2, "sm16", [128, TT16], F32)
            df16 = sbuf(stk_b2, "df16", [128, TT16 * E], F32)
            ex16 = sbuf(stk_b2, "ex16", [128, TT16 * E], F32)
            eq16 = sbuf(stk_b2, "eq16", [128, TT16 * E], F32)

            gwT_s = sbuf(stk_b2, "gwT_s", [128, DT8 * E], F32)
            lgs = sbuf(stk_b2, "lgs", [16, TLOC], F32)
            logit_t = sbuf(stk_b2, "logit_t", [128, TT16 * E], F32)
            triu_s = sbuf(stk_b2, "triu_s", [128, 128], F32)
            trib_s = sbuf(stk_b2, "trib_s", [16, 16 * 128], F32)
            rmb_s = sbuf(stk_b2, "rmb_s", [8, 128], F32)
            cntag = sbuf(stk_b2, "cntag", [8, E], F32)
            cnt2 = sbuf(stk_b2, "cnt2", [16, E], F32)
            cntrow = sbuf(stk_b2, "cntrow", [1, TT16 * E], F32)
            oh_all = sbuf(stk_b2, "oh_all", [128, TT16 * E], F32)
            cum_s = sbuf(stk_b2, "cum_s", [128, TT16 * E], F32)
            pos_s = sbuf(stk_b2, "pos_s", [128, TT16 * E], F32)
            ctot_s = sbuf(stk_b2, "ctot_s", [1, E], F32)
            sexp = sbuf(stk_b2, "sexp", [128, E], F32)
            x1t_s = [sbuf(stk_b2, f"x1t_s{i}", [128, 1024], BF16) for i in range(2)]
            PS2 = stk_b2.enter_context(nc.psum_tensor("PS2", [128, 4096], F32))

            dma("sync", gLdB, gwT_s[:, :].rearrange("p (j e) -> p j e", j=DT8),
                gwT.ap().rearrange("(j p) e -> p j e", p=128))
            dma("sync", gLdB, triu_s[:, :], triu_d[:, :])
            dma("sync", gLdB, trib_s[:, :], trib_d[:, :])
            dma("sync", gLdB, rmb_s[:, :], rmb_d[:, :])
            for j in range(DT8):
                dma("sync", gLdB, zz[:, j * TLOC:(j + 1) * TLOC],
                    zT[j * 128:(j + 1) * 128, 0:TLOC])
            vLdB = gLdB.n

            wt("sync", gST1, sums_done[NPAIR - 1])
            v_sl = dma("sync", gRow, srec[:, :].rearrange("p (h c) -> p h c", h=16),
                       sums_dram.ap().rearrange("h (p c) -> p h c", c=16))
            wt("vector", gRow, v_sl)
            v_r = rec("vector", lambda e: e.reciprocal(srec[:, :], srec[:, :]), sDV)
            wt("sync", sDV, v_r)
            v_sr = dma("sync", gRow, sums_r.ap().rearrange("h (p c) -> p h c", c=16),
                       srec[:, :].rearrange("p (h c) -> p h c", h=16))
            v_scaled = None
            v_bc_last = {}
            for p2 in range(NPAIR):
                if p2 > 0:
                    wt("gpsimd", sPE, v_bc_last[p2 - 1])
                else:
                    wt("gpsimd", gRow, v_sr)
                v_ld = dma("gpsimd", gCNT, browb[0:1, :].rearrange("o (h t) -> o h t", h=2),
                           sums_r[2 * p2:2 * p2 + 2, :])
                wt("tensor", gCNT, v_ld)
                for hh2 in range(2):
                    par2 = hh2
                    if p2 > 0:
                        wt("tensor", sDV, v_sc_prev[hh2])
                    vb = None
                    for c4 in range(4):
                        def bc(e, hh2=hh2, c4=c4, par2=par2):
                            return e.matmul(
                                PS2[:, par2 * 2048 + c4 * 512:par2 * 2048 + (c4 + 1) * 512],
                                ones1b[0:1, :],
                                browb[0:1, hh2 * 2048 + c4 * 512:hh2 * 2048 + (c4 + 1) * 512],
                                start=True, stop=True, skip_group_check=True)
                        vb = rec("tensor", bc, sPE if c4 == 3 else None)
                    v_bc_last[p2] = vb
                    wt("vector", sPE, vb)
                    if p2 == 0 and hh2 == 0:
                        wt("vector", sDV, v_attn_done)
                        if debug:
                            wt("vector", gDbg, v_dbg_attn)
                    rows = slice(64 * hh2, 64 * hh2 + 64)
                    def sc(e, rows=rows, p2=p2, par2=par2):
                        return e.tensor_tensor(attn_f[rows, p2 * TLOC:(p2 + 1) * TLOC],
                                               attn_f[rows, p2 * TLOC:(p2 + 1) * TLOC],
                                               PS2[rows, par2 * 2048:(par2 + 1) * 2048],
                                               ALU.mult)
                    v_scaled = rec("vector", sc, sDV)
                    v_sc_prev = locals().get("v_sc_prev", {})
                    v_sc_prev[hh2] = v_scaled

            wt("vector", gLdB, vLdB)
            wt("vector", sDV, v_scaled)
            v_x = None
            for j in range(DT8):
                def add(e, j=j):
                    return e.tensor_add(zz[:, j * TLOC:(j + 1) * TLOC],
                                        zz[:, j * TLOC:(j + 1) * TLOC],
                                        attn_f[:, j * TLOC:(j + 1) * TLOC])
                v_x = rec("vector", add, sDV if j == DT8 - 1 else None)

            sx_off = [0, 512, 3072, 3584]
            v_sq = {}
            v_stat = None
            v_mmrow = None
            for j in range(DT8):
                for hf in range(2):
                    if j == 0 and hf == 0:
                        wt("scalar", sDV, v_x)
                    elif v_mmrow is not None:
                        wt("scalar", sPE, v_mmrow)
                    def sq(e, j=j, hf=hf):
                        return e.activation(scrA[:, hf * 1024:(hf + 1) * 1024],
                                            zz[:, j * TLOC + hf * 1024: j * TLOC + hf * 1024 + 1024],
                                            AF.Square)
                    v_sq[(j, hf)] = rec("scalar", sq, sAC)
                    if j == 0 and hf == 0:
                        wt("tensor", sDV, v_x)
                    wt("tensor", sAC, v_sq[(j, hf)])
                    for c2 in range(2):
                        c4 = hf * 2 + c2
                        def m1(e, j=j, c4=c4):
                            return e.matmul(PS2[0:1, sx_off[c4]:sx_off[c4] + 512], onesf[:, :],
                                            zz[:, j * TLOC + c4 * 512: j * TLOC + (c4 + 1) * 512],
                                            start=(j == 0), stop=(j == DT8 - 1),
                                            skip_group_check=True)
                        rec("tensor", m1)
                    for c2 in range(2):
                        c4 = hf * 2 + c2
                        def m2(e, j=j, hf=hf, c2=c2, c4=c4):
                            return e.matmul(PS2[0:1, 1024 + c4 * 512:1024 + (c4 + 1) * 512], onesf[:, :],
                                            scrA[:, hf * 1024 + c2 * 512: hf * 1024 + (c2 + 1) * 512],
                                            start=(j == 0), stop=(j == DT8 - 1),
                                            skip_group_check=True)
                        v_mmrow = rec("tensor", m2, sPE if c2 == 1 else None)
            v_stat = v_mmrow
            wt("vector", sPE, v_stat)
            rec("vector", lambda e: e.tensor_copy(rowA[0:1, 0:1024], PS2[0:1, 0:1024]))
            v1 = rec("vector", lambda e: e.tensor_copy(rowA[0:1, 1024:2048], PS2[0:1, 3072:4096]), sDV)
            v2 = rec("vector", lambda e: e.tensor_copy(rowB[0:1, :], PS2[0:1, 1024:3072]), sDV)
            wt("vector", sDV, v2)
            v3 = rec("vector", lambda e: e.tensor_scalar_mul(rowA[0:1, :], rowA[0:1, :], 1.0 / D), sDV)
            wt("vector", sDV, v3)
            v4 = rec("vector", lambda e: e.tensor_scalar_mul(rowB[0:1, :], rowB[0:1, :], 1.0 / D), sDV)
            wt("vector", sDV, v4)
            v5 = rec("vector", lambda e: e.tensor_tensor(rowC[0:1, :], rowA[0:1, :], rowA[0:1, :], ALU.mult), sDV)
            wt("vector", sDV, v5)
            v6 = rec("vector", lambda e: e.tensor_tensor(rowB[0:1, :], rowB[0:1, :], rowC[0:1, :], ALU.subtract), sDV)
            wt("scalar", sDV, v6)
            v7 = rec("scalar", lambda e: e.activation(rowC[0:1, :], rowB[0:1, :], AF.Ln, bias=epsB[0:1, :], scale=1.0), sAC)
            wt("scalar", sAC, v7)
            v8 = rec("scalar", lambda e: e.activation(rowB[0:1, :], rowC[0:1, :], AF.Exp, bias=0.0, scale=-0.5), sAC)
            wt("vector", sAC, v8)
            v9 = rec("vector", lambda e: e.scalar_tensor_tensor(rowD[0:1, :], rowA[0:1, :], -1.0,
                                                                rowB[0:1, :], ALU.mult, ALU.mult), sDV)
            v_ln = None
            for hhf in range(2):
                wt("tensor", sDV, v9)
                if v_ln is not None:
                    wt("tensor", sDV, v_ln)
                vb = None
                for c2 in range(2):
                    def bcr(e, hhf=hhf, c2=c2):
                        return e.matmul(PS2[:, 1024 + c2 * 512:1024 + (c2 + 1) * 512], ones1f[0:1, :],
                                        rowB[0:1, hhf * 1024 + c2 * 512:hhf * 1024 + (c2 + 1) * 512],
                                        start=True, stop=True, skip_group_check=True)
                    rec("tensor", bcr)
                    def bcn(e, hhf=hhf, c2=c2):
                        return e.matmul(PS2[:, 2048 + c2 * 512:2048 + (c2 + 1) * 512], ones1f[0:1, :],
                                        rowD[0:1, hhf * 1024 + c2 * 512:hhf * 1024 + (c2 + 1) * 512],
                                        start=True, stop=True, skip_group_check=True)
                    vb = rec("tensor", bcn, sPE if c2 == 1 else None)
                wt("vector", sPE, vb)
                for j in range(DT8):
                    def t1(e, j=j, hhf=hhf):
                        return e.tensor_tensor(scrB[:, 0:1024],
                                               zz[:, j * TLOC + hhf * 1024: j * TLOC + hhf * 1024 + 1024],
                                               PS2[:, 2048:3072], ALU.add)
                    va = rec("vector", t1, sDV)
                    wt("vector", sDV, va)
                    def t2(e, j=j, hhf=hhf):
                        return e.tensor_tensor(zz[:, j * TLOC + hhf * 1024: j * TLOC + hhf * 1024 + 1024],
                                               scrB[:, 0:1024], PS2[:, 1024:2048], ALU.mult)
                    v_ln = rec("vector", t2, sDV)
                    wt("vector", sDV, v_ln)

            wt("gpsimd", sDV, v_ln)
            for j in range(DT8):
                dma("gpsimd", gFM, x1fm_dram[j * 128:(j + 1) * 128, :], zz[:, j * TLOC:(j + 1) * TLOC])
            vFM = gFM.n
            v_xb = {}
            for tt in range(TT16):
                buf = tt % 2
                if tt == 0:
                    wt("sync", gFM, vFM)
                if tt >= 2:
                    wt("sync", gPub, tt * 16)
                v_xb[tt] = rec("sync", lambda e, tt=tt, buf=buf: e.dma_start(
                    out=x1t_s[buf][:, :], in_=x1fm_dram[:, tt * 128:(tt + 1) * 128],
                    transpose=True), gXb, 16)
                wt("sync", gXb, v_xb[tt])
                dma("sync", gPub, x1pub[tt * 128:(tt + 1) * 128, :], x1t_s[buf][:, :])
            vPub = gPub.n
            wt("gpsimd", gPub, vPub)
            v_ag_x = rec("gpsimd", lambda e: e.collective_compute(
                "AllGather", ALU.bypass, replica_groups=[list(range(NCORE))],
                ins=[x1pub.ap()], outs=[xall.ap()]), sCC)

            wt("tensor", gLdB, vLdB)
            wt("tensor", sDV, v_ln)
            v_lg = None
            for j in range(DT8):
                for c4 in range(4):
                    def mg(e, j=j, c4=c4):
                        return e.matmul(PS2[0:16, 1024 + c4 * 512:1024 + (c4 + 1) * 512],
                                        gwT_s[:, j * E:(j + 1) * E],
                                        zz[:, j * TLOC + c4 * 512: j * TLOC + (c4 + 1) * 512],
                                        start=(j == 0), stop=(j == DT8 - 1),
                                        skip_group_check=True)
                    v_lg = rec("tensor", mg, sPE if (j == DT8 - 1 and c4 == 3) else None)
            wt("vector", sPE, v_lg)
            v_lge = rec("vector", lambda e: e.tensor_copy(lgs[:, :], PS2[0:16, 1024:3072]), sDV)
            wt("tensor", sDV, v_lge)
            v_lt = None
            for tt in range(TT16):
                def tr(e, tt=tt):
                    return e.transpose(PS2[:, tt * 16:(tt + 1) * 16],
                                       lgs[0:16, tt * 128:(tt + 1) * 128], identf[0:16, 0:16])
                v_lt = rec("tensor", tr, sPE if tt == TT16 - 1 else None)
            wt("vector", sPE, v_lt)
            v_lte = rec("vector", lambda e: e.tensor_copy(logit_t[:, :], PS2[:, 0:256]), sDV)
            if debug:
                wt("sync", sDV, v_lte)
                dma("sync", gDbg, dbg["logits"].ap(), logit_t[:, :])

            def seg(ap):
                return ap.rearrange("p (t e) -> p t e", e=E)

            def bc16(tile_ap):
                return bass.AP(tile_ap.tensor, tile_ap.offset,
                               [[TT16, 128], [1, TT16], [0, E]])

            wt("vector", sDV, v_lte)
            r1 = rec("vector", lambda e: e.tensor_reduce(mx16[:, :], seg(logit_t[:, :]),
                                                         axis=AX, op=ALU.max), sDV)
            wt("vector", sDV, r1)
            r2 = rec("vector", lambda e: e.tensor_tensor(seg(df16[:, :]), seg(logit_t[:, :]),
                                                         bc16(mx16[:, :]), ALU.subtract), sDV)
            wt("scalar", sDV, r2)
            r3 = rec("scalar", lambda e: e.activation(ex16[:, :], df16[:, :], AF.Exp), sAC)
            wt("vector", sAC, r3)
            r4 = rec("vector", lambda e: e.tensor_reduce(sm16[:, :], seg(ex16[:, :]),
                                                         axis=AX, op=ALU.add), sDV)
            wt("vector", sDV, r4)
            rec("vector", lambda e: e.reciprocal(prob_s[:, :], sm16[:, :]))
            r6 = rec("vector", lambda e: e.tensor_tensor(seg(eq16[:, :]), seg(logit_t[:, :]),
                                                         bc16(mx16[:, :]), ALU.is_ge), sDV)
            wt("vector", sDV, r6)
            r7 = rec("vector", lambda e: e.tensor_tensor(eq16[:, :], eq16[:, :],
                                                         iotmb_t[:, :], ALU.mult), sDV)
            wt("vector", sDV, r7)
            r8 = rec("vector", lambda e: e.tensor_scalar_add(eq16[:, :], eq16[:, :], 100.0), sDV)
            wt("vector", sDV, r8)
            r9 = rec("vector", lambda e: e.tensor_reduce(eidx_s[:, :], seg(eq16[:, :]),
                                                         axis=AX, op=ALU.min), sDV)
            wt("vector", sDV, r9)
            v_ohv = rec("vector", lambda e: e.tensor_tensor(seg(oh_all[:, :]), iota_t[:, :].rearrange("p (t e) -> p t e", e=E),
                                                            bc16(eidx_s[:, :]), ALU.is_equal), sDV)

            wt("tensor", gLdB, vLdB)
            wt("tensor", sDV, v_ohv)
            for tt in range(TT16):
                def mc(e, tt=tt):
                    return e.matmul(PS2[:, 256 + tt * 16:256 + (tt + 1) * 16], triu_s[:, :],
                                    oh_all[:, tt * E:(tt + 1) * E], start=True, stop=True,
                                    skip_group_check=True)
                rec("tensor", mc)
            for tt in range(TT16):
                def mn(e, tt=tt):
                    return e.matmul(PS2[0:1, 512 + tt * 16:512 + (tt + 1) * 16], onesf[:, :],
                                    oh_all[:, tt * E:(tt + 1) * E], start=True, stop=True,
                                    skip_group_check=True)
                rec("tensor", mn)
            v_ct = None
            for tt in range(TT16):
                def mt(e, tt=tt):
                    return e.matmul(PS2[0:1, 768:784], onesf[:, :],
                                    oh_all[:, tt * E:(tt + 1) * E],
                                    start=(tt == 0), stop=(tt == TT16 - 1),
                                    skip_group_check=True)
                v_ct = rec("tensor", mt, sPE if tt == TT16 - 1 else None)
            wt("vector", sPE, v_ct)
            v_cume = rec("vector", lambda e: e.tensor_copy(cum_s[:, :], PS2[:, 256:512]), sDV)
            v_cnte = rec("vector", lambda e: e.tensor_copy(cntrow[0:1, :], PS2[0:1, 512:768]), sDV)
            v_cte = rec("vector", lambda e: e.tensor_copy(ctot_s[0:1, :], PS2[0:1, 768:784]), sDV)
            wt("sync", sDV, v_cte)
            vCP = dma("sync", gCP, cnt_in[0:1, :], ctot_s[0:1, :])
            wt("gpsimd", gCP, vCP)
            v_ag_c = rec("gpsimd", lambda e: e.collective_compute(
                "AllGather", ALU.bypass, replica_groups=[list(range(NCORE))],
                ins=[cnt_in.ap()], outs=[cnt_out.ap()]), sCC)
            wt("gpsimd", sDV, v_cnte)
            for tt in range(TT16):
                dma("gpsimd", gCNT, cnt2[tt:tt + 1, :], cntrow[0:1, tt * E:(tt + 1) * E])
            wt("gpsimd", sCC, v_ag_c)
            dma("gpsimd", gCNT, cntag[:, :], cnt_out.ap())
            vCNT = gCNT.n
            wt("tensor", gCNT, vCNT)
            v_pb = None
            for tt in range(TT16):
                def mb1(e, tt=tt):
                    return e.matmul(PS2[:, 784 + tt * 16:784 + (tt + 1) * 16],
                                    trib_s[:, tt * 128:(tt + 1) * 128], cnt2[:, :],
                                    start=True, stop=False, skip_group_check=True)
                rec("tensor", mb1)
                def mb2(e, tt=tt):
                    return e.matmul(PS2[:, 784 + tt * 16:784 + (tt + 1) * 16],
                                    rmb_s[:, 0:128], cntag[:, :],
                                    start=False, stop=True, skip_group_check=True)
                v_pb = rec("tensor", mb2, sPE if tt == TT16 - 1 else None)
            wt("vector", sPE, v_pb)
            wt("vector", sDV, v_cume)
            v_pos = rec("vector", lambda e: e.tensor_tensor(pos_s[:, :], cum_s[:, :],
                                                            PS2[:, 784:1040], ALU.add), sDV)
            wt("vector", sDV, v_pos)
            s1_ = rec("vector", lambda e: e.tensor_tensor(eq16[:, :], oh_all[:, :],
                                                          pos_s[:, :], ALU.mult), sDV)
            wt("vector", sDV, s1_)
            s2_ = rec("vector", lambda e: e.tensor_reduce(psel_s[:, :], seg(eq16[:, :]),
                                                          axis=AX, op=ALU.add), sDV)
            wt("vector", sDV, s2_)
            s3_ = rec("vector", lambda e: e.tensor_single_scalar(kept_s[:, :], psel_s[:, :],
                                                                 float(CAP), ALU.is_lt), sDV)
            wt("vector", sDV, s3_)
            rec("vector", lambda e: e.tensor_tensor(peff_s[:, :], prob_s[:, :],
                                                    kept_s[:, :], ALU.mult))
            s5_ = rec("vector", lambda e: e.scalar_tensor_tensor(slotf_s[:, :], eidx_s[:, :],
                                                                 float(CAP), psel_s[:, :],
                                                                 ALU.mult, ALU.add), sDV)
            sp0 = rec("vector", lambda e: e.tensor_tensor(ex16[:, :], oh_all[:, :],
                                                          parT_s[:, :], ALU.mult), sDV)
            wt("vector", sDV, sp0)
            sp1 = rec("vector", lambda e: e.tensor_reduce(epar_s[:, :], seg(ex16[:, :]),
                                                          axis=AX, op=ALU.add), sDV)
            wt("vector", sDV, sp1)
            sp0b = rec("vector", lambda e: e.tensor_tensor(ex16[:, :], oh_all[:, :],
                                                           ehalfT_s[:, :], ALU.mult), sDV)
            wt("vector", sDV, sp0b)
            sp2 = rec("vector", lambda e: e.tensor_reduce(mx16[:, :], seg(ex16[:, :]),
                                                          axis=AX, op=ALU.add), sDV)
            wt("vector", sDV, sp2)
            sp3 = rec("vector", lambda e: e.scalar_tensor_tensor(mx16[:, :], mx16[:, :],
                                                                 float(CAP), psel_s[:, :],
                                                                 ALU.mult, ALU.add), sDV)
            wt("vector", sDV, sp3)
            sp4 = rec("vector", lambda e: e.tensor_tensor(mx16[:, :], mx16[:, :],
                                                          kept_s[:, :], ALU.mult), sDV)
            wt("vector", sDV, sp4)
            rec("vector", lambda e: e.tensor_copy(sg_i[:, :], mx16[:, :]))
            wt("vector", sDV, s5_)
            s6_ = rec("vector", lambda e: e.tensor_tensor(df16[:, 0:TT16], slotf_s[:, :],
                                                          kept_s[:, :], ALU.mult), sDV)
            wt("vector", sDV, s6_)
            s8_ = rec("vector", lambda e: e.scalar_tensor_tensor(df16[:, TT16:2 * TT16], kept_s[:, :],
                                                                 -float(TRASH), df16[:, 0:TT16],
                                                                 ALU.mult, ALU.add), sDV)
            wt("vector", sDV, s8_)
            s9_ = rec("vector", lambda e: e.tensor_scalar_add(df16[:, TT16:2 * TT16],
                                                              df16[:, TT16:2 * TT16],
                                                              float(TRASH)), sDV)
            wt("vector", sDV, s9_)
            v_slots = rec("vector", lambda e: e.tensor_copy(st_i[:, :], df16[:, TT16:2 * TT16]), sDV)
            if debug:
                wt("sync", sDV, v_slots)
                dma("sync", gDbg, dbg["route"][:, 0 * TT16:1 * TT16], eidx_s[:, :])
                dma("sync", gDbg, dbg["route"][:, 1 * TT16:2 * TT16], psel_s[:, :])
                dma("sync", gDbg, dbg["route"][:, 2 * TT16:3 * TT16], kept_s[:, :])
                dma("sync", gDbg, dbg["route"][:, 3 * TT16:4 * TT16], prob_s[:, :])
                dma("sync", gDbg, dbg["route"][:, 4 * TT16:5 * TT16], slotf_s[:, :])
                wt("sync", gPub, vPub)
                dma("sync", gDbg, dbg["x1"].ap(), x1pub.ap())

            wt("gpsimd", sDV, v_slots)
            wt("gpsimd", gA0, vA0)
            for tt in range(TT16):
                def sct(e, tt=tt):
                    return e.indirect_dma_start(
                        tbl_in[:, :], IndirectOffsetOnAxis(st_i[:, tt:tt + 1], 0),
                        tokidx_s[:, tt:tt + 1], None)
                rec("gpsimd", sct, gSCT, 16)
            wt("gpsimd", gSCT, gSCT.n)
            v_ar_t = rec("gpsimd", lambda e: e.collective_compute(
                "AllReduce", ALU.add, replica_groups=[list(range(NCORE))],
                ins=[tbl_in.ap()], outs=[tbl_out.ap()]), sCC)

        if stop_after not in ("A", "B"):
            _stages = ["tbl", "g", "gf", "h", "o", "eo", None]
            def c_active(stage):
                return _stages.index(stage) <= _stages.index(c_sub) if c_sub else True
            stk_b2.close()
            stk_b1.close()
            G = sbuf(stk_c, "G", [128, 10 * 1024], BF16)
            Gf = sbuf(stk_c, "Gf", [128, DT8 * CAP], BF16)
            hT = sbuf(stk_c, "hT", [128, 32 * CAP], BF16)
            oT = sbuf(stk_c, "oT", [128, DT8 * CAP], BF16)
            EOs = sbuf(stk_c, "EOs", [128, 10 * 1024], BF16)
            w1b = [sbuf(stk_c, f"w1b{i}", [128, DT8 * 128], BF16) for i in range(2)]
            w2b = [sbuf(stk_c, f"w2b{i}", [128, 32 * 128], BF16) for i in range(2)]
            PSC = stk_c.enter_context(nc.psum_tensor("PSC", [128, 3072], F32))
            PSB = stk_c.enter_context(nc.psum_tensor("PSB", [128, 2048], BF16))

            w1r = w1o.ap().rearrange("(l j p) hh -> l p j hh", l=2, p=128)
            w2r = w2o.ap().rearrange("(l i p) dd -> l p i dd", l=2, p=128)

            wt("gpsimd", sCC, v_ar_t)
            wt("gpsimd", gA0, vA0)
            for k in range(20):
                def tg(e, k=k):
                    return e.indirect_dma_start(
                        tblown[:, k:k + 1], None, tbl_out[:, :],
                        IndirectOffsetOnAxis(slotsel_s[:, k:k + 1], 0))
                rec("gpsimd", tg, gTBL, 16)
            vTBL = gTBL.n

            gf_tr_last = {}
            v_gd_prev = [None]
            v_ag_eo = {}
            gf_ev = {}
            ht_mm_last = {}
            ht_ac = {}
            ot_mm_last = {}
            ot_ev = {}
            eo_ev = {}
            v_eo_w = {}
            v_w1 = {}
            v_w2 = {}
            v_dbg_eo = {}
            def emit_stage_G(le):
                wt("gpsimd", sCC, v_ag_x)
                wt("gpsimd", gTBL, vTBL)
                if le == 1:
                    wt("gpsimd", gGA, v_gd_prev[0])
                    wt("gpsimd", gXb, gf_tr_last[0])
                for k in range(10):
                    def gg(e, le=le, k=k):
                        return e.indirect_dma_start(
                            G[:, k * 1024:(k + 1) * 1024], None, xall[:, :],
                            IndirectOffsetOnAxis(tblown[:, le * 10 + k:le * 10 + k + 1], 0))
                    rec("gpsimd", gg, gGA, 16)
                v_gd = dma("gpsimd", gGA, gdram.ap().rearrange("(k p) dd -> p k dd", p=128),
                           G[:, :].rearrange("p (k dd) -> p k dd", k=10))
                v_gd_prev[0] = v_gd
                wt("sync", gGA, v_gd)
                if le == 1:
                    wt("sync", sPE, ht_mm_last[(0, 31)])
                v_gfx = 0
                for j in range(DT8):
                    v_gfx = rec("sync", lambda e, j=j: e.dma_start(
                        out=Gf[:, j * CAP:(j + 1) * CAP],
                        in_=gdram[:, j * 128:(j + 1) * 128], transpose=True), gXb, 16)
                gf_tr_last[le] = v_gfx

            emit_stage_G(0)
            for le in range(2):
                if le == 0:
                    for i in range(2):
                        v_w1[(le, i)] = dma("gpsimd", gWs[i % 2],
                                            w1b[i % 2][:, :].rearrange("p (j hh) -> p j hh", j=DT8),
                                            w1r[le, :, :, i * 128:(i + 1) * 128])
                for i in range(32):
                    buf = i % 2
                    wt("tensor", gWs[buf], v_w1[(le, i)])
                    if i == 0:
                        wt("tensor", gXb, gf_tr_last[le])
                        if le == 1:
                            wt("tensor", sDV, ot_ev[(0, DT8 - 1)])
                    if i >= 2:
                        wt("tensor", sAC, ht_ac[(le, i - 2)])
                    for j in range(DT8):
                        for c3 in range(3):
                            w_ = 512 if c3 < 2 else 256
                            def mh(e, i=i, j=j, c3=c3, w_=w_, buf=buf):
                                return e.matmul(PSC[:, buf * 1536 + c3 * 512: buf * 1536 + c3 * 512 + w_],
                                                w1b[buf][:, j * 128:(j + 1) * 128],
                                                Gf[:, j * CAP + c3 * 512: j * CAP + c3 * 512 + w_],
                                                start=(j == 0), stop=(j == DT8 - 1),
                                                skip_group_check=True)
                            if j == DT8 - 1 and c3 == 2:
                                ht_mm_last[(le, i)] = rec("tensor", mh, sPE)
                            else:
                                rec("tensor", mh)
                    wt("scalar", sPE, ht_mm_last[(le, i)])
                    def rl(e, i=i, buf=buf):
                        return e.activation(hT[:, i * CAP:(i + 1) * CAP],
                                            PSC[:, buf * 1536:buf * 1536 + CAP], AF.Relu)
                    ht_ac[(le, i)] = rec("scalar", rl, sAC)
                    if i + 2 < 32:
                        wt("gpsimd", sPE, ht_mm_last[(le, i)])
                        v_w1[(le, i + 2)] = dma("gpsimd", gWs[buf],
                                                w1b[buf][:, :].rearrange("p (j hh) -> p j hh", j=DT8),
                                                w1r[le, :, :, (i + 2) * 128:(i + 3) * 128])
                if le == 0:
                    emit_stage_G(1)
                for j in range(2):
                    if le == 1 and (0, 6 + j) in ot_mm_last:
                        wt("gpsimd", sPE, ot_mm_last[(0, 6 + j)])
                    v_w2[(le, j)] = dma("gpsimd", gWs[j % 2],
                                        w2b[j % 2][:, :].rearrange("p (i hh) -> p i hh", i=32),
                                        w2r[le, :, :, j * 128:(j + 1) * 128])
                if le == 0:
                    for i in range(2):
                        wt("gpsimd", sPE, ht_mm_last[(0, 30 + i)])
                        v_w1[(1, i)] = dma("gpsimd", gWs[i % 2],
                                           w1b[i % 2][:, :].rearrange("p (j hh) -> p j hh", j=DT8),
                                           w1r[1, :, :, i * 128:(i + 1) * 128])
                for j in range(DT8):
                    buf = j % 2
                    wt("tensor", gWs[buf], v_w2[(le, j)])
                    if j == 0:
                        wt("tensor", sAC, ht_ac[(le, 31)])
                    if j >= 2:
                        wt("tensor", sDV, ot_ev[(le, j - 2)])
                    for i in range(32):
                        for c3 in range(3):
                            w_ = 512 if c3 < 2 else 256
                            def mo(e, i=i, j=j, c3=c3, w_=w_, buf=buf):
                                return e.matmul(PSC[:, buf * 1536 + c3 * 512: buf * 1536 + c3 * 512 + w_],
                                                w2b[buf][:, i * 128:(i + 1) * 128],
                                                hT[:, i * CAP + c3 * 512: i * CAP + c3 * 512 + w_],
                                                start=(i == 0), stop=(i == 31),
                                                skip_group_check=True)
                            if i == 31 and c3 == 2:
                                ot_mm_last[(le, j)] = rec("tensor", mo, sPE)
                            else:
                                rec("tensor", mo)
                    wt("vector", sPE, ot_mm_last[(le, j)])
                    def oev(e, j=j, buf=buf):
                        return e.tensor_copy(oT[:, j * CAP:(j + 1) * CAP],
                                             PSC[:, buf * 1536:buf * 1536 + CAP])
                    ot_ev[(le, j)] = rec("vector", oev, sDV)
                    if j + 2 < DT8:
                        wt("gpsimd", sPE, ot_mm_last[(le, j)])
                        v_w2[(le, j + 2)] = dma("gpsimd", gWs[buf],
                                                w2b[buf][:, :].rearrange("p (i hh) -> p i hh", i=32),
                                                w2r[le, :, :, (j + 2) * 128:(j + 3) * 128])
                wt("gpsimd", sDV, ot_ev[(le, DT8 - 1)])
                v_od = dma("gpsimd", gEO, odram.ap().rearrange("(j p) s -> p j s", p=128),
                           oT[:, :].rearrange("p (j s) -> p j s", j=DT8))
                wt("sync", gEO, gEO.n)
                v_eox = 0
                for k in range(10):
                    v_eox = rec("sync", lambda e, k=k: e.dma_start(
                        out=EOs[:, k * 1024:(k + 1) * 1024],
                        in_=odram[:, k * 128:(k + 1) * 128], transpose=True), gXE, 16)
                wt("gpsimd", gXE, v_eox)
                v_eo_w[le] = dma("gpsimd", gEO,
                                 eo_in[le].ap().rearrange("(k p) dd -> p k dd", p=128),
                                 EOs[:, :].rearrange("p (k dd) -> p k dd", k=10))
                wt("gpsimd", gEO, gEO.n)
                v_ag_eo[le] = rec("gpsimd", lambda e, le=le: e.collective_compute(
                    "AllGather", ALU.bypass, replica_groups=[list(range(NCORE))],
                    ins=[eo_in[le].ap()],
                    outs=[(eo_allE if le == 0 else eo_allO).ap()]), sCC)
                if debug:
                    wt("sync", gEO, v_eo_w[le])
                    v_dbg_eo[le] = dma("sync", gDbg,
                                       dbg["eo"].ap().rearrange("(l k p) dd -> l p k dd", l=2, p=128)[le],
                                       EOs[:, :].rearrange("p (k dd) -> p k dd", k=10))
            if c_sub is None and stop_after == "C":
                wt("gpsimd", sCC, v_ag_eo[1])

        if stop_after is None:
            stk_c.close()
            x1tm = sbuf(stk_d, "x1tm", [128, TT16 * 1024], BF16)
            moe = sbuf(stk_d, "moe", [128, TT16 * 1024], BF16)
            moeO = sbuf(stk_d, "moeO", [128, TT16 * 1024], BF16)
            x2t = [sbuf(stk_d, f"x2t{i}", [128, 1024], F32) for i in range(2)]
            yt = [sbuf(stk_d, f"yt{i}", [128, 1024], F32) for i in range(2)]
            dsc = sbuf(stk_d, "dsc", [128, 1024], F32)
            dsc2 = sbuf(stk_d, "dsc2", [128, 1024], BF16)
            cst = sbuf(stk_d, "cst", [128, 16], F32)

            wt("gpsimd", gPub, vPub)
            for tt in range(TT16):
                dma("gpsimd", gD, x1tm[:, tt * 1024:(tt + 1) * 1024], x1pub[tt * 128:(tt + 1) * 128, :])
            wt("gpsimd", sCC, v_ag_eo[1])
            for tt in range(TT16):
                def mgthE(e, tt=tt):
                    return e.indirect_dma_start(
                        moe[:, tt * 1024:(tt + 1) * 1024], None, eo_allE[:, :],
                        IndirectOffsetOnAxis(sg_i[:, tt:tt + 1], 0))
                rec("gpsimd", mgthE, gD, 16)
                def mgthO(e, tt=tt):
                    return e.indirect_dma_start(
                        moeO[:, tt * 1024:(tt + 1) * 1024], None, eo_allO[:, :],
                        IndirectOffsetOnAxis(sg_i[:, tt:tt + 1], 0))
                rec("gpsimd", mgthO, gD, 16)
            vD = gD.n

            if debug:
                wt("sync", gD, vD)
                dma("sync", gDbg, dbg["moeE"].ap(), moe[:, :])
                dma("sync", gDbg, dbg["moeO"].ap(), moeO[:, :])
                wt("sync", sDV, v_slots)
                dma("sync", gDbg, dbg["sg"].ap(), sg_i[:, :])
            vst = {}
            for tt in range(TT16):
                buf = tt % 2
                if tt == 0:
                    wt("vector", gD, vD)
                if tt >= 2:
                    wt("vector", gY[buf], vst[tt - 2])
                c0a = rec("vector", lambda e, tt=tt: e.tensor_tensor(
                    dsc[:, :], moeO[:, tt * 1024:(tt + 1) * 1024],
                    moe[:, tt * 1024:(tt + 1) * 1024], ALU.subtract), sDV)
                wt("vector", sDV, c0a)
                c0b = rec("vector", lambda e, tt=tt: e.scalar_tensor_tensor(
                    dsc[:, :], dsc[:, :], epar_s[:, tt:tt + 1],
                    moe[:, tt * 1024:(tt + 1) * 1024], ALU.mult, ALU.add), sDV)
                wt("vector", sDV, c0b)
                c1 = rec("vector", lambda e, tt=tt, buf=buf: e.tensor_scalar(
                    x2t[buf][:, :], dsc[:, :],
                    peff_s[:, tt:tt + 1], None, ALU.mult), sDV)
                wt("vector", sDV, c1)
                c2 = rec("vector", lambda e, tt=tt, buf=buf: e.tensor_tensor(
                    x2t[buf][:, :], x2t[buf][:, :], x1tm[:, tt * 1024:(tt + 1) * 1024],
                    ALU.add), sDV)
                wt("vector", sDV, c2)
                c3 = rec("vector", lambda e, buf=buf: e.reduce_sum(cst[:, 0:1], x2t[buf][:, :],
                                                                   axis=AX), sDV)
                wt("scalar", sDV, c2)
                c4 = rec("scalar", lambda e, buf=buf: e.activation(dsc2[:, :], x2t[buf][:, :],
                                                                   AF.Square, accum_out=cst[:, 1:2]), sAC)
                wt("vector", sAC, c4)
                wt("vector", sDV, c3)
                c5 = rec("vector", lambda e: e.tensor_scalar_mul(cst[:, 2:3], cst[:, 0:1], 1.0 / D), sDV)
                wt("vector", sDV, c5)
                c6 = rec("vector", lambda e: e.tensor_scalar_mul(cst[:, 3:4], cst[:, 1:2], 1.0 / D), sDV)
                wt("vector", sDV, c6)
                c7 = rec("vector", lambda e: e.tensor_tensor(cst[:, 4:5], cst[:, 2:3], cst[:, 2:3],
                                                             ALU.mult), sDV)
                wt("vector", sDV, c7)
                c8 = rec("vector", lambda e: e.tensor_tensor(cst[:, 5:6], cst[:, 3:4], cst[:, 4:5],
                                                             ALU.subtract), sDV)
                wt("scalar", sDV, c8)
                c9 = rec("scalar", lambda e: e.activation(cst[:, 6:7], cst[:, 5:6], AF.Sqrt,
                                                          bias=epsB[:, :], scale=1.0), sAC)
                wt("vector", sAC, c9)
                c10 = rec("vector", lambda e: e.reciprocal(cst[:, 7:8], cst[:, 6:7]), sDV)
                wt("vector", sDV, c10)
                c11 = rec("vector", lambda e: e.scalar_tensor_tensor(cst[:, 8:9], cst[:, 2:3], -1.0,
                                                                     cst[:, 7:8], ALU.mult, ALU.mult), sDV)
                wt("vector", sDV, c11)
                c12 = rec("vector", lambda e, buf=buf: e.tensor_scalar(
                    yt[buf][:, :], x2t[buf][:, :], cst[:, 7:8], cst[:, 8:9],
                    ALU.mult, ALU.add), sDV)
                wt("sync", sDV, c12)
                vst[tt] = dma("sync", gY[buf], y_out[tt * 128:(tt + 1) * 128, :], yt[buf][:, :])
            wt("sync", gY[0], gY[0].n)
            wt("sync", gY[1], gY[1].n)
            if debug:
                wt("sync", gDbg, gDbg.n)
            stk_d.close()

        if stop_after is not None:
            if debug:
                wt("sync", gDbg, gDbg.n)
            if stop_after == "A":
                stk_a.close()
                stk_b1.close()
            elif stop_after == "B":
                stk_b2.close()
                stk_b1.close()
            elif stop_after == "C":
                stk_c.close()
        for s, v in waits_record:
            assert v <= s.n, f"wait on {s.name} for {v} but final count {s.n}"

        with nc.Block() as block:
            @block.tensor
            def _(eng):
                for f in ops["tensor"]:
                    f(eng)

            @block.vector
            def _(eng):
                for f in ops["vector"]:
                    f(eng)

            @block.scalar
            def _(eng):
                for f in ops["scalar"]:
                    f(eng)

            @block.gpsimd
            def _(eng):
                for f in ops["gpsimd"]:
                    f(eng)

            @block.sync
            def _(eng):
                for f in ops["sync"]:
                    f(eng)

    return nc


_CACHE = {}


def _consts():
    identb = np.eye(128, dtype=ml_dtypes.bfloat16)
    identf = np.eye(128, dtype=np.float32)
    triu = (np.arange(128)[:, None] < np.arange(128)[None, :]).astype(np.float32)
    trib = np.zeros((16, 16 * 128), dtype=np.float32)
    for tt in range(16):
        trib[:, tt * 128:(tt + 1) * 128] = (np.arange(16)[:, None] < tt).astype(np.float32)
    return identb, identf, triu, trib


def make_in_maps(z, wqkv, gw, w1, w2):
    identb, identf, triu, trib = _consts()
    wqkvT = np.ascontiguousarray(wqkv.T)
    gwT = np.ascontiguousarray(gw.T)
    zT = np.ascontiguousarray(np.transpose(z, (0, 2, 1)))
    in_maps = []
    for c in range(NCORE):
        b, qoff = c // 2, (c % 2) * TLOC
        zTc = np.roll(zT[b], -qoff, axis=1) if qoff else zT[b]
        rmb = np.zeros((8, 128), dtype=np.float32)
        rmb[0:c, :] = 1.0
        tok = (c * TLOC + np.arange(TT16)[None, :] * 128 +
               np.arange(128)[:, None]).astype(np.int32)
        ssel = (2 * c * CAP + np.arange(20)[None, :] * 128 +
                np.arange(128)[:, None]).astype(np.int32)
        parT = np.tile((np.arange(16) % 2).astype(np.float32), (128, 16))
        ehalfT = np.tile((np.arange(16) // 2).astype(np.float32), (128, 16))
        in_maps.append({
            "zT": np.ascontiguousarray(zTc),
            "wqkvT": wqkvT,
            "gwT": gwT,
            "w1o": np.ascontiguousarray(w1[2 * c:2 * c + 2].reshape(2 * D, HID)),
            "w2o": np.ascontiguousarray(w2[2 * c:2 * c + 2].reshape(2 * HID, D)),
            "identbd": identb,
            "identfd": identf,
            "triud": triu,
            "tribd": trib,
            "rmbd": rmb,
            "tokidxd": tok,
            "slotseld": ssel,
            "partd": parT,
            "ehalfd": ehalfT,
        })
    return in_maps


def kernel(L0_attn_z, L0_attn_W_qkv, L0_ffn_gate_w, L0_ffn_w1, L0_ffn_w2):
    z = np.asarray(L0_attn_z, dtype=np.float32)
    wqkv = np.asarray(L0_attn_W_qkv, dtype=np.float32)
    gw = np.asarray(L0_ffn_gate_w, dtype=np.float32)
    w1 = np.asarray(L0_ffn_w1, dtype=np.float32)
    w2 = np.asarray(L0_ffn_w2, dtype=np.float32)

    if "nc" not in _CACHE:
        _CACHE["nc"] = build_program(debug=False)
    nc = _CACHE["nc"]
    in_maps = make_in_maps(z, wqkv, gw, w1, w2)
    res = run_bass_kernel_spmd(nc, in_maps, core_ids=list(range(NCORE)))
    y = np.concatenate([np.asarray(res.results[c]["y"]) for c in range(NCORE)], axis=0)
    return y.reshape(4, SEQ, D)
